# version 06
# speedup vs baseline: 1.0819x; 1.0036x over previous
import numpy as np
from contextlib import ExitStack

import concourse.bass as bass
import concourse.mybir as mybir
from concourse.bass_utils import run_bass_kernel_spmd

F32 = mybir.dt.float32
U32 = mybir.dt.uint32
I32 = mybir.dt.int32
AF = mybir.ActivationFunctionType
ALU = mybir.AluOpType
AX = mybir.AxisListType

T = 4096
C = 1024
NCH = 8
NCORES = 8
TT = 512

LN_EPS = 1e-5
RMS_EPS = 1e-6

QSEM = {"tensor": "qt", "vector": "qv", "scalar": "qs", "gpsimd": "qg", "sync": "qy"}


class Prog:
    ENGINES = ("tensor", "vector", "scalar", "gpsimd", "sync")

    def __init__(self, nc, es, pfx="", fused=False, final_outputs=(), alias=None):
        self.nc = nc
        self.es = es
        self.pfx = pfx
        self.fused = fused
        self.final_outputs = set(final_outputs)
        self.alias = dict(alias or {})
        self.dram_map = {}
        self.scope = es
        self.ops = {e: [] for e in self.ENGINES}
        self.sems = {}
        self.cnt = {}
        self.banks = [nc.alloc_psum_tensor(f"{pfx}bank{i}", [128, 512], F32).ap()
                      for i in range(8)]
        self.bank_release = [None] * 8
        self._rr = 0
        self.last_store = None

    def resolve(self, name):
        seen = set()
        while name in self.alias and name not in seen:
            seen.add(name)
            name = self.alias[name]
        return name

    def dram(self, name, shape, kind, dt=F32):
        name = self.resolve(name)
        if name in self.dram_map:
            return self.dram_map[name]
        k = kind
        if self.fused and kind == "ExternalOutput" and name not in self.final_outputs:
            k = "Internal"
        ap = self.nc.dram_tensor(name, shape, dt, kind=k).ap()
        self.dram_map[name] = ap
        return ap

    def sbuf(self, name, shape, dt=F32):
        return self.scope.enter_context(self.nc.sbuf_tensor(name, shape, dt)).ap()

    def push_scope(self):
        self.scope = ExitStack()
        return self.scope

    def pop_scope(self):
        self.scope.close()
        self.scope = self.es

    def sem(self, name):
        if name not in self.sems:
            self.sems[name] = self.es.enter_context(self.nc.semaphore(self.pfx + name))
            self.cnt[name] = 0
        return self.sems[name]

    def emit(self, engine, fn, inc=False, k=1):
        s = v = None
        if inc:
            name = QSEM[engine]
            s = self.sem(name)
            self.cnt[name] += k
            v = k

        def thunk(eng, fn=fn, s=s, v=v):
            ins = fn(eng)
            if s is not None:
                ins.then_inc(s, v)
        self.ops[engine].append(thunk)
        return (QSEM[engine], self.cnt[QSEM[engine]]) if inc else None

    DMA_POOL = {"sync": 10, "gpsimd": 4}

    def dma(self, engine, fn):
        npool = self.DMA_POOL[engine]
        key = f"_dmarr_{engine}"
        idx = getattr(self, key, 0)
        setattr(self, key, idx + 1)
        name = f"d{engine[0]}{idx % npool}"
        s = self.sem(name)
        prev = self.cnt[name]
        if prev > 0:
            self.ops[engine].append(lambda eng, s=s, v=prev: eng.wait_ge(s, v))
        self.cnt[name] += 16

        def thunk(eng, fn=fn, s=s):
            fn(eng).then_inc(s, 16)
        self.ops[engine].append(thunk)
        return (name, self.cnt[name])

    def t(self, name):
        self.sem(name)
        return (name, self.cnt[name])

    def w(self, engine, ticket):
        if ticket is not None and ticket[1] > 0:
            s = self.sem(ticket[0])
            v = ticket[1]
            self.ops[engine].append(lambda eng, s=s, v=v: eng.wait_ge(s, v))

    def barrier(self, engines=None):
        snap = [(n, c) for n, c in self.cnt.items() if c > 0]
        for e in (engines or self.ENGINES):
            for n, c in snap:
                self.w(e, (n, c))

    def bank(self):
        i = self._rr % 7
        self._rr += 1
        self.w("tensor", self.bank_release[i])
        return i

    def bank7(self):
        self.w("tensor", self.bank_release[7])
        return 7

    def bank_done(self, i, ticket):
        self.bank_release[i] = ticket

    def finalize(self):
        with self.nc.Block() as block:
            for e in self.ENGINES:
                thunks = self.ops[e]

                def runner(eng, thunks=thunks):
                    for th in thunks:
                        th(eng)
                getattr(block, e)(runner)


def dma_in(p, sb_ap, dram_ap, eng="sync"):
    return p.dma(eng, lambda e, o=sb_ap, i=dram_ap: e.dma_start(out=o, in_=i))


def dma_out(p, dram_ap, sb_ap, eng="sync"):
    return p.dma(eng, lambda e, o=dram_ap, i=sb_ap: e.dma_start(out=o, in_=i))


def load_weight_T(p, w_dram, wT_sb, wnat_sb, ident, d_out=C, d_in=C):
    t = dma_in(p, wnat_sb, w_dram.rearrange("(m p) c -> p m c", p=128))
    p.w("tensor", t)
    for m in range(d_out // 128):
        for kc in range(d_in // 128):
            b = p.bank()
            tpe = p.emit("tensor",
                         lambda e, b=b, m=m, kc=kc: e.transpose(
                             p.banks[b][:, 0:128],
                             wnat_sb[:, m, kc * 128:(kc + 1) * 128], ident),
                         inc=True)
            p.w("vector", tpe)
            tev = p.emit("vector",
                         lambda e, b=b, m=m, kc=kc: e.tensor_copy(
                             wT_sb[:, kc, m * 128:(m + 1) * 128], p.banks[b][:, 0:128]),
                         inc=True)
            p.bank_done(b, tev)
    return p.t("qv")


def proj_fm(p, out_cb, xt_sb, w_sb, n, n_m=NCH, n_k=NCH):
    for m in range(n_m):
        b = p.bank()
        tpe = None
        for kc in range(n_k):
            tpe = p.emit("tensor",
                         lambda e, b=b, m=m, kc=kc: e.matmul(
                             p.banks[b][:, 0:n], w_sb[:, kc, m * 128:(m + 1) * 128],
                             xt_sb[:, kc, 0:n],
                             start=(kc == 0), stop=(kc == n_k - 1)),
                         inc=(kc == n_k - 1))
        tev = out_cb(m, b, tpe)
        p.bank_done(b, tev)


def ln_tile(p, x_sb, n, sq_sb, a_bc, b_bc, rows, ones_col, ones_row,
            eps_col, kind, out_sb, w_col=None, b_col=None, n_ch=NCH):
    r_sum = rows[0:1, 0:n]
    r_sq = rows[0:1, 512:512 + n]
    r_a = rows[0:1, 1024:1024 + n]
    r_b = rows[0:1, 1536:1536 + n]
    nf = float(n_ch * 128)

    tsq = p.emit("scalar", lambda e: e.activation(
        sq_sb[:, 0:n_ch, 0:n], x_sb[:, 0:n_ch, 0:n], AF.Square), inc=True)
    if kind == "ln":
        b1 = p.bank()
        for kc in range(n_ch):
            t1 = p.emit("tensor",
                        lambda e, b1=b1, kc=kc: e.matmul(
                            p.banks[b1][0:1, 0:n], ones_col, x_sb[:, kc, 0:n],
                            start=(kc == 0), stop=(kc == n_ch - 1)),
                        inc=(kc == n_ch - 1))
    b2 = p.bank()
    p.w("tensor", tsq)
    for kc in range(n_ch):
        t2 = p.emit("tensor",
                    lambda e, b2=b2, kc=kc: e.matmul(
                        p.banks[b2][0:1, 0:n], ones_col, sq_sb[:, kc, 0:n],
                        start=(kc == 0), stop=(kc == n_ch - 1)),
                    inc=(kc == n_ch - 1))
    p.w("vector", t2)
    if kind == "ln":
        p.emit("vector", lambda e: e.tensor_scalar(r_sum, p.banks[b1][0:1, 0:n],
                                                   1.0 / nf, None, ALU.mult), inc=True)
        tb = p.emit("vector", lambda e: e.tensor_scalar(r_sq, p.banks[b2][0:1, 0:n],
                                                        1.0 / nf, None, ALU.mult),
                    inc=True)
        p.bank_done(b1, tb)
        p.bank_done(b2, tb)
        p.w("vector", tb)
        p.emit("vector", lambda e: e.tensor_tensor(r_a, r_sum, r_sum, ALU.mult),
               inc=True)
        p.w("vector", p.t("qv"))
        tv = p.emit("vector", lambda e: e.tensor_tensor(r_a, r_sq, r_a, ALU.subtract),
                    inc=True)
        p.w("scalar", tv)
        ts = p.emit("scalar", lambda e: e.activation(r_b, r_a, AF.Sqrt,
                                                     bias=eps_col[0:1, 0:1]), inc=True)
        p.w("vector", ts)
        p.emit("vector", lambda e: e.reciprocal(r_a, r_b), inc=True)
        p.w("vector", p.t("qv"))
        p.emit("vector", lambda e: e.tensor_tensor(r_b, r_sum, r_a, ALU.mult), inc=True)
        p.w("vector", p.t("qv"))
        tv = p.emit("vector", lambda e: e.tensor_scalar(r_b, r_b, -1.0, None, ALU.mult),
                    inc=True)
    else:
        tb = p.emit("vector", lambda e: e.tensor_scalar(r_a, p.banks[b2][0:1, 0:n],
                                                        1.0 / nf, None, ALU.mult),
                    inc=True)
        p.bank_done(b2, tb)
        p.w("scalar", tb)
        ts = p.emit("scalar", lambda e: e.activation(r_b, r_a, AF.Sqrt,
                                                     bias=eps_col[0:1, 0:1]), inc=True)
        p.w("vector", ts)
        tv = p.emit("vector", lambda e: e.reciprocal(r_a, r_b), inc=True)
    b3 = p.bank()
    p.w("tensor", tv)
    t3 = p.emit("tensor", lambda e, b3=b3: e.matmul(p.banks[b3][:, 0:n], ones_row, r_a,
                                                    start=True, stop=True), inc=True)
    if kind == "ln":
        b4 = p.bank()
        t3 = p.emit("tensor", lambda e, b4=b4: e.matmul(p.banks[b4][:, 0:n], ones_row,
                                                        r_b, start=True, stop=True),
                    inc=True)
    p.w("vector", t3)
    ta = p.emit("vector", lambda e, b3=b3: e.tensor_copy(a_bc[:, 0:n],
                                                         p.banks[b3][:, 0:n]), inc=True)
    p.bank_done(b3, ta)
    if kind == "ln":
        ta = p.emit("vector", lambda e, b4=b4: e.tensor_copy(b_bc[:, 0:n],
                                                             p.banks[b4][:, 0:n]),
                    inc=True)
        p.bank_done(b4, ta)
    p.w("vector", ta)
    for kc in range(n_ch):
        if kind == "ln":
            t = p.emit("vector", lambda e, kc=kc: e.tensor_tensor(
                out_sb[:, kc, 0:n], x_sb[:, kc, 0:n], a_bc[:, 0:n], ALU.mult), inc=True)
            p.w("vector", t)
            t = p.emit("vector", lambda e, kc=kc: e.tensor_tensor(
                out_sb[:, kc, 0:n], out_sb[:, kc, 0:n], b_bc[:, 0:n], ALU.add),
                inc=True)
            if w_col is not None:
                p.w("vector", t)
                p.emit("vector", lambda e, kc=kc: e.tensor_scalar(
                    out_sb[:, kc, 0:n], out_sb[:, kc, 0:n],
                    w_col[:, kc:kc + 1], b_col[:, kc:kc + 1], ALU.mult, ALU.add),
                    inc=True)
        else:
            if w_col is None:
                p.emit("vector", lambda e, kc=kc: e.tensor_tensor(
                    out_sb[:, kc, 0:n], x_sb[:, kc, 0:n], a_bc[:, 0:n], ALU.mult),
                    inc=True)
            else:
                p.emit("vector", lambda e, kc=kc: e.scalar_tensor_tensor(
                    out_sb[:, kc, 0:n], x_sb[:, kc, 0:n], w_col[:, kc:kc + 1],
                    a_bc[:, 0:n], ALU.mult, ALU.mult), inc=True)
    return p.t("qv")


def mix_proj_pass(p, src_d, mix, wT, dst, sig, xc2, xm2, xsq, osb2, Tl,
                  evac_extra=None):
    ntt = Tl // TT
    use_tk = {}
    st_tk = {}
    mt = {}

    def prep(it, wi):
        t0 = it * TT
        p.w("sync", use_tk.get(wi))
        p.w("vector", use_tk.get(wi))
        tl1 = dma_in(p, xc2[wi], src_d[:, :, t0:t0 + TT].rearrange("c p t -> p c t"))
        if t0 == 0:
            tl2 = dma_in(p, xm2[wi][:, :, 1:TT],
                         src_d[:, :, 0:TT - 1].rearrange("c p t -> p c t"))
            p.w("vector", tl1)
            p.w("vector", tl2)
            p.emit("vector", lambda e, wi=wi: e.memset(xm2[wi][:, :, 0:1], 0.0),
                   inc=True)
        else:
            tl2 = dma_in(p, xm2[wi],
                         src_d[:, :, t0 - 1:t0 + TT - 1].rearrange("c p t -> p c t"))
            p.w("vector", tl1)
            p.w("vector", tl2)
        for kc in range(NCH):
            p.w("vector", p.t("qv"))
            p.emit("vector", lambda e, kc=kc, wi=wi: e.scalar_tensor_tensor(
                xsq[:, kc, :], xm2[wi][:, kc, :], mix[:, kc:kc + 1],
                xm2[wi][:, kc, :], ALU.mult, ALU.subtract), inc=True)
            p.w("vector", p.t("qv"))
            p.emit("vector", lambda e, kc=kc, wi=wi: e.tensor_scalar(
                xm2[wi][:, kc, :], xc2[wi][:, kc, :], mix[:, kc:kc + 1], None,
                ALU.mult), inc=True)
            p.w("vector", p.t("qv"))
            p.emit("vector", lambda e, kc=kc, wi=wi: e.tensor_tensor(
                xm2[wi][:, kc, :], xm2[wi][:, kc, :], xsq[:, kc, :], ALU.subtract),
                inc=True)
        return p.t("qv")

    mt[0] = prep(0, 0)
    for it in range(ntt):
        wi = it % 2
        if it + 1 < ntt:
            mt[1 - wi] = prep(it + 1, 1 - wi)
        p.w("tensor", mt[wi])

        def evac(m, b, pet, wi=wi, sig=sig):
            eng = "scalar" if sig else "vector"
            p.w(eng, pet)
            p.w(eng, st_tk.get(wi))
            if sig:
                t = p.emit("scalar", lambda e, m=m, b=b, wi=wi: e.activation(
                    osb2[wi][:, m, :], p.banks[b][:, 0:TT], AF.Sigmoid), inc=True)
            else:
                t = p.emit("vector", lambda e, m=m, b=b, wi=wi: e.tensor_copy(
                    osb2[wi][:, m, :], p.banks[b][:, 0:TT]), inc=True)
            return t
        proj_fm(p, evac, xm2[wi], wT, TT)
        use_tk[wi] = p.t("qt")
        p.w("sync", p.t("qv"))
        p.w("sync", p.t("qs"))
        t0 = it * TT
        st_tk[wi] = dma_out(p, dst[:, :, t0:t0 + TT].rearrange("c p t -> p c t"),
                            osb2[wi])
    p.barrier()


def emit_stage1(p, Tl=T):
    nc = p.nc
    p.push_scope()
    ntt = Tl // TT

    D = lambda name, shape, kind: p.dram(
        ("s1_" + name) if (kind == "Internal" or p.fused) else name, shape, kind)
    x0 = D("x0", [NCH, 128, Tl], "ExternalInput")
    Wk = D("Wk", [C, C], "ExternalInput")
    Wv = D("Wv", [C, C], "ExternalInput")
    Wr = D("Wr", [C, C], "ExternalInput")
    Wo = D("Wo", [C, C], "ExternalInput")
    vecs = D("vecs", [128, 8 * NCH], "ExternalInput")
    consts = D("consts", [128, 8], "ExternalInput")
    idc = D("identity", [128, 128], "ExternalInput")
    xn0 = D("xn0", [NCH, 128, Tl], "ExternalOutput")
    a0 = D("a0", [NCH, 128, Tl], "ExternalOutput")
    xt_d = D("xt_d", [NCH, 128, Tl], "Internal")
    K_d = D("K_d", [NCH, 128, Tl], "Internal")
    V_d = D("V_d", [NCH, 128, Tl], "Internal")
    R_d = D("R_d", [NCH, 128, Tl], "Internal")
    Z_d = D("Z_d", [NCH, 128, Tl], "Internal")

    A = lambda name, shape, dt=F32: p.sbuf("s1_" + name, shape, dt)
    ident = A("ident", [128, 128])
    ones_col = A("ones_col", [128, 1])
    ones_row = A("ones_row", [1, 128])
    sb_vec = A("sb_vec", [128, 8 * NCH])
    sb_cst = A("sb_cst", [128, 8])
    fcol = A("fcol", [128, NCH])
    wT = A("wT", [128, NCH, C])
    wnat = A("wnat", [128, NCH, C])
    xa = A("xa", [128, NCH, TT])
    xb = A("xb", [128, NCH, TT])
    xc = A("xc", [128, NCH, TT])
    xm = A("xm", [128, NCH, TT])
    xsq = A("xsq", [128, NCH, TT])
    osb = A("osb", [128, NCH, TT])
    a_bc = A("a_bc", [128, TT])
    b_bc = A("b_bc", [128, TT])
    rows = A("rows", [1, 2048])

    xcB = A("xcB", [128, NCH, TT])
    xmB = A("xmB", [128, NCH, TT])
    osbB = wnat[:, :, 0:TT]
    kc_t = xa.rearrange("p a b -> p (a b)")[:, 0:Tl]
    vc_t = xb.rearrange("p a b -> p (a b)")[:, 0:Tl]
    rc_t = xc.rearrange("p a b -> p (a b)")[:, 0:Tl]
    sc_t = xm.rearrange("p a b -> p (a b)")[:, 0:Tl]

    dma_in(p, ident, idc)
    t = dma_in(p, sb_vec, vecs)
    dma_in(p, sb_cst, consts)
    mixk = sb_vec[:, 0:NCH]
    mixv = sb_vec[:, NCH:2 * NCH]
    mixr = sb_vec[:, 2 * NCH:3 * NCH]
    tdec = sb_vec[:, 3 * NCH:4 * NCH]
    tfirst = sb_vec[:, 4 * NCH:5 * NCH]
    lnw = sb_vec[:, 5 * NCH:6 * NCH]
    lnb = sb_vec[:, 6 * NCH:7 * NCH]
    dcol = sb_vec[:, 7 * NCH:8 * NCH]
    eps_ln = sb_cst[:, 0:1]
    p.w("scalar", t)
    p.emit("scalar", lambda e: e.activation(dcol, tdec, AF.Sigmoid), inc=True)
    p.emit("scalar", lambda e: e.activation(fcol, tfirst, AF.Exp), inc=True)
    p.emit("vector", lambda e: e.memset(ones_col, 1.0), inc=True)
    p.emit("vector", lambda e: e.memset(ones_row, 1.0), inc=True)
    p.barrier()

    for it in range(ntt):
        t0 = it * TT
        dma_in(p, xa, x0[:, :, t0:t0 + TT].rearrange("c p t -> p c t"))
        p.barrier()
        tv = ln_tile(p, xa, TT, xsq, a_bc, b_bc, rows, ones_col, ones_row, eps_ln,
                     "ln", xb)
        p.w("sync", tv)
        dma_out(p, xn0[:, :, t0:t0 + TT].rearrange("c p t -> p c t"), xb)
        p.barrier()
        tv = ln_tile(p, xb, TT, xsq, a_bc, b_bc, rows, ones_col, ones_row, eps_ln,
                     "ln", xc, w_col=lnw, b_col=lnb)
        p.w("sync", tv)
        dma_out(p, xt_d[:, :, t0:t0 + TT].rearrange("c p t -> p c t"), xc)
        p.barrier()

    for (W_dram, mix, dst, sig) in ((Wk, mixk, K_d, False), (Wv, mixv, V_d, False),
                                    (Wr, mixr, R_d, True)):
        load_weight_T(p, W_dram, wT, wnat, ident)
        p.barrier()
        mix_proj_pass(p, xt_d, mix, wT, dst, sig, [xc, xcB], [xm, xmB], xsq,
                      [osb, osbB], Tl)

    load_weight_T(p, Wo, wT, wnat, ident)
    for kc in range(NCH):
        dma_in(p, kc_t, K_d[kc])
        dma_in(p, vc_t, V_d[kc])
        dma_in(p, rc_t, R_d[kc])
        p.barrier()
        t = p.emit("vector", lambda e, kc=kc: e.tensor_tensor_scan(
            sc_t, dcol[:, kc:kc + 1].to_broadcast([128, Tl]), kc_t,
            0.0, ALU.mult, ALU.add), inc=True)
        p.w("vector", t)
        t = p.emit("vector", lambda e, kc=kc: e.tensor_scalar(
            sc_t, sc_t, fcol[:, kc:kc + 1], None, ALU.mult), inc=True)
        p.w("vector", t)
        t = p.emit("vector", lambda e: e.tensor_tensor(sc_t, sc_t, kc_t, ALU.add),
                   inc=True)
        p.w("vector", t)
        t = p.emit("vector", lambda e: e.tensor_tensor(sc_t, sc_t, vc_t, ALU.mult),
                   inc=True)
        p.w("vector", t)
        tv = p.emit("vector", lambda e: e.tensor_tensor(sc_t, sc_t, rc_t, ALU.mult),
                    inc=True)
        p.w("sync", tv)
        dma_out(p, Z_d[kc], sc_t)
        p.barrier()

    p.barrier()
    for it in range(ntt):
        t0 = it * TT
        dma_in(p, xa, Z_d[:, :, t0:t0 + TT].rearrange("c p t -> p c t"))
        dma_in(p, xb, xn0[:, :, t0:t0 + TT].rearrange("c p t -> p c t"))
        p.barrier()

        def evac(m, b, pet):
            p.w("vector", pet)
            return p.emit("vector", lambda e, m=m, b=b: e.tensor_tensor(
                osb[:, m, :], p.banks[b][:, 0:TT], xb[:, m, :], ALU.add), inc=True)
        proj_fm(p, evac, xa, wT, TT)
        p.barrier(("sync",))
        dma_out(p, a0[:, :, t0:t0 + TT].rearrange("c p t -> p c t"), osb)
        p.barrier()

    p.barrier()
    p.pop_scope()


def emit_dec_qkv(p, pfx, S):
    nc = p.nc
    p.push_scope()
    ntt = S // TT

    D = lambda n, s, k: p.dram(pfx + n, s, k)
    hs = D("hs", [NCH, 128, S], "ExternalInput")
    wq = D("wq", [C, C], "ExternalInput")
    wk = D("wk", [C, C], "ExternalInput")
    wv = D("wv", [C, C], "ExternalInput")
    vec = D("vec", [128, NCH], "ExternalInput")
    cst = D("cst", [128, 8], "ExternalInput")
    cos_d = D("cos", [S, 32], "ExternalInput")
    sin_d = D("sin", [S, 32], "ExternalInput")
    QT = D("QT", [NCH, 128, S], "ExternalOutput")
    KT = D("KT", [NCH, 128, S], "ExternalOutput")
    Vtok = D("Vtok", [S, C], "ExternalOutput")
    idc = D("identity", [128, 128], "ExternalInput")

    A = lambda n, s, dt=F32: p.sbuf(pfx + n, s, dt)
    ident = A("ident", [128, 128])
    ones_col = A("ones_col", [128, 1])
    ones_row = A("ones_row", [1, 128])
    sb_vec = A("sb_vec", [128, NCH])
    sb_cst = A("sb_cst", [128, 8])
    w1 = A("w1", [128, NCH, C])
    w2 = A("w2", [128, NCH, C])
    w3 = A("w3", [128, NCH, C])
    xa = A("xa", [128, NCH, TT])
    xh = A("xh", [128, NCH, TT])
    xsq = A("xsq", [128, NCH, TT])
    ptile = A("ptile", [128, C])
    ptile2 = A("ptile2", [128, C])
    ptile3 = A("ptile3", [128, 512])
    oT = A("oT", [128, NCH, TT])
    cs = A("cs", [128, 64])
    a_bc = A("a_bc", [128, TT])
    b_bc = A("b_bc", [128, TT])
    rows = A("rows", [1, 2048])

    dma_in(p, ident, idc)
    t = dma_in(p, sb_vec, vec)
    dma_in(p, sb_cst, cst)
    dma_in(p, w1, wq.rearrange("(a p) b -> p a b", p=128))
    dma_in(p, w2, wk.rearrange("(a p) b -> p a b", p=128))
    dma_in(p, w3, wv.rearrange("(a p) b -> p a b", p=128))
    p.emit("vector", lambda e: e.memset(ones_col, 1.0), inc=True)
    p.emit("vector", lambda e: e.memset(ones_row, 1.0), inc=True)
    p.barrier()
    eps_rms = sb_cst[:, 1:2]

    for it in range(ntt):
        t0 = it * TT
        dma_in(p, xa, hs[:, :, t0:t0 + TT].rearrange("c p t -> p c t"))
        p.barrier()
        ln_tile(p, xa, TT, xsq, a_bc, b_bc, rows, ones_col, ones_row, eps_rms,
                "rms", xh, w_col=sb_vec)
        p.barrier()
        for wi, (wsb, dst, rope) in enumerate(((w1, QT, True), (w2, KT, True),
                                               (w3, Vtok, False))):
            for st in range(TT // 128):
                tok0 = t0 + st * 128
                p.w("vector", p.last_store)
                for half in range(2):
                    b = p.bank()
                    for kc in range(NCH):
                        tpe = p.emit("tensor",
                                     lambda e, b=b, kc=kc, st=st, half=half, wsb=wsb:
                                     e.matmul(p.banks[b][:, 0:512],
                                              xh[:, kc, st * 128:(st + 1) * 128],
                                              wsb[:, kc, half * 512:(half + 1) * 512],
                                              start=(kc == 0), stop=(kc == NCH - 1)),
                                     inc=(kc == NCH - 1))
                    p.w("vector", tpe)
                    tev = p.emit("vector", lambda e, b=b, half=half: e.tensor_copy(
                        ptile[:, half * 512:(half + 1) * 512], p.banks[b][:, 0:512]),
                        inc=True)
                    p.bank_done(b, tev)
                src = ptile
                if rope:
                    p.w("sync", p.t("qv"))
                    dma_in(p, cs[:, 0:32], cos_d[tok0:tok0 + 128, :])
                    dma_in(p, cs[:, 32:64], sin_d[tok0:tok0 + 128, :])
                    p.barrier(("vector",))
                    v = ptile.rearrange("p (h d) -> p h d", h=16)
                    o = ptile2.rearrange("p (h d) -> p h d", h=16)
                    sq16 = ptile3.rearrange("p (h d) -> p h d", h=16)
                    cosb = cs[:, 0:32].rearrange("p (o t) -> p o t", o=1).to_broadcast([128, 16, 32])
                    sinb = cs[:, 32:64].rearrange("p (o t) -> p o t", o=1).to_broadcast([128, 16, 32])
                    p.emit("vector", lambda e: e.tensor_tensor(
                        o[:, :, 0:32], v[:, :, 0:32], cosb, ALU.mult), inc=True)
                    t2 = p.emit("vector", lambda e: e.tensor_tensor(
                        sq16, v[:, :, 32:64], sinb, ALU.mult), inc=True)
                    p.w("vector", t2)
                    p.emit("vector", lambda e: e.tensor_tensor(
                        o[:, :, 0:32], o[:, :, 0:32], sq16, ALU.subtract), inc=True)
                    t4 = p.emit("vector", lambda e: e.tensor_tensor(
                        o[:, :, 32:64], v[:, :, 32:64], cosb, ALU.mult), inc=True)
                    p.w("vector", t4)
                    t5 = p.emit("vector", lambda e: e.tensor_tensor(
                        sq16, v[:, :, 0:32], sinb, ALU.mult), inc=True)
                    p.w("vector", t5)
                    p.emit("vector", lambda e: e.tensor_tensor(
                        o[:, :, 32:64], o[:, :, 32:64], sq16, ALU.add), inc=True)
                    src = ptile2
                    p.barrier(("tensor",))
                if dst is Vtok:
                    p.barrier(("sync",))
                    p.last_store = dma_out(p, Vtok[tok0:tok0 + 128, :], src)
                else:
                    for j in range(NCH):
                        b = p.bank()
                        tpe = p.emit("tensor", lambda e, b=b, j=j, src=src: e.transpose(
                            p.banks[b][:, 0:128], src[:, j * 128:(j + 1) * 128], ident),
                            inc=True)
                        p.w("vector", tpe)
                        tev = p.emit("vector", lambda e, b=b, j=j, st=st: e.tensor_copy(
                            oT[:, j, st * 128:(st + 1) * 128], p.banks[b][:, 0:128]),
                            inc=True)
                        p.bank_done(b, tev)
            if dst is not Vtok:
                p.barrier(("sync",))
                p.last_store = dma_out(
                    p, dst[:, :, t0:t0 + TT].rearrange("c p t -> p c t"), oT)
            p.barrier()
    p.barrier()
    p.pop_scope()


def emit_dec_attn(p, pfx, S):
    nc = p.nc
    p.push_scope()
    nq = S // 128

    D = lambda n, s, k: p.dram(pfx + n, s, k)
    QT = D("QT", [NCH, 128, S], "ExternalInput")
    KT = D("KT", [NCH, 128, S], "ExternalInput")
    Vtok = D("Vtok", [S, C], "ExternalInput")
    maskn = D("maskn", [128, 128], "ExternalInput")
    cst = D("cst", [128, 8], "ExternalInput")
    idc = D("identity", [128, 128], "ExternalInput")
    Otok = D("Otok", [S, C], "ExternalOutput")

    A = lambda n, s, dt=F32: p.sbuf(pfx + n, s, dt)
    ident = A("ident", [128, 128])
    qp2 = [A(f"qp{i}", [128, S]) for i in range(2)]
    kp2 = [A(f"kp{i}", [128, S]) for i in range(2)]
    vp2 = [A(f"vp{i}", [128, S // 128, 128]) for i in range(2)]
    op = A("op", [128, S // 128, 128])
    arow = [A(f"arow{i}", [128, S]) for i in range(2)]
    msk = A("msk", [128, 128])
    at4 = A("at4", [128, 512])
    lsum = [A(f"lsum{i}", [128, 4]) for i in range(2)]
    sb_cst = A("sb_cst", [128, 8])

    dma_in(p, msk, maskn)
    dma_in(p, ident, idc)
    dma_in(p, sb_cst, cst)
    c125 = sb_cst[:, 2:3]

    def emit_qk(h2, qt, bi):
        qp, kp = qkv_cur[0], qkv_cur[1]
        hb = h2 * 64
        W = (qt + 1) * 128
        tev = None
        for s0 in range(0, W, 512):
            n = min(512, W - s0)
            b = p.bank()
            tpe = p.emit("tensor", lambda e, b=b, qt=qt, s0=s0, n=n, hb=hb:
                         e.matmul(p.banks[b][:, 0:n],
                                  qp[hb:hb + 64, qt * 128:(qt + 1) * 128],
                                  kp[hb:hb + 64, s0:s0 + n],
                                  start=True, stop=True), inc=True)
            p.w("vector", tpe)
            tev = p.emit("vector", lambda e, b=b, s0=s0, n=n, bi=bi: e.tensor_copy(
                arow[bi][:, s0:s0 + n], p.banks[b][:, 0:n]), inc=True)
            p.bank_done(b, tev)
        return tev

    def emit_softmax(qt, bi, tev):
        W = (qt + 1) * 128
        p.w("vector", tev)
        t = p.emit("vector", lambda e, qt=qt, W=W, bi=bi: e.tensor_tensor(
            arow[bi][:, qt * 128:W], arow[bi][:, qt * 128:W], msk, ALU.add),
            inc=True)
        p.w("vector", t)
        t = p.emit("vector", lambda e, W=W, bi=bi: e.reduce_max(
            lsum[bi][:, 1:2], arow[bi][:, 0:W], axis=AX.X, negate=True), inc=True)
        p.w("vector", t)
        t = p.emit("vector", lambda e, bi=bi: e.tensor_scalar(
            lsum[bi][:, 1:2], lsum[bi][:, 1:2], c125, None, ALU.mult), inc=True)
        p.w("scalar", t)
        te = p.emit("scalar", lambda e, W=W, bi=bi: e.activation(
            arow[bi][:, 0:W], arow[bi][:, 0:W], AF.Exp, bias=lsum[bi][:, 1:2],
            scale=c125, accum_out=lsum[bi][:, 0:1]), inc=True)
        p.w("vector", te)
        tl = p.emit("vector", lambda e, bi=bi: e.reciprocal(
            lsum[bi][:, 2:3], lsum[bi][:, 0:1]), inc=True)
        return te, tl

    def emit_av(h2, qt, bi, te, tl):
        vp = qkv_cur[2]
        hb = h2 * 64
        ob = p.bank7()
        nkt = qt + 1
        n_av = 0
        for g0 in range(0, nkt, 4):
            gn = min(4, nkt - g0)
            tb = p.bank()
            p.w("tensor", te)
            for i in range(gn):
                tpe = p.emit("tensor", lambda e, tb=tb, g0=g0, i=i, bi=bi:
                             e.transpose(
                                 p.banks[tb][:, i * 128:(i + 1) * 128],
                                 arow[bi][:, (g0 + i) * 128:(g0 + i + 1) * 128],
                                 ident), inc=(i == gn - 1))
            p.w("vector", tpe)
            tev = p.emit("vector", lambda e, tb=tb, gn=gn: e.tensor_copy(
                at4[:, 0:gn * 128], p.banks[tb][:, 0:gn * 128]), inc=True)
            p.bank_done(tb, tev)
            p.w("tensor", tev)
            for i in range(gn):
                p.emit("tensor", lambda e, ob=ob, g0=g0, i=i, hb=hb,
                       first=(n_av == 0 and i == 0):
                       e.matmul(p.banks[ob][:, 0:64],
                                at4[:, i * 128:(i + 1) * 128],
                                vp[:, g0 + i, hb:hb + 64],
                                start=first, stop=False,
                                skip_group_check=True),
                       inc=(i == gn - 1))
            n_av += gn
        tpe = p.t("qt")
        p.w("vector", tpe)
        p.w("vector", tl)
        tev = p.emit("vector", lambda e, ob=ob, qt=qt, hb=hb, bi=bi: e.tensor_scalar(
            op[:, qt, hb:hb + 64], p.banks[ob][:, 0:64], lsum[bi][:, 2:3], None,
            ALU.mult), inc=True)
        p.bank_done(ob, tev)

    use_j = {}
    ld_j = {}
    qkv_cur = [None, None, None]

    def ld_pair(j, pi):
        p.w("sync", use_j.get(pi))
        t1 = dma_in(p, qp2[pi], QT[j])
        t2 = dma_in(p, kp2[pi], KT[j])
        t3 = dma_in(p, vp2[pi], Vtok[:, j * 128:(j + 1) * 128].rearrange(
            "(c p) f -> p c f", p=128))
        ld_j[pi] = (t1, t2, t3)

    p.barrier()
    ld_pair(0, 0)
    ost = None
    for j in range(NCH):
        pi = j % 2
        if j + 1 < NCH:
            ld_pair(j + 1, 1 - pi)
        qkv_cur[0], qkv_cur[1], qkv_cur[2] = qp2[pi], kp2[pi], vp2[pi]
        for tt in ld_j[pi]:
            p.w("tensor", tt)
        p.w("vector", ost)
        jobs = [(h2, qt) for h2 in range(2) for qt in range(nq)]
        pend = None
        for ji, (h2, qt) in enumerate(jobs):
            bi = ji % 2
            tev = emit_qk(h2, qt, bi)
            if pend is not None:
                emit_av(*pend)
            te, tl = emit_softmax(qt, bi, tev)
            pend = (h2, qt, bi, te, tl)
        emit_av(*pend)
        use_j[pi] = p.t("qt")
        p.w("sync", p.t("qv"))
        ost = dma_out(p, Otok[:, j * 128:(j + 1) * 128].rearrange(
            "(c p) f -> p c f", p=128), op)
    p.barrier()
    p.barrier()
    p.pop_scope()


def emit_dec_tail(p, pfx, S):
    nc = p.nc
    p.push_scope()
    ntt = S // TT
    FB = 512
    NFB = 4096 // FB

    D = lambda n, s, k: p.dram(pfx + n, s, k)
    hs = D("hs", [NCH, 128, S], "ExternalInput")
    Otok = D("Otok", [S, C], "ExternalInput")
    wo = D("wo", [C, C], "ExternalInput")
    wg = D("wg", [C, 4 * C], "ExternalInput")
    wu = D("wu", [C, 4 * C], "ExternalInput")
    wd = D("wd", [4 * C, C], "ExternalInput")
    vec = D("vec", [128, NCH], "ExternalInput")
    cst = D("cst", [128, 8], "ExternalInput")
    idc = D("identity", [128, 128], "ExternalInput")
    proc = D("proc", [NCH, 128, S], "ExternalOutput")
    x2_d = D("x2_d", [NCH, 128, S], "Internal")

    A = lambda n, s, dt=F32: p.sbuf(pfx + n, s, dt)
    ident = A("ident", [128, 128])
    ones_col = A("ones_col", [128, 1])
    ones_row = A("ones_row", [1, 128])
    sb_vec = A("sb_vec", [128, NCH])
    sb_cst = A("sb_cst", [128, 8])
    xa = A("xa", [128, NCH, TT])
    xb = A("xb", [128, NCH, TT])
    xsq = A("xsq", [128, NCH, TT])
    osb = A("osb", [128, NCH, TT])
    gsb = A("gsb", [128, 4, TT])
    husb = A("husb", [128, 4, TT])
    a_bc = A("a_bc", [128, TT])
    b_bc = A("b_bc", [128, TT])
    rows = A("rows", [1, 2048])
    inner = ExitStack()
    Ai = lambda n, s, dt=F32: inner.enter_context(
        nc.sbuf_tensor(pfx + n, s, dt)).ap()
    wosb = Ai("wosb", [128, NCH, C])

    dma_in(p, ident, idc)
    dma_in(p, sb_vec, vec)
    dma_in(p, sb_cst, cst)
    dma_in(p, wosb, wo.rearrange("(a p) b -> p a b", p=128))
    p.emit("vector", lambda e: e.memset(ones_col, 1.0), inc=True)
    p.emit("vector", lambda e: e.memset(ones_row, 1.0), inc=True)
    p.barrier()
    eps_rms = sb_cst[:, 1:2]

    otile = Ai("otile", [128, C])
    for it in range(ntt):
        t0 = it * TT
        dma_in(p, xa, hs[:, :, t0:t0 + TT].rearrange("c p t -> p c t"))
        for st in range(TT // 128):
            p.w("sync", p.t("qt"))
            tld = dma_in(p, otile, Otok[t0 + st * 128:t0 + st * 128 + 128, :])
            p.w("tensor", tld)
            for j in range(NCH):
                b = p.bank()
                tpe = p.emit("tensor", lambda e, b=b, j=j: e.transpose(
                    p.banks[b][:, 0:128], otile[:, j * 128:(j + 1) * 128], ident),
                    inc=True)
                p.w("vector", tpe)
                tev = p.emit("vector", lambda e, b=b, j=j, st=st: e.tensor_copy(
                    xb[:, j, st * 128:(st + 1) * 128], p.banks[b][:, 0:128]), inc=True)
                p.bank_done(b, tev)
        p.barrier(("tensor",))

        def evac(m, b, pet):
            p.w("vector", pet)
            return p.emit("vector", lambda e, m=m, b=b: e.tensor_tensor(
                osb[:, m, :], p.banks[b][:, 0:TT], xa[:, m, :], ALU.add), inc=True)
        proj_fm(p, evac, xb, wosb, TT)
        p.barrier(("sync",))
        dma_out(p, x2_d[:, :, t0:t0 + TT].rearrange("c p t -> p c t"), osb)
        p.barrier()

    inner.close()
    wgb = [A(f"wgb{i}", [128, NCH, FB]) for i in range(2)]
    wub = [A(f"wub{i}", [128, NCH, FB]) for i in range(2)]
    wdb = [A(f"wdb{i}", [128, FB // 128, C]) for i in range(2)]
    wuse = {}
    wld = {}

    def ld_fb(fb, wi):
        f0 = fb * FB
        p.w("sync", wuse.get(wi))
        t1 = dma_in(p, wgb[wi], wg[:, f0:f0 + FB].rearrange("(a p) b -> p a b",
                                                            p=128))
        t2 = dma_in(p, wub[wi], wu[:, f0:f0 + FB].rearrange("(a p) b -> p a b",
                                                            p=128))
        t3 = dma_in(p, wdb[wi], wd[f0:f0 + FB, :].rearrange("(a p) b -> p a b",
                                                            p=128))
        wld[wi] = (t1, t2, t3)

    for it in range(ntt):
        t0 = it * TT
        dma_in(p, xa, x2_d[:, :, t0:t0 + TT].rearrange("c p t -> p c t"))
        p.barrier()
        ln_tile(p, xa, TT, xsq, a_bc, b_bc, rows, ones_col, ones_row, eps_rms,
                "rms", xb, w_col=sb_vec)
        p.barrier()
        ld_fb(0, 0)
        for fb in range(NFB):
            wi = fb % 2
            if fb + 1 < NFB:
                ld_fb(fb + 1, 1 - wi)
            for tt_ in wld[wi]:
                p.w("tensor", tt_)
            for m in range(FB // 128):
                b = p.bank()
                for kc in range(NCH):
                    tpe = p.emit("tensor", lambda e, b=b, m=m, kc=kc, wi=wi: e.matmul(
                        p.banks[b][:, 0:TT], wgb[wi][:, kc, m * 128:(m + 1) * 128],
                        xb[:, kc, 0:TT], start=(kc == 0), stop=(kc == NCH - 1)),
                        inc=(kc == NCH - 1))
                p.w("scalar", tpe)
                tev = p.emit("scalar", lambda e, b=b, m=m: e.activation(
                    gsb[:, m, :], p.banks[b][:, 0:TT], AF.Silu), inc=True)
                p.bank_done(b, tev)
            for m in range(FB // 128):
                b = p.bank()
                for kc in range(NCH):
                    tpe = p.emit("tensor", lambda e, b=b, m=m, kc=kc, wi=wi: e.matmul(
                        p.banks[b][:, 0:TT], wub[wi][:, kc, m * 128:(m + 1) * 128],
                        xb[:, kc, 0:TT], start=(kc == 0), stop=(kc == NCH - 1)),
                        inc=(kc == NCH - 1))
                p.w("vector", tpe)
                p.w("vector", p.t("qs"))
                tev = p.emit("vector", lambda e, b=b, m=m: e.tensor_tensor(
                    husb[:, m, :], p.banks[b][:, 0:TT], gsb[:, m, :], ALU.mult),
                    inc=True)
                p.bank_done(b, tev)
            p.barrier(("tensor",))
            for m in range(NCH):
                b = p.bank()
                for kf in range(FB // 128):
                    tpe = p.emit("tensor", lambda e, b=b, m=m, kf=kf, wi=wi: e.matmul(
                        p.banks[b][:, 0:TT], wdb[wi][:, kf, m * 128:(m + 1) * 128],
                        husb[:, kf, 0:TT], start=(kf == 0), stop=(kf == FB // 128 - 1)),
                        inc=(kf == FB // 128 - 1))
                p.w("vector", tpe)
                p.w("vector", p.t("qv"))
                if fb == 0:
                    tev = p.emit("vector", lambda e, b=b, m=m: e.tensor_tensor(
                        osb[:, m, :], p.banks[b][:, 0:TT], xa[:, m, :], ALU.add),
                        inc=True)
                else:
                    tev = p.emit("vector", lambda e, b=b, m=m: e.tensor_tensor(
                        osb[:, m, :], p.banks[b][:, 0:TT], osb[:, m, :], ALU.add),
                        inc=True)
                p.bank_done(b, tev)
            wuse[wi] = p.t("qt")
        p.barrier(("sync",))
        dma_out(p, proc[:, :, t0:t0 + TT].rearrange("c p t -> p c t"), osb)
        p.barrier()
    p.barrier()
    p.pop_scope()


def emit_stage3(p, Tl=T):
    nc = p.nc
    p.push_scope()
    ntt = Tl // TT

    D = lambda n, s, k: p.dram("s3_" + n, s, k)
    xn0 = D("xn0", [NCH, 128, Tl], "ExternalInput")
    proc = D("proc", [NCH, 128, Tl], "ExternalInput")
    pw = D("pw", [C, 2 * C], "ExternalInput")
    vec = D("vec", [128, NCH], "ExternalInput")
    cst = D("cst", [128, 8], "ExternalInput")
    idc = D("identity", [128, 128], "ExternalInput")
    dst_d = D("dst", [1, Tl], "ExternalOutput")
    dch_d = D("dch", [1, Tl], "ExternalOutput")
    parts = D("parts", [1, 2], "ExternalOutput")
    ptok = D("ptok", [Tl, C], "ExternalOutput")
    f0tok = D("f0tok", [Tl, C], "ExternalOutput")

    A = lambda n, s, dt=F32: p.sbuf("s3_" + n, s, dt)
    ident = A("ident", [128, 128])
    ones_col = A("ones_col", [128, 1])
    ones_row = A("ones_row", [1, 128])
    sb_vec = A("sb_vec", [128, NCH])
    sb_cst = A("sb_cst", [128, 8])
    pwsb = A("pwsb", [128, NCH, 2 * C])
    xa = A("xa", [128, NCH, TT])
    xb = A("xb", [128, NCH, TT])
    xc = A("xc", [128, NCH, TT])
    xd = A("xd", [128, NCH, TT])
    xsq = A("xsq", [128, NCH, TT])
    musb = A("musb", [128, NCH, TT])
    lvsb = A("lvsb", [128, NCH, TT])
    otile = A("otile", [128, C])
    a_bc = A("a_bc", [128, TT])
    b_bc = A("b_bc", [128, TT])
    rows = A("rows", [1, 2048])
    racc = A("racc", [1, 4])

    dma_in(p, ident, idc)
    dma_in(p, sb_vec, vec)
    dma_in(p, sb_cst, cst)
    dma_in(p, pwsb, pw.rearrange("(a p) b -> p a b", p=128))
    p.emit("vector", lambda e: e.memset(ones_col, 1.0), inc=True)
    p.emit("vector", lambda e: e.memset(ones_row, 1.0), inc=True)
    p.emit("vector", lambda e: e.memset(racc, 0.0), inc=True)
    p.barrier()
    eps_rms = sb_cst[:, 1:2]
    c_st = sb_cst[:, 3:4]
    c_ch = sb_cst[:, 4:5]
    c_nh = sb_cst[:, 5:6]

    for it in range(ntt):
        t0 = it * TT
        dma_in(p, xa, xn0[:, :, t0:t0 + TT].rearrange("c p t -> p c t"))
        dma_in(p, xb, proc[:, :, t0:t0 + TT].rearrange("c p t -> p c t"))
        p.barrier()
        for st in range(TT // 128):
            for j in range(NCH):
                b = p.bank()
                tpe = p.emit("tensor", lambda e, b=b, j=j, st=st: e.transpose(
                    p.banks[b][:, 0:128], xb[:, j, st * 128:(st + 1) * 128], ident),
                    inc=True)
                p.w("vector", tpe)
                tev = p.emit("vector", lambda e, b=b, j=j: e.tensor_copy(
                    otile[:, j * 128:(j + 1) * 128], p.banks[b][:, 0:128]), inc=True)
                p.bank_done(b, tev)
            p.barrier(("sync",))
            t1 = dma_out(p, ptok[t0 + st * 128:t0 + st * 128 + 128, :], otile)
            t2 = dma_out(p, f0tok[t0 + st * 128:t0 + st * 128 + 128, :], otile)
            p.w("vector", t1)
            p.w("vector", t2)
        ln_tile(p, xa, TT, xsq, a_bc, b_bc, rows, ones_col, ones_row, eps_rms,
                "rms", xc, w_col=sb_vec)
        p.barrier(("tensor",))
        for m in range(16):
            b = p.bank()
            for kc in range(NCH):
                tpe = p.emit("tensor", lambda e, b=b, m=m, kc=kc: e.matmul(
                    p.banks[b][:, 0:TT], pwsb[:, kc, m * 128:(m + 1) * 128],
                    xc[:, kc, 0:TT], start=(kc == 0), stop=(kc == NCH - 1)),
                    inc=(kc == NCH - 1))
            p.w("vector", tpe)
            dstt = musb if m < 8 else lvsb
            tev = p.emit("vector", lambda e, b=b, m=m, dstt=dstt: e.tensor_copy(
                dstt[:, m % 8, :], p.banks[b][:, 0:TT]), inc=True)
            p.bank_done(b, tev)
        for kc in range(NCH):
            p.emit("vector", lambda e, kc=kc: e.tensor_tensor(
                xd[:, kc, :], xb[:, kc, :], xa[:, kc, :], ALU.subtract), inc=True)
        tres = p.t("qv")
        p.w("scalar", tres)
        t = p.emit("scalar", lambda e: e.activation(xsq, xd, AF.Square), inc=True)
        b = p.bank()
        p.w("tensor", t)
        for kc in range(NCH):
            tpe = p.emit("tensor", lambda e, b=b, kc=kc: e.matmul(
                p.banks[b][0:1, 0:TT], ones_col, xsq[:, kc, 0:TT],
                start=(kc == 0), stop=(kc == NCH - 1)), inc=(kc == NCH - 1))
        p.w("vector", tpe)
        trow = p.emit("vector", lambda e, b=b: e.tensor_scalar(
            rows[0:1, 0:TT], p.banks[b][0:1, 0:TT], 1.0 / 1024.0, None, ALU.mult),
            inc=True)
        p.bank_done(b, trow)
        p.w("sync", trow)
        dma_out(p, dst_d[:, t0:t0 + TT], rows[0:1, 0:TT])
        p.w("vector", trow)
        t = p.emit("vector", lambda e: e.reduce_sum(rows[0:1, 1024:1025],
                                                    rows[0:1, 0:TT], axis=AX.X),
                   inc=True)
        p.w("vector", t)
        t = p.emit("vector", lambda e: e.tensor_tensor(
            racc[0:1, 0:1], racc[0:1, 0:1], rows[0:1, 1024:1025], ALU.add), inc=True)
        for kc in range(NCH):
            tt1 = p.emit("vector", lambda e, kc=kc: e.tensor_tensor(
                musb[:, kc, :], xd[:, kc, :], musb[:, kc, :], ALU.subtract), inc=True)
        tdm = p.t("qv")
        p.w("scalar", tdm)
        t = p.emit("scalar", lambda e: e.activation(xsq, musb, AF.Square), inc=True)
        p.w("scalar", t)
        t = p.emit("scalar", lambda e: e.activation(musb, lvsb, AF.Copy, scale=-1.0),
                   inc=True)
        p.w("scalar", t)
        t = p.emit("scalar", lambda e: e.activation(musb, musb, AF.Exp), inc=True)
        p.w("vector", t)
        for kc in range(NCH):
            t = p.emit("vector", lambda e, kc=kc: e.tensor_scalar(
                xsq[:, kc, :], xsq[:, kc, :], 1.0, None, ALU.add), inc=True)
            p.w("vector", t)
            t = p.emit("vector", lambda e, kc=kc: e.tensor_tensor(
                xsq[:, kc, :], xsq[:, kc, :], musb[:, kc, :], ALU.mult), inc=True)
            p.w("vector", t)
            t = p.emit("vector", lambda e, kc=kc: e.tensor_tensor(
                xsq[:, kc, :], xsq[:, kc, :], lvsb[:, kc, :], ALU.add), inc=True)
        tch = p.t("qv")
        b = p.bank()
        p.w("tensor", tch)
        for kc in range(NCH):
            tpe = p.emit("tensor", lambda e, b=b, kc=kc: e.matmul(
                p.banks[b][0:1, 0:TT], ones_col, xsq[:, kc, 0:TT],
                start=(kc == 0), stop=(kc == NCH - 1)), inc=(kc == NCH - 1))
        p.w("vector", tpe)
        trow = p.emit("vector", lambda e, b=b: e.tensor_scalar(
            rows[0:1, 512:512 + TT], p.banks[b][0:1, 0:TT], 0.5 / 1024.0, -0.5,
            ALU.mult, ALU.add), inc=True)
        p.bank_done(b, trow)
        p.w("sync", trow)
        dma_out(p, dch_d[:, t0:t0 + TT], rows[0:1, 512:512 + TT])
        p.w("vector", trow)
        t = p.emit("vector", lambda e: e.reduce_sum(rows[0:1, 1030:1031],
                                                    rows[0:1, 512:512 + TT],
                                                    axis=AX.X), inc=True)
        p.w("vector", t)
        t = p.emit("vector", lambda e: e.tensor_tensor(
            racc[0:1, 1:2], racc[0:1, 1:2], rows[0:1, 1030:1031], ALU.add), inc=True)
        p.barrier()
    p.w("sync", p.t("qv"))
    dma_out(p, parts, racc[0:1, 0:2])
    p.barrier()
    p.pop_scope()


def emit_stage4(p, Tl=T):
    nc = p.nc
    p.push_scope()
    K = Tl // 4
    NT = Tl // 128
    KC = K // 128

    D = lambda n, s, k, dt=F32: p.dram("s4_" + n, s, k, dt)
    dst_d = D("dst", [1, Tl], "ExternalInput")
    dch_d = D("dch", [1, Tl], "ExternalInput")
    if p.fused:
        parts = p.dram("G_ccdst", [128, 8], "Internal")
    else:
        parts = D("parts", [NCORES, 2], "ExternalInput")
    ptok = D("ptok", [Tl, C], "ExternalInput")
    cos_d = D("cos", [Tl, 32], "ExternalInput")
    sin_d = D("sin", [Tl, 32], "ExternalInput")
    iosb_d = D("iota", [128, Tl], "ExternalInput")
    ivals_d = D("ivals", [128, NT], "ExternalInput")
    prem_d = D("prem", [128, 128], "ExternalInput")
    cst = D("cst", [128, 8], "ExternalInput")
    idc = D("identity", [128, 128], "ExternalInput")

    g2_d = D("g2", [128, NT], "Internal")
    isc_d = D("isc", [K, 8], "ExternalOutput")
    seltok = D("seltok", [K, C], "ExternalOutput")
    selfm = D("selfm", [NCH, 128, K], "ExternalOutput")
    cossel = D("cossel", [K, 32], "ExternalOutput")
    sinsel = D("sinsel", [K, 32], "ExternalOutput")

    A = lambda n, s, dt=F32: p.sbuf("s4_" + n, s, dt)
    ident = A("ident", [128, 128])
    ones_col = A("ones_col", [128, 1])
    ones_row = A("ones_row", [1, 128])
    sb_cst = A("sb_cst", [128, 8])
    grow = A("grow", [1, Tl])
    drow = A("drow", [1, Tl])
    gbc = A("gbc", [128, Tl])
    iobc = A("iobc", [128, Tl])
    ivals = A("ivals_sb", [128, NT])
    prem = A("prem_sb", [128, 128])
    gcol = A("gcol", [128, NT])
    rank = A("rank", [128, NT])
    sel = A("sel", [128, NT])
    slot = A("slot", [128, NT])
    s2 = A("s2", [128, NT])
    tmp1 = A("tmp1", [128, Tl])
    tmp2 = A("tmp2", [128, Tl])
    tmp3 = A("tmp3", [128, Tl])
    psb = A("psb", [NCORES, 2])
    mrow = A("mrow", [1, 2])
    u32c = A("u32c", [128, NT], U32)
    idxu = A("idxu", [128, KC], U32)
    combo = A("combo", [128, NT, 8])
    iscsb = A("iscsb", [128, KC, 8])
    stile = A("stile", [128, C])
    sfmt = A("sfmt", [128, NCH, 128])
    cstile = A("cstile", [128, 64])

    dma_in(p, ident, idc)
    dma_in(p, sb_cst, cst)
    dma_in(p, grow, dst_d)
    dma_in(p, drow, dch_d)
    if p.fused:
        dma_in(p, mrow, parts[0:1, 0:2])
    else:
        dma_in(p, psb, parts)
    dma_in(p, iobc, iosb_d)
    dma_in(p, ivals, ivals_d)
    dma_in(p, prem, prem_d)
    p.emit("vector", lambda e: e.memset(ones_col, 1.0), inc=True)
    p.emit("vector", lambda e: e.memset(ones_row, 1.0), inc=True)
    p.barrier()
    cK = sb_cst[:, 6:7]
    cBIG = sb_cst[:, 7:8]

    if p.fused:
        t = p.emit("vector", lambda e: e.tensor_scalar(
            mrow, mrow, 1.0 / (NCORES * Tl), None, ALU.mult), inc=True)
        p.w("vector", t)
    else:
        b = p.bank()
        t = p.emit("tensor", lambda e, b=b: e.matmul(p.banks[b][0:1, 0:2],
                                                     ones_col[0:8, :], psb,
                                                     start=True, stop=True), inc=True)
        p.w("vector", t)
        t = p.emit("vector", lambda e, b=b: e.tensor_scalar(
            mrow, p.banks[b][0:1, 0:2], 1.0 / (NCORES * Tl), None, ALU.mult),
            inc=True)
        p.bank_done(b, t)
        p.w("vector", t)
    t = p.emit("vector", lambda e: e.tensor_tensor(
        mrow[0:1, 0:1], mrow[0:1, 0:1], mrow[0:1, 1:2], ALU.add), inc=True)
    p.w("vector", t)
    t = p.emit("vector", lambda e: e.tensor_tensor(grow, grow, drow, ALU.add),
               inc=True)
    p.w("vector", t)
    t = p.emit("vector", lambda e: e.tensor_scalar(
        grow, grow, mrow[0:1, 0:1], None, ALU.subtract), inc=True)
    p.w("scalar", t)
    t = p.emit("scalar", lambda e: e.activation(grow, grow, AF.Sigmoid), inc=True)
    p.w("sync", t)
    tg = dma_out(p, g2_d.rearrange("p t -> (p t)").rearrange("(o f) -> o f", o=1), grow)
    p.w("sync", tg)
    tgl = dma_in(p, gcol, g2_d)
    p.w("tensor", t)
    for s0 in range(0, Tl, 512):
        b = p.bank()
        tpe = p.emit("tensor", lambda e, b=b, s0=s0: e.matmul(
            p.banks[b][:, 0:512], ones_row, grow[0:1, s0:s0 + 512],
            start=True, stop=True), inc=True)
        p.w("vector", tpe)
        tev = p.emit("vector", lambda e, b=b, s0=s0: e.tensor_copy(
            gbc[:, s0:s0 + 512], p.banks[b][:, 0:512]), inc=True)
        p.bank_done(b, tev)
    p.w("vector", tgl)
    p.barrier(("vector",))
    for tc in range(NT):
        t1 = p.emit("vector", lambda e, tc=tc: e.tensor_scalar(
            tmp1, gbc, gcol[:, tc:tc + 1], None, ALU.is_gt), inc=True)
        t2 = p.emit("vector", lambda e, tc=tc: e.tensor_scalar(
            tmp2, gbc, gcol[:, tc:tc + 1], None, ALU.is_equal), inc=True)
        t3 = p.emit("vector", lambda e, tc=tc: e.tensor_scalar(
            tmp3, iobc, ivals[:, tc:tc + 1], None, ALU.is_lt), inc=True)
        p.w("vector", t3)
        t4 = p.emit("vector", lambda e: e.tensor_tensor(tmp2, tmp2, tmp3, ALU.mult),
                    inc=True)
        p.w("vector", t4)
        t5 = p.emit("vector", lambda e: e.tensor_tensor(tmp1, tmp1, tmp2, ALU.add),
                    inc=True)
        p.w("vector", t5)
        t6 = p.emit("vector", lambda e, tc=tc: e.reduce_sum(
            rank[:, tc:tc + 1], tmp1, axis=AX.X), inc=True)
        p.w("vector", t6)
    t = p.emit("vector", lambda e: e.tensor_scalar(sel, rank, cK, None, ALU.is_lt),
               inc=True)
    p.w("vector", t)
    t = p.emit("vector", lambda e: e.tensor_tensor_scan(
        slot, ones_col.to_broadcast([128, NT]), sel, 0.0, ALU.mult, ALU.add), inc=True)
    p.w("vector", t)
    p.w("tensor", t)
    b = p.bank()
    tpe = p.emit("tensor", lambda e, b=b: e.matmul(
        p.banks[b][:, 0:1], prem, slot[:, NT - 1:NT], start=True, stop=True), inc=True)
    p.w("vector", tpe)
    toffs = p.emit("vector", lambda e, b=b: e.tensor_copy(
        s2[:, 0:1], p.banks[b][:, 0:1]), inc=True)
    p.bank_done(b, toffs)
    p.w("vector", toffs)
    t = p.emit("vector", lambda e: e.tensor_tensor(slot, slot, sel, ALU.subtract),
               inc=True)
    p.w("vector", t)
    t = p.emit("vector", lambda e: e.tensor_scalar(
        slot, slot, s2[:, 0:1], None, ALU.add), inc=True)
    p.w("vector", t)
    t = p.emit("vector", lambda e: e.tensor_scalar(
        tmp1[:, 0:NT], slot, 128.0, None, ALU.is_ge), inc=True)
    for j in range(2, KC):
        p.w("vector", t)
        t = p.emit("vector", lambda e, j=j: e.tensor_scalar(
            tmp2[:, 0:NT], slot, float(j) * 128.0, None, ALU.is_ge), inc=True)
        p.w("vector", t)
        t = p.emit("vector", lambda e: e.tensor_tensor(
            tmp1[:, 0:NT], tmp1[:, 0:NT], tmp2[:, 0:NT], ALU.add), inc=True)
    p.w("vector", t)
    t = p.emit("vector", lambda e: e.scalar_tensor_tensor(
        s2, tmp1[:, 0:NT], -128.0, slot, ALU.mult, ALU.add), inc=True)
    p.w("vector", t)
    t = p.emit("vector", lambda e: e.scalar_tensor_tensor(
        s2, s2, float(KC), tmp1[:, 0:NT], ALU.mult, ALU.add), inc=True)
    p.w("vector", t)
    t = p.emit("vector", lambda e: e.tensor_scalar(
        tmp1[:, 0:NT], sel, -1.0, 1.0, ALU.mult, ALU.add), inc=True)
    p.w("vector", t)
    t = p.emit("vector", lambda e: e.tensor_scalar(
        tmp1[:, 0:NT], tmp1[:, 0:NT], cBIG, None, ALU.mult), inc=True)
    p.w("vector", t)
    t = p.emit("vector", lambda e: e.tensor_tensor(s2, s2, sel, ALU.mult), inc=True)
    p.w("vector", t)
    t = p.emit("vector", lambda e: e.tensor_tensor(s2, s2, tmp1[:, 0:NT], ALU.add),
               inc=True)
    p.w("vector", t)
    tu = p.emit("vector", lambda e: e.tensor_copy(u32c, s2), inc=True)
    p.w("gpsimd", tu)
    t = p.emit("vector", lambda e: e.memset(combo, 0.0), inc=True)
    p.w("vector", t)
    t = p.emit("vector", lambda e: e.tensor_copy(combo[:, :, 0:1],
                                                 ivals.rearrange(
                                                     "p (t o) -> p t o", o=1)),
               inc=True)
    p.w("vector", t)
    t = p.emit("vector", lambda e: e.tensor_copy(combo[:, :, 1:2],
                                                 gcol.rearrange(
                                                     "p (t o) -> p t o", o=1)),
               inc=True)
    p.w("gpsimd", t)
    tprev = None
    for tc in range(NT):
        p.w("gpsimd", tprev)
        tprev = p.dma("gpsimd", lambda e, tc=tc: e.indirect_dma_start(
            out=isc_d, out_offset=bass.IndirectOffsetOnAxis(u32c[:, tc:tc + 1], 0),
            in_=combo[:, tc, :], in_offset=None,
            bounds_check=K - 1, oob_is_err=False))
    p.barrier()
    dma_in(p, iscsb, isc_d.rearrange("(p c) e -> p c e", p=128))
    p.barrier(("vector",))
    tu = p.emit("vector", lambda e: e.tensor_copy(
        idxu, iscsb[:, :, 0:1].rearrange("p c o -> p (c o)")), inc=True)
    p.w("gpsimd", tu)
    for c2 in range(KC):
        tg = p.dma("gpsimd", lambda e, c2=c2: e.indirect_dma_start(
            out=stile, out_offset=None, in_=ptok,
            in_offset=bass.IndirectOffsetOnAxis(idxu[:, c2:c2 + 1], 0)))
        p.w("sync", tg)
        dma_out(p, seltok[c2 * 128:(c2 + 1) * 128, :], stile)
        p.w("tensor", tg)
        for j in range(NCH):
            b = p.bank()
            tpe = p.emit("tensor", lambda e, b=b, j=j: e.transpose(
                p.banks[b][:, 0:128], stile[:, j * 128:(j + 1) * 128], ident),
                inc=True)
            p.w("vector", tpe)
            tev = p.emit("vector", lambda e, b=b, j=j: e.tensor_copy(
                sfmt[:, j, :], p.banks[b][:, 0:128]), inc=True)
            p.bank_done(b, tev)
        p.barrier(("sync",))
        dma_out(p, selfm[:, :, c2 * 128:(c2 + 1) * 128].rearrange("c p t -> p c t"),
                sfmt)
        tg = p.dma("gpsimd", lambda e, c2=c2: e.indirect_dma_start(
            out=cstile[:, 0:32], out_offset=None, in_=cos_d,
            in_offset=bass.IndirectOffsetOnAxis(idxu[:, c2:c2 + 1], 0)))
        tg2 = p.dma("gpsimd", lambda e, c2=c2: e.indirect_dma_start(
            out=cstile[:, 32:64], out_offset=None, in_=sin_d,
            in_offset=bass.IndirectOffsetOnAxis(idxu[:, c2:c2 + 1], 0)))
        p.w("sync", tg)
        p.w("sync", tg2)
        dma_out(p, cossel[c2 * 128:(c2 + 1) * 128, :], cstile[:, 0:32])
        tls = dma_out(p, sinsel[c2 * 128:(c2 + 1) * 128, :], cstile[:, 32:64])
        p.w("gpsimd", tls)
        p.barrier(("gpsimd",))
    p.barrier()
    p.pop_scope()


def emit_stage5(p, Tl=T):
    nc = p.nc
    p.push_scope()
    K = Tl // 4
    KC = K // 128

    D = lambda n, s, k, dt=F32: p.dram("s5_" + n, s, k, dt)
    proc2 = D("proc2", [NCH, 128, K], "ExternalInput")
    seltok = D("seltok", [K, C], "ExternalInput")
    isc_d = D("isc", [K, 8], "ExternalInput")
    f0tok = D("f0tok", [Tl, C], "ExternalInput")
    x0 = D("x0", [NCH, 128, Tl], "ExternalInput")
    a0 = D("a0", [NCH, 128, Tl], "ExternalInput")
    idc = D("identity", [128, 128], "ExternalInput")
    x1_d = D("x1", [NCH, 128, Tl], "ExternalOutput")
    f0w = D("f0w", [Tl, C], "Internal")

    A = lambda n, s, dt=F32: p.sbuf("s5_" + n, s, dt)
    ident = A("ident", [128, 128])
    ytile = A("ytile", [128, C])
    stile = A("stile", [128, C])
    ptile = A("ptile", [128, NCH, 128])
    iscsb = A("iscsb", [128, KC, 8])
    idxu = A("idxu", [128, KC], U32)
    xa = A("xa", [128, NCH, TT])
    xb = A("xb", [128, NCH, TT])
    xc = A("xc", [128, NCH, TT])

    dma_in(p, ident, idc)
    dma_in(p, iscsb, isc_d.rearrange("(p c) e -> p c e", p=128))
    p.barrier()
    tu = p.emit("vector", lambda e: e.tensor_copy(
        idxu, iscsb[:, :, 0:1].rearrange("p c o -> p (c o)")), inc=True)
    Q4 = Tl // 4
    for c2 in range(4):
        p.dma("sync", lambda e, c2=c2: e.dma_start(
            out=f0w[c2 * Q4:(c2 + 1) * Q4, :], in_=f0tok[c2 * Q4:(c2 + 1) * Q4, :]))
    p.barrier()
    for c2 in range(KC):
        dma_in(p, ptile, proc2[:, :, c2 * 128:(c2 + 1) * 128].rearrange(
            "c p t -> p c t"))
        tld = dma_in(p, stile, seltok[c2 * 128:(c2 + 1) * 128, :])
        p.barrier(("tensor", "vector"))
        for j in range(NCH):
            b = p.bank()
            tpe = p.emit("tensor", lambda e, b=b, j=j: e.transpose(
                p.banks[b][:, 0:128], ptile[:, j, :], ident), inc=True)
            p.w("vector", tpe)
            tev = p.emit("vector", lambda e, b=b, j=j: e.tensor_copy(
                ytile[:, j * 128:(j + 1) * 128], p.banks[b][:, 0:128]), inc=True)
            p.bank_done(b, tev)
        tev = p.t("qv")
        p.w("vector", tev)
        t = p.emit("vector", lambda e: e.tensor_tensor(ytile, ytile, stile,
                                                       ALU.subtract), inc=True)
        p.w("vector", t)
        t = p.emit("vector", lambda e, c2=c2: e.tensor_scalar(
            ytile, ytile, iscsb[:, c2, 1:2], None, ALU.mult), inc=True)
        p.w("vector", t)
        t = p.emit("vector", lambda e: e.tensor_tensor(ytile, ytile, stile, ALU.add),
                   inc=True)
        p.w("gpsimd", t)
        tsc = p.dma("gpsimd", lambda e, c2=c2: e.indirect_dma_start(
            out=f0w, out_offset=bass.IndirectOffsetOnAxis(idxu[:, c2:c2 + 1], 0),
            in_=ytile, in_offset=None))
        p.w("sync", tsc)
        p.w("vector", tsc)
        p.barrier(("gpsimd",))
    p.barrier()
    for it in range(Tl // TT):
        t0 = it * TT
        dma_in(p, xa, x0[:, :, t0:t0 + TT].rearrange("c p t -> p c t"))
        dma_in(p, xb, a0[:, :, t0:t0 + TT].rearrange("c p t -> p c t"))
        p.barrier()
        for st in range(TT // 128):
            tld = dma_in(p, ytile, f0w[t0 + st * 128:t0 + st * 128 + 128, :])
            p.w("tensor", tld)
            for j in range(NCH):
                b = p.bank()
                tpe = p.emit("tensor", lambda e, b=b, j=j: e.transpose(
                    p.banks[b][:, 0:128], ytile[:, j * 128:(j + 1) * 128], ident),
                    inc=True)
                p.w("vector", tpe)
                tev = p.emit("vector", lambda e, b=b, j=j, st=st: e.tensor_copy(
                    xc[:, j, st * 128:(st + 1) * 128], p.banks[b][:, 0:128]),
                    inc=True)
                p.bank_done(b, tev)
            p.w("sync", p.t("qt"))
        tev = p.t("qv")
        p.w("vector", tev)
        for kc in range(NCH):
            t = p.emit("vector", lambda e, kc=kc: e.tensor_tensor(
                xa[:, kc, :], xa[:, kc, :], xb[:, kc, :], ALU.add), inc=True)
            p.w("vector", t)
            t = p.emit("vector", lambda e, kc=kc: e.tensor_tensor(
                xa[:, kc, :], xa[:, kc, :], xc[:, kc, :], ALU.add), inc=True)
        p.w("sync", p.t("qv"))
        dma_out(p, x1_d[:, :, t0:t0 + TT].rearrange("c p t -> p c t"), xa)
        p.barrier()
    p.barrier()
    p.pop_scope()


SC = 16


def emit_stage6(p, Tl=T):
    nc = p.nc
    p.push_scope()
    ntt = Tl // TT
    ntt128 = Tl // 128

    D = lambda n, s, k: p.dram("s6_" + n, s, k)
    x1 = D("x1", [NCH, 128, Tl], "ExternalInput")
    Wr = D("Wr", [C, C], "ExternalInput")
    Wk = D("Wk", [C, C], "ExternalInput")
    Wv = D("Wv", [C, C], "ExternalInput")
    Ww = D("Ww", [C, C], "ExternalInput")
    Wg = D("Wg", [C, C], "ExternalInput")
    Wo = D("Wo", [C, C], "ExternalInput")
    vecs = D("vecs", [128, 8 * NCH], "ExternalInput")
    cst = D("cst", [128, 8], "ExternalInput")
    idc = D("identity", [128, 128], "ExternalInput")
    rmask_d = D("rmask", [128, Tl], "ExternalInput")
    bar_d = D("bar", [128, Tl], "ExternalInput")
    bmask_d = D("bmask", [128, 128], "ExternalInput")
    xn1 = D("xn1", [NCH, 128, Tl], "ExternalOutput")
    a1 = D("a1", [NCH, 128, Tl], "ExternalOutput")
    xl_d = D("xl_d", [NCH, 128, Tl], "Internal")
    r_d = D("r_d", [NCH, 128, Tl], "Internal")
    k_d = D("k_d", [NCH, 128, Tl], "Internal")
    v_d = D("v_d", [NCH, 128, Tl], "Internal")
    w_d = D("w_d", [NCH, 128, Tl], "Internal")
    g_d = D("g_d", [NCH, 128, Tl], "Internal")
    e_d = D("e_d", [NCH, 128, Tl], "Internal")
    C_d = D("C_d", [NCH, 128, Tl], "Internal")
    rt_d = D("rt_d", [NCH, 128, Tl], "Internal")
    kh_d = D("kh_d", [NCH, 128, Tl], "Internal")
    dE_d = D("dE_d", [NCH, 128, Tl // SC], "Internal")
    khtok = D("khtok", [Tl, C], "Internal")
    vtok = D("vtok", [Tl, C], "Internal")
    gy_d = D("gy_d", [NCH, 128, Tl], "Internal")

    A = lambda n, s, dt=F32: p.sbuf("s6_" + n, s, dt)
    ident = A("ident", [128, 128])
    ones_col = A("ones_col", [128, 1])
    ones_row = A("ones_row", [1, 128])
    sb_vec = A("sb_vec", [128, 8 * NCH])
    sb_cst = A("sb_cst", [128, 8])
    sigm = A("sigm", [128, 4 * NCH])
    wT = A("wT", [128, NCH, C])
    wnat = A("wnat", [128, NCH, C])
    xa = A("xa", [128, NCH, TT])
    xb = A("xb", [128, NCH, TT])
    xc = A("xc", [128, NCH, TT])
    xm = A("xm", [128, NCH, TT])
    xsq = A("xsq", [128, NCH, TT])
    osb = A("osb", [128, NCH, TT])
    a_bc = A("a_bc", [128, TT])
    b_bc = A("b_bc", [128, TT])
    rows = A("rows", [1, 2048])

    dma_in(p, ident, idc)
    tv = dma_in(p, sb_vec, vecs)
    dma_in(p, sb_cst, cst)
    mixs = [sb_vec[:, i * NCH:(i + 1) * NCH] for i in range(4)]
    tdec = sb_vec[:, 4 * NCH:5 * NCH]
    lnw = sb_vec[:, 5 * NCH:6 * NCH]
    lnb = sb_vec[:, 6 * NCH:7 * NCH]
    ucol = sb_vec[:, 7 * NCH:8 * NCH]
    eps_ln = sb_cst[:, 0:1]
    p.w("scalar", tv)
    p.emit("scalar", lambda e: e.activation(sigm, sb_vec[:, 0:4 * NCH], AF.Sigmoid),
           inc=True)
    p.emit("vector", lambda e: e.memset(ones_col, 1.0), inc=True)
    p.emit("vector", lambda e: e.memset(ones_row, 1.0), inc=True)
    p.barrier()

    for it in range(ntt):
        t0 = it * TT
        dma_in(p, xa, x1[:, :, t0:t0 + TT].rearrange("c p t -> p c t"))
        p.barrier()
        tvv = ln_tile(p, xa, TT, xsq, a_bc, b_bc, rows, ones_col, ones_row, eps_ln,
                      "ln", xb)
        p.w("sync", tvv)
        dma_out(p, xn1[:, :, t0:t0 + TT].rearrange("c p t -> p c t"), xb)
        p.barrier()
        tvv = ln_tile(p, xb, TT, xsq, a_bc, b_bc, rows, ones_col, ones_row, eps_ln,
                      "ln", xc, w_col=lnw, b_col=lnb)
        p.w("sync", tvv)
        dma_out(p, xl_d[:, :, t0:t0 + TT].rearrange("c p t -> p c t"), xc)
        p.barrier()

    inner6 = ExitStack()
    Ai6 = lambda n, sh: inner6.enter_context(nc.sbuf_tensor("s6_" + n, sh, F32)).ap()
    xcB = Ai6("xcB", [128, NCH, TT])
    xmB = Ai6("xmB", [128, NCH, TT])
    osbB = wnat[:, :, 0:TT]
    for (W_dram, mi, dst, sig) in ((Wr, 2, r_d, True), (Wg, 2, g_d, True),
                                   (Wk, 0, k_d, False), (Wv, 1, v_d, False),
                                   (Ww, 3, w_d, False), (Wo, None, None, False)):
        if dst is None:
            break
        mix = sigm[:, mi * NCH:(mi + 1) * NCH]
        load_weight_T(p, W_dram, wT, wnat, ident)
        p.barrier()
        mix_proj_pass(p, xl_d, mix, wT, dst, sig, [xc, xcB], [xm, xmB], xsq,
                      [osb, osbB], Tl)

    inner6.close()
    wc_t = wT.rearrange("p a b -> p (a b)")[:, 0:Tl]
    cc_t = wT.rearrange("p a b -> p (a b)")[:, Tl:2 * Tl] if Tl <= 4096 else None
    rm_t = wnat.rearrange("p a b -> p (a b)")[:, 0:Tl]
    sc_t = wnat.rearrange("p a b -> p (a b)")[:, Tl:2 * Tl]
    for kc in range(NCH):
        dma_in(p, wc_t, w_d[kc])
        dma_in(p, rm_t, rmask_d)
        p.barrier()
        t = p.emit("vector", lambda e, kc=kc: e.tensor_scalar(
            wc_t, wc_t, tdec[:, kc:kc + 1], None, ALU.add), inc=True)
        p.w("vector", t)
        t = p.emit("vector", lambda e: e.tensor_scalar(
            wc_t, wc_t, -10.0, None, ALU.max), inc=True)
        p.w("vector", t)
        t = p.emit("vector", lambda e: e.tensor_scalar(
            wc_t, wc_t, 10.0, None, ALU.min), inc=True)
        p.w("scalar", t)
        t = p.emit("scalar", lambda e: e.activation(sc_t, wc_t, AF.Exp), inc=True)
        p.w("sync", t)
        dma_out(p, e_d[kc], sc_t)
        p.w("vector", t)
        t = p.emit("vector", lambda e: e.tensor_tensor_scan(
            cc_t, rm_t, sc_t, 0.0, ALU.mult, ALU.add), inc=True)
        p.w("sync", t)
        dma_out(p, C_d[kc], cc_t)
        p.barrier()
    for kc in range(NCH):
        dma_in(p, wc_t, C_d[kc])
        dma_in(p, rm_t, e_d[kc])
        dma_in(p, sc_t, r_d[kc])
        p.barrier()
        t = p.emit("vector", lambda e: e.tensor_tensor(rm_t, wc_t, rm_t,
                                                       ALU.subtract), inc=True)
        p.w("scalar", t)
        t = p.emit("scalar", lambda e: e.activation(rm_t, rm_t, AF.Exp, scale=-1.0),
                   inc=True)
        p.w("vector", t)
        tmul = p.emit("vector", lambda e: e.tensor_tensor(sc_t, sc_t, rm_t,
                                                          ALU.mult), inc=True)
        p.w("sync", tmul)
        tso = dma_out(p, rt_d[kc], sc_t)
        cv = wc_t.rearrange("p (s j) -> p s j", j=SC)
        ends = cv[:, :, SC - 1:SC]
        p.w("scalar", tmul)
        t = p.emit("scalar", lambda e: e.activation(
            rm_t[:, 0:Tl // SC], ends.rearrange("p s o -> p (s o)"), AF.Exp,
            scale=-1.0), inc=True)
        p.w("sync", t)
        dma_out(p, dE_d[kc], rm_t[:, 0:Tl // SC])
        p.w("vector", t)
        t = p.emit("vector", lambda e: e.tensor_tensor(
            cv, cv, ends.to_broadcast([128, Tl // SC, SC]), ALU.subtract), inc=True)
        p.w("scalar", t)
        t = p.emit("scalar", lambda e: e.activation(wc_t, wc_t, AF.Exp), inc=True)
        p.w("sync", tso)
        tld = dma_in(p, sc_t, k_d[kc])
        p.w("vector", t)
        p.w("vector", tld)
        t = p.emit("vector", lambda e: e.tensor_tensor(sc_t, sc_t, wc_t, ALU.mult),
                   inc=True)
        p.w("sync", t)
        dma_out(p, kh_d[kc], sc_t)
        p.barrier()
    xm_flat = xm.rearrange("p a b -> p (a b)")
    otile = xm_flat[:, 0:C]
    o2 = xm_flat[:, C:2 * C]
    for st in range(ntt128):
        dma_in(p, xa[:, :, 0:128], kh_d[:, :, st * 128:(st + 1) * 128].rearrange(
            "c p t -> p c t"))
        dma_in(p, xb[:, :, 0:128], v_d[:, :, st * 128:(st + 1) * 128].rearrange(
            "c p t -> p c t"))
        p.barrier()
        for (src, dst_t, ot) in ((xa, khtok, otile), (xb, vtok, o2)):
            for j in range(NCH):
                b = p.bank()
                tpe = p.emit("tensor", lambda e, b=b, j=j, src=src: e.transpose(
                    p.banks[b][:, 0:128], src[:, j, 0:128], ident), inc=True)
                p.w("vector", tpe)
                tev = p.emit("vector", lambda e, b=b, j=j, ot=ot: e.tensor_copy(
                    ot[:, j * 128:(j + 1) * 128], p.banks[b][:, 0:128]), inc=True)
                p.bank_done(b, tev)
            p.barrier(("sync",))
            dma_out(p, dst_t[st * 128:(st + 1) * 128, :], ot)
        p.barrier()

    S_t = [A(f"S{c}", [128, 128]) for c in range(NCH)]
    khsg = wT[0:16, :, :]
    vsg = wnat[0:16, :, :]
    rtt = xa[:, :, 0:128]
    rft = xa[:, :, 128:256]
    kft = xa[:, :, 256:384]
    eft = xa[:, :, 384:512]
    vft = xb[:, :, 0:128]
    yt = xb[:, :, 128:256]
    Dt = xb[:, :, 256:384]
    Et = xb[:, :, 384:512]
    Pt = xc[:, :, 0:128]
    gt_ = xc[:, :, 128:256]
    bart = A("bart", [128, 128])
    rmt4 = A("rmt4", [128, 128])
    bmask = A("bmask_sb", [128, 128])
    dEt = A("dEt", [128, NCH, 8])

    dma_in(p, bmask, bmask_d)
    for c in range(NCH):
        p.emit("vector", lambda e, c=c: e.memset(S_t[c], 0.0), inc=True)
    p.barrier()
    for it in range(ntt128):
        t0 = it * 128
        dma_in(p, rtt, rt_d[:, :, t0:t0 + 128].rearrange("c p t -> p c t"))
        dma_in(p, rft, r_d[:, :, t0:t0 + 128].rearrange("c p t -> p c t"))
        dma_in(p, kft, k_d[:, :, t0:t0 + 128].rearrange("c p t -> p c t"))
        dma_in(p, eft, e_d[:, :, t0:t0 + 128].rearrange("c p t -> p c t"))
        dma_in(p, vft, v_d[:, :, t0:t0 + 128].rearrange("c p t -> p c t"))
        dma_in(p, bart, bar_d[:, t0:t0 + 128])
        dma_in(p, rmt4, rmask_d[:, t0:t0 + 128])
        dma_in(p, khsg, khtok[t0:t0 + 128, :].rearrange("(s o) f -> o s f", o=SC))
        dma_in(p, vsg, vtok[t0:t0 + 128, :].rearrange("(s o) f -> o s f", o=SC))
        dma_in(p, dEt, dE_d[:, :, it * 8:(it + 1) * 8].rearrange("c p s -> p c s"))
        p.barrier()
        t = p.emit("vector", lambda e: e.tensor_tensor(
            eft, eft, bart.rearrange("p (o t) -> p o t", o=1).to_broadcast(
                [128, NCH, 128]), ALU.add), inc=True)
        tz = p.emit("vector", lambda e: e.memset(yt, 0.0), inc=True)
        tz2 = p.emit("vector", lambda e: e.memset(Dt, 0.0), inc=True)
        p.barrier(("vector", "scalar", "tensor"))
        for dl in range(SC):
            p.w("vector", p.t("qt"))
            p.w("scalar", p.t("qv"))
            if dl == 0:
                t = p.emit("vector", lambda e: e.tensor_tensor(Pt, rft, kft,
                                                               ALU.mult), inc=True)
                p.w("vector", t)
                t = p.emit("vector", lambda e: e.tensor_tensor(
                    Pt, Pt, ucol.rearrange("p (c o) -> p c o", o=1).to_broadcast(
                        [128, NCH, 128]), ALU.mult), inc=True)
                nn = 128
            else:
                if dl > 1:
                    t = p.emit("vector", lambda e, dl=dl: e.tensor_tensor(
                        Dt[:, :, dl:], Dt[:, :, dl:], eft[:, :, 1:128 - dl + 2 - 1],
                        ALU.add), inc=True)
                    p.w("vector", t)
                    p.w("scalar", t)
                t = p.emit("scalar", lambda e, dl=dl: e.activation(
                    Et[:, :, dl:], Dt[:, :, dl:], AF.Exp, scale=-1.0), inc=True)
                p.w("vector", t)
                t = p.emit("vector", lambda e, dl=dl: e.tensor_tensor(
                    Et[:, :, dl:], Et[:, :, dl:], rft[:, :, dl:], ALU.mult), inc=True)
                p.w("vector", t)
                t = p.emit("vector", lambda e, dl=dl: e.tensor_tensor(
                    Pt[:, :, dl:], Et[:, :, dl:], kft[:, :, 0:128 - dl], ALU.mult),
                    inc=True)
                if dl == 1:
                    p.w("vector", t)
                    t = p.emit("vector", lambda e: e.tensor_tensor(
                        Pt[:, :, 1:], Pt[:, :, 1:],
                        rmt4.rearrange("p (o t) -> p o t", o=1).to_broadcast(
                            [128, NCH, 128])[:, :, 1:], ALU.mult), inc=True)
                nn = 128 - dl
            p.w("vector", t)
            p.w("tensor", t)
            for hh in range(2):
                b = p.bank()
                tpe = p.emit("tensor", lambda e, b=b, hh=hh, dl=dl, nn=nn: e.matmul(
                    p.banks[b][:, 0:4 * nn],
                    bmask, Pt[:, 4 * hh:4 * (hh + 1), 128 - nn:],
                    start=True, stop=True), inc=True)
                p.w("vector", tpe)
                t = p.emit("vector", lambda e, b=b, hh=hh, dl=dl, nn=nn:
                           e.tensor_tensor(
                               Et[:, 4 * hh:4 * (hh + 1), 128 - nn:],
                               p.banks[b][:, 0:4 * nn].rearrange(
                                   "p (c t) -> p c t", c=4),
                               vft[:, 4 * hh:4 * (hh + 1), 0:nn], ALU.mult),
                           inc=True)
                p.bank_done(b, t)
                p.w("vector", t)
                t = p.emit("vector", lambda e, hh=hh, dl=dl, nn=nn: e.tensor_tensor(
                    yt[:, 4 * hh:4 * (hh + 1), 128 - nn:],
                    yt[:, 4 * hh:4 * (hh + 1), 128 - nn:],
                    Et[:, 4 * hh:4 * (hh + 1), 128 - nn:], ALU.add), inc=True)
                p.w("vector", t)
        for sc in range(8):
            c0 = sc * SC
            p.w("tensor", p.t("qv"))
            b = p.bank()
            for c in range(NCH):
                tpe = p.emit("tensor", lambda e, b=b, c=c, c0=c0: e.matmul(
                    p.banks[b][:, c * SC:(c + 1) * SC], S_t[c],
                    rtt[:, c, c0:c0 + SC], start=True, stop=True),
                    inc=(c == NCH - 1))
            p.w("vector", tpe)
            t = p.emit("vector", lambda e, b=b, c0=c0: e.tensor_tensor(
                yt[:, :, c0:c0 + SC],
                yt[:, :, c0:c0 + SC],
                p.banks[b][:, 0:128].rearrange("p (c t) -> p c t", c=NCH), ALU.add),
                inc=True)
            p.bank_done(b, t)
            for c in range(NCH):
                b = p.bank()
                tpe = p.emit("tensor", lambda e, b=b, c=c, sc=sc: e.matmul(
                    p.banks[b][:, 0:128], khsg[:, sc, c * 128:(c + 1) * 128],
                    vsg[:, sc, c * 128:(c + 1) * 128], start=True, stop=True),
                    inc=True)
                p.w("vector", tpe)
                t = p.emit("vector", lambda e, b=b: e.tensor_tensor(
                    Pt[:, 0, :], p.banks[b][:, 0:128], bmask, ALU.mult), inc=True)
                p.bank_done(b, t)
                p.w("vector", t)
                t = p.emit("vector", lambda e, c=c, sc=sc: e.scalar_tensor_tensor(
                    S_t[c], S_t[c], dEt[:, c, sc:sc + 1], Pt[:, 0, :],
                    ALU.mult, ALU.add), inc=True)
                p.w("vector", t)
        tld = dma_in(p, gt_, g_d[:, :, t0:t0 + 128].rearrange("c p t -> p c t"))
        p.w("vector", tld)
        t = p.emit("vector", lambda e: e.tensor_tensor(yt, yt, gt_, ALU.mult),
                   inc=True)
        p.w("sync", t)
        dma_out(p, gy_d[:, :, t0:t0 + 128].rearrange("c p t -> p c t"), yt)
        p.barrier()

    load_weight_T(p, Wo, wT, wnat, ident)
    p.barrier()
    for it in range(ntt):
        t0 = it * TT
        dma_in(p, xa, gy_d[:, :, t0:t0 + TT].rearrange("c p t -> p c t"))
        p.barrier()

        def evac(m, b, pet):
            p.w("vector", pet)
            return p.emit("vector", lambda e, m=m, b=b: e.tensor_copy(
                osb[:, m, :], p.banks[b][:, 0:TT]), inc=True)
        proj_fm(p, evac, xa, wT, TT)
        p.barrier(("sync",))
        dma_out(p, a1[:, :, t0:t0 + TT].rearrange("c p t -> p c t"), osb)
        p.barrier()
    p.barrier()
    p.pop_scope()


def emit_stage7(p, Tl=T, e_lo=0, e_hi=8, first=True):
    nc = p.nc
    p.push_scope()
    pfx = f"s7{e_lo}_"
    ntt = Tl // TT
    FB = 512
    NFB = 4096 // FB

    D = lambda n, s, k: p.dram(pfx + n, s, k)
    xn1 = D("xn1", [NCH, 128, Tl], "ExternalInput")
    if first:
        x1_ = D("x1", [NCH, 128, Tl], "ExternalInput")
        a1_ = D("a1", [NCH, 128, Tl], "ExternalInput")
    else:
        base = D("base", [NCH, 128, Tl], "ExternalInput")
    Wg = D("Wg", [C, 8], "ExternalInput")
    bg = D("bg", [1, 8], "ExternalInput")
    W1 = D("W1", [8, C, 4 * C], "ExternalInput")
    b1 = D("b1", [8, 128, 32], "ExternalInput")
    W2 = D("W2", [8, 4 * C, C], "ExternalInput")
    b2 = D("b2", [128, 64], "ExternalInput")
    emask_d = D("emask", [128, 64], "ExternalInput")
    idc = D("identity", [128, 128], "ExternalInput")
    out_d = D("out", [NCH, 128, Tl], "ExternalOutput")
    ct_d = D("ct", [8, Tl], "Internal")

    A = lambda n, s, dt=F32: p.sbuf(pfx + n, s, dt)
    ident = A("ident", [128, 128])
    ones_row = A("ones_row", [1, 128])
    wgsb = A("wgsb", [128, NCH, 8])
    bgsb = A("bgsb", [1, 8])
    emask = A("emask_sb", [128, 64])
    b2sb = A("b2sb", [128, 64])
    xa2 = [A(f"xa{i}", [128, NCH, TT]) for i in range(2)]
    ob2 = [A(f"osb{i}", [128, NCH, TT]) for i in range(2)]
    xc2 = [None, None]
    gsb = A("gsb", [128, 4, TT])
    csb = [A(f"csb{i}", [128, TT]) for i in range(2)]
    ptile = A("ptile", [128, 8])
    prank = A("prank", [128, 8])
    pw1 = [A(f"pw1_{i}", [128, NCH, FB]) for i in range(2)]
    pw2 = [A(f"pw2_{i}", [128, FB // 128, C]) for i in range(2)]
    b1sb = [A(f"b1sb_{i}", [128, 32]) for i in range(2)]
    lsum = A("lsum", [128, 4])
    ct_sb = A("ct_sb", [8, 128])
    if first:
        xc2 = [A(f"xc7{i}", [128, NCH, TT]) for i in range(2)]
    crow = [A(f"crow{i}", [1, TT]) for i in range(2)]

    dma_in(p, ident, idc)
    dma_in(p, wgsb, Wg.rearrange("(a p) b -> p a b", p=128))
    dma_in(p, bgsb, bg)
    dma_in(p, emask, emask_d)
    dma_in(p, b2sb, b2)
    p.emit("vector", lambda e: e.memset(ones_row, 1.0), inc=True)
    p.barrier()

    if first:
        for st in range(Tl // 128):
            t0 = st * 128
            dma_in(p, xa2[0][:, :, 0:128], xn1[:, :, t0:t0 + 128].rearrange(
                "c p t -> p c t"))
            p.barrier(("tensor", "vector", "scalar"))
            b = p.bank()
            for kc in range(NCH):
                p.emit("tensor", lambda e, b=b, kc=kc: e.matmul(
                    p.banks[b][:, 0:8], xa2[0][:, kc, 0:128], wgsb[:, kc, :],
                    start=(kc == 0), stop=False), inc=False)
            tpe = p.emit("tensor", lambda e, b=b: e.matmul(
                p.banks[b][:, 0:8], ones_row, bgsb, start=False, stop=True),
                inc=True)
            p.w("vector", tpe)
            t = p.emit("vector", lambda e, b=b: e.tensor_copy(ptile,
                                                              p.banks[b][:, 0:8]),
                       inc=True)
            p.bank_done(b, t)
            p.w("vector", t)
            t = p.emit("vector", lambda e: e.reduce_max(lsum[:, 0:1], ptile,
                                                        axis=AX.X, negate=True),
                       inc=True)
            p.w("scalar", t)
            t = p.emit("scalar", lambda e: e.activation(
                ptile, ptile, AF.Exp, bias=lsum[:, 0:1], accum_out=lsum[:, 1:2]),
                inc=True)
            p.w("vector", t)
            t = p.emit("vector", lambda e: e.reciprocal(lsum[:, 2:3], lsum[:, 1:2]),
                       inc=True)
            p.w("vector", t)
            t = p.emit("vector", lambda e: e.tensor_scalar(
                ptile, ptile, lsum[:, 2:3], None, ALU.mult), inc=True)
            p.w("vector", t)
            for ee in range(8):
                t1 = p.emit("vector", lambda e, ee=ee: e.tensor_scalar(
                    prank, ptile, ptile[:, ee:ee + 1], None, ALU.is_gt), inc=True)
                t2 = p.emit("scalar", lambda e, ee=ee: e.activation(
                    lsum[:, 3:4], ptile[:, ee:ee + 1], AF.Copy), inc=True) \
                    if False else None
                t2 = p.emit("vector", lambda e, ee=ee: e.tensor_scalar(
                    gsb[:, 0, 0:8], ptile, ptile[:, ee:ee + 1], None, ALU.is_equal),
                    inc=True)
                p.w("vector", t2)
                t3 = p.emit("vector", lambda e, ee=ee: e.tensor_tensor(
                    gsb[:, 0, 0:8], gsb[:, 0, 0:8], emask[:, ee * 8:(ee + 1) * 8],
                    ALU.mult), inc=True)
                p.w("vector", t3)
                t4 = p.emit("vector", lambda e: e.tensor_tensor(
                    prank, prank, gsb[:, 0, 0:8], ALU.add), inc=True)
                p.w("vector", t4)
                t5 = p.emit("vector", lambda e, ee=ee: e.reduce_sum(
                    gsb[:, 1, ee:ee + 1], prank, axis=AX.X), inc=True)
                p.w("vector", t5)
            t = p.emit("vector", lambda e: e.tensor_scalar(
                gsb[:, 1, 0:8], gsb[:, 1, 0:8], 2.0, None, ALU.is_lt), inc=True)
            p.w("vector", t)
            t = p.emit("vector", lambda e: e.tensor_tensor(
                ptile, ptile, gsb[:, 1, 0:8], ALU.mult), inc=True)
            p.w("vector", t)
            t = p.emit("vector", lambda e: e.reduce_sum(lsum[:, 0:1], ptile,
                                                        axis=AX.X), inc=True)
            p.w("vector", t)
            t = p.emit("vector", lambda e: e.tensor_scalar(
                lsum[:, 0:1], lsum[:, 0:1], 1e-8, None, ALU.add), inc=True)
            p.w("vector", t)
            t = p.emit("vector", lambda e: e.reciprocal(lsum[:, 1:2], lsum[:, 0:1]),
                       inc=True)
            p.w("vector", t)
            t = p.emit("vector", lambda e: e.tensor_scalar(
                ptile, ptile, lsum[:, 1:2], None, ALU.mult), inc=True)
            p.w("vector", t)
            p.w("tensor", t)
            b = p.bank()
            tpe = p.emit("tensor", lambda e, b=b: e.transpose(
                p.banks[b][0:8, 0:128], ptile, ident), inc=True)
            p.w("vector", tpe)
            t = p.emit("vector", lambda e, b=b: e.tensor_copy(
                ct_sb, p.banks[b][0:8, 0:128]), inc=True)
            p.bank_done(b, t)
            p.w("sync", t)
            tst = dma_out(p, ct_d[:, t0:t0 + 128], ct_sb)
            p.w("vector", tst)
            p.w("tensor", tst)
    else:
        pass

    p.barrier()
    ct_in = D("ct_in", [8, Tl], "ExternalInput") if not first else ct_d
    ct_out = D("ct_out", [8, Tl], "ExternalOutput")
    tile_use = {}

    def ld_tile(it, ti):
        t0 = it * TT
        p.w("sync", tile_use.get(ti))
        t1 = dma_in(p, xa2[ti], xn1[:, :, t0:t0 + TT].rearrange("c p t -> p c t"))
        if first:
            t2 = dma_in(p, ob2[ti], x1_[:, :, t0:t0 + TT].rearrange("c p t -> p c t"))
            t3 = dma_in(p, xc2[ti], a1_[:, :, t0:t0 + TT].rearrange("c p t -> p c t"))
            return (t1, t2, t3)
        t2 = dma_in(p, ob2[ti], base[:, :, t0:t0 + TT].rearrange("c p t -> p c t"))
        return (t1, t2)

    tile_ld = {0: ld_tile(0, 0)}
    for it in range(ntt):
        t0 = it * TT
        ti = it % 2
        xa = xa2[ti]
        osb = ob2[ti]
        if it + 1 < ntt:
            tile_ld[1 - ti] = ld_tile(it + 1, 1 - ti)
        for tt in tile_ld[ti]:
            p.w("tensor", tt)
            p.w("vector", tt)
        if first:
            for kc in range(NCH):
                p.w("vector", p.t("qv"))
                p.emit("vector", lambda e, kc=kc, osb=osb, ti=ti: e.tensor_tensor(
                    osb[:, kc, :], osb[:, kc, :], xc2[ti][:, kc, :], ALU.add),
                    inc=True)
        if not hasattr(p, "wld"):
            p.wld = {}
            p.wuse = {}
            p.crow_use = {}

        def ld_fb(job, wi):
            ee2, fb = job
            f0 = fb * FB
            p.w("sync", p.wuse.get(wi))
            t1 = dma_in(p, pw1[wi], W1[ee2][:, f0:f0 + FB].rearrange(
                "(a p) b -> p a b", p=128))
            t2 = dma_in(p, pw2[wi], W2[ee2][f0:f0 + FB, :].rearrange(
                "(a p) b -> p a b", p=128))
            t3 = dma_in(p, b1sb[wi][:, 0:FB // 128],
                        b1[ee2][:, fb * (FB // 128):(fb + 1) * (FB // 128)])
            p.wld[wi] = (t1, t2, t3)

        def ld_crow(ee2, ci):
            p.w("sync", p.crow_use.get(ci))
            return dma_in(p, crow[ci], ct_in[ee2:ee2 + 1, t0:t0 + TT])

        jobs = [(ee2, fb) for ee2 in range(e_lo, e_hi) for fb in range(NFB)]
        ld_fb(jobs[0], 0)
        tcr = {jobs[0][0]: ld_crow(jobs[0][0], jobs[0][0] % 2)}
        tcsb = {}
        for ji, (ee, fb) in enumerate(jobs):
            wi = ji % 2
            if ji + 1 < len(jobs):
                ld_fb(jobs[ji + 1], 1 - wi)
                ne = jobs[ji + 1][0]
                if ne != ee and ne not in tcr:
                    tcr[ne] = ld_crow(ne, ne % 2)
            if fb == 0:
                b = p.bank()
                p.w("tensor", tcr[ee])
                tpe = p.emit("tensor", lambda e, b=b, ee=ee: e.matmul(
                    p.banks[b][:, 0:TT], ones_row, crow[ee % 2],
                    start=True, stop=True), inc=True)
                p.w("vector", tpe)
                p.w("vector", p.t("qv"))
                tc = p.emit("vector", lambda e, b=b, ee=ee: e.tensor_copy(
                    csb[ee % 2], p.banks[b][:, 0:TT]), inc=True)
                p.bank_done(b, tc)
                tcsb[ee] = tc
                p.crow_use[ee % 2] = p.t("qt")
            for tt in p.wld[wi]:
                p.w("tensor", tt)
                p.w("scalar", tt)
            for m in range(FB // 128):
                b = p.bank()
                for kc in range(NCH):
                    tpe = p.emit("tensor", lambda e, b=b, m=m, kc=kc, wi=wi, xa=xa:
                                 e.matmul(
                        p.banks[b][:, 0:TT], pw1[wi][:, kc, m * 128:(m + 1) * 128],
                        xa[:, kc, 0:TT], start=(kc == 0), stop=(kc == NCH - 1)),
                        inc=(kc == NCH - 1))
                p.w("scalar", tpe)
                tev = p.emit("scalar", lambda e, b=b, m=m, wi=wi: e.activation(
                    gsb[:, m, :], p.banks[b][:, 0:TT], AF.Gelu_apprx_tanh,
                    bias=b1sb[wi][:, m:m + 1]), inc=True)
                p.bank_done(b, tev)
                p.w("vector", tev)
                p.w("vector", tcsb[ee])
                t = p.emit("vector", lambda e, m=m, ee=ee: e.tensor_tensor(
                    gsb[:, m, :], gsb[:, m, :], csb[ee % 2], ALU.mult), inc=True)
            p.w("tensor", p.t("qv"))
            for m in range(NCH):
                b = p.bank()
                for kf in range(FB // 128):
                    tpe = p.emit("tensor", lambda e, b=b, m=m, kf=kf, wi=wi:
                                 e.matmul(
                        p.banks[b][:, 0:TT], pw2[wi][:, kf, m * 128:(m + 1) * 128],
                        gsb[:, kf, 0:TT], start=(kf == 0),
                        stop=(kf == FB // 128 - 1)), inc=(kf == FB // 128 - 1))
                p.w("vector", tpe)
                p.w("vector", p.t("qv"))
                tev = p.emit("vector", lambda e, b=b, m=m, osb=osb: e.tensor_tensor(
                    osb[:, m, :], p.banks[b][:, 0:TT], osb[:, m, :], ALU.add),
                    inc=True)
                p.bank_done(b, tev)
            p.wuse[wi] = p.t("qt")
            if fb == NFB - 1:
                for m in range(NCH):
                    p.w("vector", p.t("qv"))
                    p.emit("vector", lambda e, m=m, ee=ee, osb=osb:
                           e.scalar_tensor_tensor(
                        osb[:, m, :], csb[ee % 2],
                        b2sb[:, ee * 8 + m:ee * 8 + m + 1],
                        osb[:, m, :], ALU.mult, ALU.add), inc=True)
                p.crow_use[ee % 2] = p.t("qv")
        del tcr
        p.w("sync", p.t("qv"))
        tile_use[ti] = dma_out(
            p, out_d[:, :, t0:t0 + TT].rearrange("c p t -> p c t"), osb)
    p.barrier()
    if first:
        for st in range(Tl // 512):
            tld = dma_in(p, csb[0][0:8, :], ct_d[:, st * 512:(st + 1) * 512])
            p.w("sync", tld)
            tst = dma_out(p, ct_out[:, st * 512:(st + 1) * 512], csb[0][0:8, :])
            p.w("sync", tst)
    else:
        for st in range(Tl // 512):
            tld = dma_in(p, csb[0][0:8, :], ct_in[:, st * 512:(st + 1) * 512])
            p.w("sync", tld)
            tst = dma_out(p, ct_out[:, st * 512:(st + 1) * 512], csb[0][0:8, :])
            p.w("sync", tst)
    p.barrier()
    p.pop_scope()


def fm(x):
    return np.ascontiguousarray(np.asarray(x).T.reshape(NCH, 128, -1))


def fm_inv(x):
    return np.ascontiguousarray(np.asarray(x).reshape(C, -1).T)


def vec_col(v):
    return np.ascontiguousarray(np.asarray(v).reshape(NCH, 128).T)


def stage1_inputs(x_row_fm, w):
    vecs = np.concatenate([
        vec_col(w["L0_attn_time_mix_k"]), vec_col(w["L0_attn_time_mix_v"]),
        vec_col(w["L0_attn_time_mix_r"]), vec_col(w["L0_attn_time_decay"]),
        vec_col(w["L0_attn_time_first"]), vec_col(w["L0_attn_ln_w"]),
        vec_col(w["L0_attn_ln_b"]), np.zeros((128, NCH), np.float32)], axis=1)
    consts = np.zeros((128, 8), np.float32)
    consts[:, 0] = LN_EPS
    consts[:, 1] = RMS_EPS
    return {
        "x0": x_row_fm,
        "Wk": np.ascontiguousarray(w["L0_attn_Wk"]),
        "Wv": np.ascontiguousarray(w["L0_attn_Wv"]),
        "Wr": np.ascontiguousarray(w["L0_attn_Wr"]),
        "Wo": np.ascontiguousarray(w["L0_attn_Wo"]),
        "vecs": np.ascontiguousarray(vecs.astype(np.float32)), "consts": consts,
        "identity": np.eye(128, dtype=np.float32),
    }


def _mk(pfx, emit, *a, **kw):
    es = ExitStack()
    nc = bass.Bass("TRN2", target_bir_lowering=False)
    p = Prog(nc, es, pfx=pfx)
    emit(p, *a, **kw)
    p.finalize()
    es.close()
    return nc


def build_stage1(Tl=T):
    return _mk("s1_", emit_stage1, Tl)


def build_dec_qkv(pfx, S):
    return _mk(pfx, emit_dec_qkv, pfx, S)


def build_dec_attn(pfx, S):
    return _mk(pfx, emit_dec_attn, pfx, S)


def build_dec_tail(pfx, S):
    return _mk(pfx, emit_dec_tail, pfx, S)


def build_stage3(Tl=T):
    return _mk("s3_", emit_stage3, Tl)


def build_stage4(Tl=T):
    return _mk("s4_", emit_stage4, Tl)


def build_stage5(Tl=T):
    return _mk("s5_", emit_stage5, Tl)


def build_stage6(Tl=T):
    return _mk("s6_", emit_stage6, Tl)


def build_stage7(Tl=T, e_lo=0, e_hi=8, first=True):
    return _mk(f"s7{e_lo}_", emit_stage7, Tl, e_lo, e_hi, first)


def fused_aliases(Tl):
    K = Tl // 4
    al = {}
    for st in ("dA_", "dB_", "dC_", "dD_", "dE_", "dF_", "s3_", "s4_", "s5_",
               "s6_", "s70_", "s74_"):
        al[st + "identity"] = "identity"
        al[st + "cst"] = "G_cst"
    al["consts"] = "G_cst8" if False else "consts"
    al.update({
        "s1_identity": "identity", "s1_consts": "G_cst",
        "dA_hs": "s1_xn0", "dA_cos": "G_cos", "dA_sin": "G_sin",
        "dB_QT": "dA_QT", "dB_KT": "dA_KT", "dB_Vtok": "dA_Vtok",
        "dB_maskn": "G_maskn",
        "dC_hs": "s1_xn0", "dC_Otok": "dB_Otok",
        "s3_xn0": "s1_xn0", "s3_proc": "dC_proc",
        "s4_dst": "s3_dst", "s4_dch": "s3_dch", "s4_ptok": "s3_ptok",
        "s4_cos": "G_cos", "s4_sin": "G_sin",
        "dD_hs": "s4_selfm", "dD_cos": "s4_cossel", "dD_sin": "s4_sinsel",
        "dE_QT": "dD_QT", "dE_KT": "dD_KT", "dE_Vtok": "dD_Vtok",
        "dE_maskn": "G_maskn",
        "dF_hs": "s4_selfm", "dF_Otok": "dE_Otok",
        "s5_proc2": "dF_proc", "s5_seltok": "s4_seltok", "s5_isc": "s4_isc",
        "s5_f0tok": "s3_f0tok", "s5_x0": "s1_x0", "s5_a0": "s1_a0",
        "s6_x1": "s5_x1",
        "s70_xn1": "s6_xn1", "s70_x1": "s5_x1", "s70_a1": "s6_a1",
        "s74_xn1": "s6_xn1", "s74_base": "s70_out", "s74_ct_in": "s70_ct_out",
    })
    for nm in ("Wg", "bg", "W1", "b1", "W2", "b2", "emask"):
        al["s74_" + nm] = "s70_" + nm
    return al


def build_fused(Tl=T):
    es = ExitStack()
    nc = bass.Bass("TRN2", target_bir_lowering=False)
    p = Prog(nc, es, pfx="f_", fused=True, final_outputs={"s74_out"},
             alias=fused_aliases(Tl))
    p.dram("identity", [128, 128], "ExternalInput")
    p.dram("G_cst", [128, 8], "ExternalInput")
    p.dram("G_cos", [Tl, 32], "ExternalInput")
    p.dram("G_sin", [Tl, 32], "ExternalInput")
    p.dram("G_maskn", [128, 128], "ExternalInput")
    ccsrc = p.dram("G_ccsrc", [128, 8], "Internal")
    ccdst = p.dram("G_ccdst", [128, 8], "Internal")

    K = Tl // 4
    emit_stage1(p, Tl)
    emit_dec_qkv(p, "dA_", Tl)
    emit_dec_attn(p, "dB_", Tl)
    emit_dec_tail(p, "dC_", Tl)
    emit_stage3(p, Tl)
    p.push_scope()
    padt = p.sbuf("f_pad", [128, 8])
    t = p.emit("vector", lambda e: e.memset(padt, 0.0), inc=True)
    parts_ap = p.dram_map["s3_parts"]
    p.w("sync", t)
    tld = dma_in(p, padt[0:1, 0:2], parts_ap)
    p.w("sync", tld)
    if COLLECTIVE:
        tst = dma_out(p, ccsrc, padt)
        p.w("gpsimd", tst)
        p.emit("gpsimd", lambda e: e.collective_compute(
            "AllReduce", ALU.add, replica_groups=[list(range(NCORES))],
            ins=[ccsrc.opt()], outs=[ccdst.opt()]), inc=True)
    else:
        p.w("vector", t)
        p.w("vector", tld)
        t2 = p.emit("vector", lambda e: e.tensor_scalar(
            padt, padt, 8.0, None, ALU.mult), inc=True)
        p.w("sync", t2)
        dma_out(p, ccdst, padt)
    p.barrier()
    p.pop_scope()
    emit_stage4(p, Tl)
    emit_dec_qkv(p, "dD_", K)
    emit_dec_attn(p, "dE_", K)
    emit_dec_tail(p, "dF_", K)
    emit_stage5(p, Tl)
    emit_stage6(p, Tl)
    emit_stage7(p, Tl, 0, 4, True)
    emit_stage7(p, Tl, 4, 8, False)
    p.finalize()
    es.close()
    return nc


_NC_CACHE = {}
COLLECTIVE = True
SIM = False
FUSED = True
RUN_CORES = NCORES


def get_nc(name, builder):
    if name not in _NC_CACHE:
        _NC_CACHE[name] = builder()
    return _NC_CACHE[name]


def run_stage(nc, in_maps, out_names):
    import time as _time
    _t0 = _time.time()
    ncore = len(in_maps)
    if SIM:
        from concourse import bass_interp
        res = []
        for m in in_maps:
            sim = bass_interp.MultiCoreSim(nc, 1)
            for k, v in m.items():
                sim.cores[0].tensor(k)[:] = np.asarray(v).reshape(
                    sim.cores[0].tensor(k).shape)
            sim.simulate()
            res.append({o: np.array(sim.cores[0].mem_tensor(o)) for o in out_names})
        return res
    r = run_bass_kernel_spmd(nc, in_maps, core_ids=list(range(ncore)))
    out = [{o: np.asarray(r.results[b][o]) for o in out_names}
           for b in range(ncore)]
    print(f"[stage {out_names[0]}] {_time.time() - _t0:.0f}s", flush=True)
    return out


def cos_sin_tables(S):
    half = 32
    inv = (1.0 / (np.float32(10000.0) ** (np.arange(half, dtype=np.float32)
                                          / np.float32(half)))).astype(np.float32)
    pos = np.arange(S, dtype=np.float32)
    ang = pos[:, None] * inv
    return np.cos(ang).astype(np.float32), np.sin(ang).astype(np.float32)


def module_consts(Tl):
    K = Tl // 4
    cst = np.zeros((128, 8), np.float32)
    cst[:, 0] = LN_EPS
    cst[:, 1] = RMS_EPS
    cst[:, 2] = 0.125
    cst[:, 3] = 1.0 / 1024
    cst[:, 4] = 0.5 / 1024
    cst[:, 5] = -0.5
    cst[:, 6] = float(K)
    cst[:, 7] = 1e9
    ident = np.eye(128, dtype=np.float32)
    maskn = np.where(np.tril(np.ones((128, 128), bool)), 0.0,
                     -3.0e38).astype(np.float32)
    cos, sin = cos_sin_tables(Tl)
    iota = np.broadcast_to(np.arange(Tl, dtype=np.float32), (128, Tl)).copy()
    ivals = (np.arange(128)[:, None] * (Tl // 128)
             + np.arange(Tl // 128)[None, :]).astype(np.float32)
    prem = np.triu(np.ones((128, 128), np.float32), 1)
    tau = np.arange(Tl)
    rmask = np.broadcast_to((tau % SC != 0).astype(np.float32), (128, Tl)).copy()
    bar = np.broadcast_to((1e6 * ((tau % SC == 0) | (tau % SC == SC - 1))
                           ).astype(np.float32), (128, Tl)).copy()
    bmask = np.zeros((128, 128), np.float32)
    bmask[:64, :64] = 1
    bmask[64:, 64:] = 1
    emask = np.zeros((128, 64), np.float32)
    for e in range(8):
        emask[:, e * 8:e * 8 + e] = 1.0
    return dict(cst=cst, ident=ident, maskn=maskn, cos=cos, sin=sin, iota=iota,
                ivals=ivals, prem=prem, rmask=rmask, bar=bar, bmask=bmask,
                emask=emask)


def kernel(**inputs):
    Tl = T
    w = {k: np.asarray(v) for k, v in inputs.items()}
    x = w["L0_attn_x"]
    nb = x.shape[0]
    CC = module_consts(Tl)
    ccos2, csin2 = cos_sin_tables(Tl)
    ar = lambda v: np.ascontiguousarray(v.astype(np.float32))

    if FUSED:
        nc = get_nc("fused", lambda: build_fused(Tl))
        b1cols = np.stack([np.ascontiguousarray(
            np.asarray(w["L1_ffn_b1"][e]).reshape(32, 128).T) for e in range(8)]
            ).astype(np.float32)
        b2cols = np.concatenate([vec_col(w["L1_ffn_b2"][e]) for e in range(8)],
                                axis=1).astype(np.float32)
        vecs6 = np.concatenate([
            vec_col(w["L1_attn_mix_k"].reshape(-1)),
            vec_col(w["L1_attn_mix_v"].reshape(-1)),
            vec_col(w["L1_attn_mix_r"].reshape(-1)),
            vec_col(w["L1_attn_mix_w"].reshape(-1)),
            vec_col(w["L1_attn_time_decay"].reshape(-1)),
            vec_col(w["L1_attn_ln_w"]), vec_col(w["L1_attn_ln_b"]),
            vec_col(w["L1_attn_time_first"].reshape(-1))], axis=1
            ).astype(np.float32)
        maps = []
        for b in range(nb):
            s1m = stage1_inputs(fm(np.ascontiguousarray(x[b])), w)
            m = {
                "identity": CC["ident"], "G_cst": CC["cst"], "G_cos": CC["cos"],
                "G_sin": CC["sin"], "G_maskn": CC["maskn"],
                "s1_x0": s1m["x0"], "s1_Wk": s1m["Wk"], "s1_Wv": s1m["Wv"],
                "s1_Wr": s1m["Wr"], "s1_Wo": s1m["Wo"], "s1_vecs": s1m["vecs"],
                "dA_wq": ar(w["L0_ffn_wq1"]), "dA_wk": ar(w["L0_ffn_wk1"]),
                "dA_wv": ar(w["L0_ffn_wv1"]), "dA_vec": vec_col(w["L0_ffn_ln1_1"]),
                "dC_wo": ar(w["L0_ffn_wo1"]), "dC_wg": ar(w["L0_ffn_wg1"]),
                "dC_wu": ar(w["L0_ffn_wu1"]), "dC_wd": ar(w["L0_ffn_wd1"]),
                "dC_vec": vec_col(w["L0_ffn_ln2_1"]),
                "s3_pw": ar(w["L0_ffn_prior_w"]),
                "s3_vec": vec_col(w["L0_ffn_prior_norm_w"]),
                "s4_iota": CC["iota"], "s4_ivals": CC["ivals"],
                "s4_prem": CC["prem"],
                "dD_wq": ar(w["L0_ffn_wq2"]), "dD_wk": ar(w["L0_ffn_wk2"]),
                "dD_wv": ar(w["L0_ffn_wv2"]), "dD_vec": vec_col(w["L0_ffn_ln1_2"]),
                "dF_wo": ar(w["L0_ffn_wo2"]), "dF_wg": ar(w["L0_ffn_wg2"]),
                "dF_wu": ar(w["L0_ffn_wu2"]), "dF_wd": ar(w["L0_ffn_wd2"]),
                "dF_vec": vec_col(w["L0_ffn_ln2_2"]),
                "s6_Wr": ar(w["L1_attn_Wr"]), "s6_Wk": ar(w["L1_attn_Wk"]),
                "s6_Wv": ar(w["L1_attn_Wv"]), "s6_Ww": ar(w["L1_attn_Ww"]),
                "s6_Wg": ar(w["L1_attn_Wg"]), "s6_Wo": ar(w["L1_attn_Wo"]),
                "s6_vecs": vecs6, "s6_rmask": CC["rmask"], "s6_bar": CC["bar"],
                "s6_bmask": CC["bmask"],
                "s70_Wg": ar(w["L1_ffn_Wg"]),
                "s70_bg": ar(w["L1_ffn_bg"]).reshape(1, 8),
                "s70_W1": ar(w["L1_ffn_W1"]), "s70_b1": b1cols,
                "s70_W2": ar(w["L1_ffn_W2"]), "s70_b2": b2cols,
                "s70_emask": CC["emask"],
            }
            maps.append(m)
        r = run_stage(nc, maps, ["s74_out"])
        out = np.stack([fm_inv(np.asarray(r[b]["s74_out"]).reshape(NCH, 128, Tl))
                        for b in range(nb)])
        return out.astype(np.float32)

    nc = get_nc("s1", lambda: build_stage1(Tl))
    maps = [stage1_inputs(fm(np.ascontiguousarray(x[b])), w) for b in range(nb)]
    r1 = run_stage(nc, maps, ["xn0", "a0"])

    nc = get_nc("dA", lambda: build_dec_qkv("dA_", Tl))
    maps = [{"dA_hs": r1[b]["xn0"], "dA_wq": ar(w["L0_ffn_wq1"]),
             "dA_wk": ar(w["L0_ffn_wk1"]), "dA_wv": ar(w["L0_ffn_wv1"]),
             "dA_vec": vec_col(w["L0_ffn_ln1_1"]), "dA_cst": CC["cst"],
             "dA_cos": CC["cos"], "dA_sin": CC["sin"],
             "dA_identity": CC["ident"]} for b in range(nb)]
    r2 = run_stage(nc, maps, ["dA_QT", "dA_KT", "dA_Vtok"])

    nc = get_nc("dB", lambda: build_dec_attn("dB_", Tl))
    maps = [{"dB_QT": r2[b]["dA_QT"], "dB_KT": r2[b]["dA_KT"],
             "dB_Vtok": r2[b]["dA_Vtok"], "dB_maskn": CC["maskn"],
             "dB_cst": CC["cst"], "dB_identity": CC["ident"]} for b in range(nb)]
    r3 = run_stage(nc, maps, ["dB_Otok"])

    nc = get_nc("dC", lambda: build_dec_tail("dC_", Tl))
    maps = [{"dC_hs": r1[b]["xn0"], "dC_Otok": r3[b]["dB_Otok"],
             "dC_wo": ar(w["L0_ffn_wo1"]), "dC_wg": ar(w["L0_ffn_wg1"]),
             "dC_wu": ar(w["L0_ffn_wu1"]), "dC_wd": ar(w["L0_ffn_wd1"]),
             "dC_vec": vec_col(w["L0_ffn_ln2_1"]), "dC_cst": CC["cst"],
             "dC_identity": CC["ident"]} for b in range(nb)]
    r4 = run_stage(nc, maps, ["dC_proc"])

    nc = get_nc("s3", lambda: build_stage3(Tl))
    maps = [{"s3_xn0": r1[b]["xn0"], "s3_proc": r4[b]["dC_proc"],
             "s3_pw": ar(w["L0_ffn_prior_w"]),
             "s3_vec": vec_col(w["L0_ffn_prior_norm_w"]), "s3_cst": CC["cst"],
             "s3_identity": CC["ident"]} for b in range(nb)]
    r5 = run_stage(nc, maps, ["s3_dst", "s3_dch", "s3_parts", "s3_ptok",
                              "s3_f0tok"])

    parts_all = np.concatenate([np.asarray(r5[b]["s3_parts"]).reshape(1, 2)
                                for b in range(nb)], axis=0).astype(np.float32)
    if nb < NCORES:
        parts_all = np.tile(parts_all[0:1], (NCORES, 1)) if nb == 1 else             np.concatenate([parts_all] * (NCORES // nb), axis=0)

    nc = get_nc("s4", lambda: build_stage4(Tl))
    maps = [{"s4_dst": r5[b]["s3_dst"], "s4_dch": r5[b]["s3_dch"],
             "s4_parts": parts_all, "s4_ptok": r5[b]["s3_ptok"],
             "s4_cos": CC["cos"], "s4_sin": CC["sin"], "s4_iota": CC["iota"],
             "s4_ivals": CC["ivals"], "s4_prem": CC["prem"], "s4_cst": CC["cst"],
             "s4_identity": CC["ident"]} for b in range(nb)]
    r6 = run_stage(nc, maps, ["s4_isc", "s4_seltok", "s4_selfm", "s4_cossel",
                              "s4_sinsel"])

    K = Tl // 4
    nc = get_nc("dD", lambda: build_dec_qkv("dD_", K))
    maps = [{"dD_hs": r6[b]["s4_selfm"], "dD_wq": ar(w["L0_ffn_wq2"]),
             "dD_wk": ar(w["L0_ffn_wk2"]), "dD_wv": ar(w["L0_ffn_wv2"]),
             "dD_vec": vec_col(w["L0_ffn_ln1_2"]), "dD_cst": CC["cst"],
             "dD_cos": r6[b]["s4_cossel"], "dD_sin": r6[b]["s4_sinsel"],
             "dD_identity": CC["ident"]} for b in range(nb)]
    r7 = run_stage(nc, maps, ["dD_QT", "dD_KT", "dD_Vtok"])

    nc = get_nc("dE", lambda: build_dec_attn("dE_", K))
    maps = [{"dE_QT": r7[b]["dD_QT"], "dE_KT": r7[b]["dD_KT"],
             "dE_Vtok": r7[b]["dD_Vtok"], "dE_maskn": CC["maskn"],
             "dE_cst": CC["cst"], "dE_identity": CC["ident"]} for b in range(nb)]
    r8 = run_stage(nc, maps, ["dE_Otok"])

    nc = get_nc("dF", lambda: build_dec_tail("dF_", K))
    maps = [{"dF_hs": r6[b]["s4_selfm"], "dF_Otok": r8[b]["dE_Otok"],
             "dF_wo": ar(w["L0_ffn_wo2"]), "dF_wg": ar(w["L0_ffn_wg2"]),
             "dF_wu": ar(w["L0_ffn_wu2"]), "dF_wd": ar(w["L0_ffn_wd2"]),
             "dF_vec": vec_col(w["L0_ffn_ln2_2"]), "dF_cst": CC["cst"],
             "dF_identity": CC["ident"]} for b in range(nb)]
    r9 = run_stage(nc, maps, ["dF_proc"])

    nc = get_nc("s5", lambda: build_stage5(Tl))
    maps = [{"s5_proc2": r9[b]["dF_proc"], "s5_seltok": r6[b]["s4_seltok"],
             "s5_isc": r6[b]["s4_isc"], "s5_f0tok": r5[b]["s3_f0tok"],
             "s5_x0": fm(np.ascontiguousarray(x[b])), "s5_a0": r1[b]["a0"],
             "s5_identity": CC["ident"]} for b in range(nb)]
    r10 = run_stage(nc, maps, ["s5_x1"])

    vecs6 = np.concatenate([
        vec_col(w["L1_attn_mix_k"].reshape(-1)),
        vec_col(w["L1_attn_mix_v"].reshape(-1)),
        vec_col(w["L1_attn_mix_r"].reshape(-1)),
        vec_col(w["L1_attn_mix_w"].reshape(-1)),
        vec_col(w["L1_attn_time_decay"].reshape(-1)),
        vec_col(w["L1_attn_ln_w"]), vec_col(w["L1_attn_ln_b"]),
        vec_col(w["L1_attn_time_first"].reshape(-1))], axis=1).astype(np.float32)
    nc = get_nc("s6", lambda: build_stage6(Tl))
    maps = [{"s6_x1": r10[b]["s5_x1"], "s6_Wr": ar(w["L1_attn_Wr"]),
             "s6_Wk": ar(w["L1_attn_Wk"]), "s6_Wv": ar(w["L1_attn_Wv"]),
             "s6_Ww": ar(w["L1_attn_Ww"]), "s6_Wg": ar(w["L1_attn_Wg"]),
             "s6_Wo": ar(w["L1_attn_Wo"]), "s6_vecs": vecs6, "s6_cst": CC["cst"],
             "s6_identity": CC["ident"], "s6_rmask": CC["rmask"],
             "s6_bar": CC["bar"], "s6_bmask": CC["bmask"]} for b in range(nb)]
    r11 = run_stage(nc, maps, ["s6_xn1", "s6_a1"])

    b1cols = np.stack([np.ascontiguousarray(
        np.asarray(w["L1_ffn_b1"][e]).reshape(32, 128).T) for e in range(8)]
        ).astype(np.float32)
    b2cols = np.concatenate([vec_col(w["L1_ffn_b2"][e]) for e in range(8)],
                            axis=1).astype(np.float32)
    com = {"Wg": ar(w["L1_ffn_Wg"]), "bg": ar(w["L1_ffn_bg"]).reshape(1, 8),
           "W1": ar(w["L1_ffn_W1"]), "b1": b1cols, "W2": ar(w["L1_ffn_W2"]),
           "b2": b2cols, "emask": CC["emask"], "identity": CC["ident"]}
    nc = get_nc("s70", lambda: build_stage7(Tl, 0, 4, True))
    maps = [{**{f"s70_{k}": v for k, v in com.items()},
             "s70_xn1": r11[b]["s6_xn1"], "s70_x1": r10[b]["s5_x1"],
             "s70_a1": r11[b]["s6_a1"]} for b in range(nb)]
    r12 = run_stage(nc, maps, ["s70_out", "s70_ct_out"])

    nc = get_nc("s74", lambda: build_stage7(Tl, 4, 8, False))
    maps = [{**{f"s74_{k}": v for k, v in com.items()},
             "s74_xn1": r11[b]["s6_xn1"], "s74_base": r12[b]["s70_out"],
             "s74_ct_in": r12[b]["s70_ct_out"]} for b in range(nb)]
    r13 = run_stage(nc, maps, ["s74_out"])

    out = np.stack([fm_inv(np.asarray(r13[b]["s74_out"]).reshape(NCH, 128, Tl))
                    for b in range(nb)])
    return out.astype(np.float32)


# version 07
# speedup vs baseline: 1.3963x; 1.2906x over previous
import numpy as np
from contextlib import ExitStack

import concourse.bass as bass
import concourse.mybir as mybir
from concourse.bass_utils import run_bass_kernel_spmd

F32 = mybir.dt.float32
BF16 = mybir.dt.bfloat16
U32 = mybir.dt.uint32
I32 = mybir.dt.int32
AF = mybir.ActivationFunctionType
ALU = mybir.AluOpType
AX = mybir.AxisListType

T = 4096
C = 1024
NCH = 8
NCORES = 8
TT = 512

LN_EPS = 1e-5
RMS_EPS = 1e-6

QSEM = {"tensor": "qt", "vector": "qv", "scalar": "qs", "gpsimd": "qg", "sync": "qy"}


class Prog:
    ENGINES = ("tensor", "vector", "scalar", "gpsimd", "sync")

    def __init__(self, nc, es, pfx="", fused=False, final_outputs=(), alias=None):
        self.nc = nc
        self.es = es
        self.pfx = pfx
        self.fused = fused
        self.final_outputs = set(final_outputs)
        self.alias = dict(alias or {})
        self.dram_map = {}
        self.scope = es
        self.ops = {e: [] for e in self.ENGINES}
        self.sems = {}
        self.cnt = {}
        self.banks = [nc.alloc_psum_tensor(f"{pfx}bank{i}", [128, 512], F32).ap()
                      for i in range(8)]
        self.bank_release = [None] * 8
        self._rr = 0
        self.last_store = None

    def resolve(self, name):
        seen = set()
        while name in self.alias and name not in seen:
            seen.add(name)
            name = self.alias[name]
        return name

    def dram(self, name, shape, kind, dt=F32):
        name = self.resolve(name)
        if name in self.dram_map:
            return self.dram_map[name]
        k = kind
        if self.fused and kind == "ExternalOutput" and name not in self.final_outputs:
            k = "Internal"
        ap = self.nc.dram_tensor(name, shape, dt, kind=k).ap()
        self.dram_map[name] = ap
        return ap

    def sbuf(self, name, shape, dt=F32):
        return self.scope.enter_context(self.nc.sbuf_tensor(name, shape, dt)).ap()

    def push_scope(self):
        self.scope = ExitStack()
        return self.scope

    def pop_scope(self):
        self.scope.close()
        self.scope = self.es

    def sem(self, name):
        if name not in self.sems:
            self.sems[name] = self.es.enter_context(self.nc.semaphore(self.pfx + name))
            self.cnt[name] = 0
        return self.sems[name]

    def emit(self, engine, fn, inc=False, k=1):
        s = v = None
        if inc:
            name = QSEM[engine]
            s = self.sem(name)
            self.cnt[name] += k
            v = k

        def thunk(eng, fn=fn, s=s, v=v):
            ins = fn(eng)
            if s is not None:
                ins.then_inc(s, v)
        self.ops[engine].append(thunk)
        return (QSEM[engine], self.cnt[QSEM[engine]]) if inc else None

    DMA_POOL = {"sync": 10, "gpsimd": 4}

    def dma(self, engine, fn):
        npool = self.DMA_POOL[engine]
        key = f"_dmarr_{engine}"
        idx = getattr(self, key, 0)
        setattr(self, key, idx + 1)
        name = f"d{engine[0]}{idx % npool}"
        s = self.sem(name)
        prev = self.cnt[name]
        if prev > 0:
            self.ops[engine].append(lambda eng, s=s, v=prev: eng.wait_ge(s, v))
        self.cnt[name] += 16

        def thunk(eng, fn=fn, s=s):
            fn(eng).then_inc(s, 16)
        self.ops[engine].append(thunk)
        return (name, self.cnt[name])

    def t(self, name):
        self.sem(name)
        return (name, self.cnt[name])

    def w(self, engine, ticket):
        if ticket is not None and ticket[1] > 0:
            s = self.sem(ticket[0])
            v = ticket[1]
            self.ops[engine].append(lambda eng, s=s, v=v: eng.wait_ge(s, v))

    def barrier(self, engines=None):
        snap = [(n, c) for n, c in self.cnt.items() if c > 0]
        for e in (engines or self.ENGINES):
            for n, c in snap:
                self.w(e, (n, c))

    def bank(self):
        i = self._rr % 7
        self._rr += 1
        self.w("tensor", self.bank_release[i])
        return i

    def bank7(self):
        self.w("tensor", self.bank_release[7])
        return 7

    def bank_done(self, i, ticket):
        self.bank_release[i] = ticket

    def finalize(self):
        with self.nc.Block() as block:
            for e in self.ENGINES:
                thunks = self.ops[e]

                def runner(eng, thunks=thunks):
                    for th in thunks:
                        th(eng)
                getattr(block, e)(runner)


def dma_in(p, sb_ap, dram_ap, eng="sync"):
    return p.dma(eng, lambda e, o=sb_ap, i=dram_ap: e.dma_start(out=o, in_=i))


def dma_out(p, dram_ap, sb_ap, eng="sync"):
    return p.dma(eng, lambda e, o=dram_ap, i=sb_ap: e.dma_start(out=o, in_=i))


def load_weight_T(p, w_dram, wT_sb, wnat_sb, ident, d_out=C, d_in=C):
    t = dma_in(p, wnat_sb, w_dram.rearrange("(m p) c -> p m c", p=128))
    p.w("tensor", t)
    for m in range(d_out // 128):
        for kc in range(d_in // 128):
            b = p.bank()
            tpe = p.emit("tensor",
                         lambda e, b=b, m=m, kc=kc: e.transpose(
                             p.banks[b][:, 0:128],
                             wnat_sb[:, m, kc * 128:(kc + 1) * 128], ident),
                         inc=True)
            p.w("vector", tpe)
            tev = p.emit("vector",
                         lambda e, b=b, m=m, kc=kc: e.tensor_copy(
                             wT_sb[:, kc, m * 128:(m + 1) * 128], p.banks[b][:, 0:128]),
                         inc=True)
            p.bank_done(b, tev)
    return p.t("qv")


def proj_fm(p, out_cb, xt_sb, w_sb, n, n_m=NCH, n_k=NCH):
    for m in range(n_m):
        b = p.bank()
        tpe = None
        for kc in range(n_k):
            tpe = p.emit("tensor",
                         lambda e, b=b, m=m, kc=kc: e.matmul(
                             p.banks[b][:, 0:n], w_sb[:, kc, m * 128:(m + 1) * 128],
                             xt_sb[:, kc, 0:n],
                             start=(kc == 0), stop=(kc == n_k - 1)),
                         inc=(kc == n_k - 1))
        tev = out_cb(m, b, tpe)
        p.bank_done(b, tev)


def ln_tile(p, x_sb, n, sq_sb, a_bc, b_bc, rows, ones_col, ones_row,
            eps_col, kind, out_sb, w_col=None, b_col=None, n_ch=NCH):
    r_sum = rows[0:1, 0:n]
    r_sq = rows[0:1, 512:512 + n]
    r_a = rows[0:1, 1024:1024 + n]
    r_b = rows[0:1, 1536:1536 + n]
    nf = float(n_ch * 128)

    tsq = p.emit("scalar", lambda e: e.activation(
        sq_sb[:, 0:n_ch, 0:n], x_sb[:, 0:n_ch, 0:n], AF.Square), inc=True)
    if kind == "ln":
        b1 = p.bank()
        for kc in range(n_ch):
            t1 = p.emit("tensor",
                        lambda e, b1=b1, kc=kc: e.matmul(
                            p.banks[b1][0:1, 0:n], ones_col, x_sb[:, kc, 0:n],
                            start=(kc == 0), stop=(kc == n_ch - 1)),
                        inc=(kc == n_ch - 1))
    b2 = p.bank()
    p.w("tensor", tsq)
    for kc in range(n_ch):
        t2 = p.emit("tensor",
                    lambda e, b2=b2, kc=kc: e.matmul(
                        p.banks[b2][0:1, 0:n], ones_col, sq_sb[:, kc, 0:n],
                        start=(kc == 0), stop=(kc == n_ch - 1)),
                    inc=(kc == n_ch - 1))
    p.w("vector", t2)
    if kind == "ln":
        p.emit("vector", lambda e: e.tensor_scalar(r_sum, p.banks[b1][0:1, 0:n],
                                                   1.0 / nf, None, ALU.mult), inc=True)
        tb = p.emit("vector", lambda e: e.tensor_scalar(r_sq, p.banks[b2][0:1, 0:n],
                                                        1.0 / nf, None, ALU.mult),
                    inc=True)
        p.bank_done(b1, tb)
        p.bank_done(b2, tb)
        p.w("vector", tb)
        p.emit("vector", lambda e: e.tensor_tensor(r_a, r_sum, r_sum, ALU.mult),
               inc=True)
        p.w("vector", p.t("qv"))
        tv = p.emit("vector", lambda e: e.tensor_tensor(r_a, r_sq, r_a, ALU.subtract),
                    inc=True)
        p.w("scalar", tv)
        ts = p.emit("scalar", lambda e: e.activation(r_b, r_a, AF.Sqrt,
                                                     bias=eps_col[0:1, 0:1]), inc=True)
        p.w("vector", ts)
        p.emit("vector", lambda e: e.reciprocal(r_a, r_b), inc=True)
        p.w("vector", p.t("qv"))
        p.emit("vector", lambda e: e.tensor_tensor(r_b, r_sum, r_a, ALU.mult), inc=True)
        p.w("vector", p.t("qv"))
        tv = p.emit("vector", lambda e: e.tensor_scalar(r_b, r_b, -1.0, None, ALU.mult),
                    inc=True)
    else:
        tb = p.emit("vector", lambda e: e.tensor_scalar(r_a, p.banks[b2][0:1, 0:n],
                                                        1.0 / nf, None, ALU.mult),
                    inc=True)
        p.bank_done(b2, tb)
        p.w("scalar", tb)
        ts = p.emit("scalar", lambda e: e.activation(r_b, r_a, AF.Sqrt,
                                                     bias=eps_col[0:1, 0:1]), inc=True)
        p.w("vector", ts)
        tv = p.emit("vector", lambda e: e.reciprocal(r_a, r_b), inc=True)
    b3 = p.bank()
    p.w("tensor", tv)
    t3 = p.emit("tensor", lambda e, b3=b3: e.matmul(p.banks[b3][:, 0:n], ones_row, r_a,
                                                    start=True, stop=True), inc=True)
    if kind == "ln":
        b4 = p.bank()
        t3 = p.emit("tensor", lambda e, b4=b4: e.matmul(p.banks[b4][:, 0:n], ones_row,
                                                        r_b, start=True, stop=True),
                    inc=True)
    p.w("vector", t3)
    ta = p.emit("vector", lambda e, b3=b3: e.tensor_copy(a_bc[:, 0:n],
                                                         p.banks[b3][:, 0:n]), inc=True)
    p.bank_done(b3, ta)
    if kind == "ln":
        ta = p.emit("vector", lambda e, b4=b4: e.tensor_copy(b_bc[:, 0:n],
                                                             p.banks[b4][:, 0:n]),
                    inc=True)
        p.bank_done(b4, ta)
    p.w("vector", ta)
    for kc in range(n_ch):
        if kind == "ln":
            t = p.emit("vector", lambda e, kc=kc: e.tensor_tensor(
                out_sb[:, kc, 0:n], x_sb[:, kc, 0:n], a_bc[:, 0:n], ALU.mult), inc=True)
            p.w("vector", t)
            t = p.emit("vector", lambda e, kc=kc: e.tensor_tensor(
                out_sb[:, kc, 0:n], out_sb[:, kc, 0:n], b_bc[:, 0:n], ALU.add),
                inc=True)
            if w_col is not None:
                p.w("vector", t)
                p.emit("vector", lambda e, kc=kc: e.tensor_scalar(
                    out_sb[:, kc, 0:n], out_sb[:, kc, 0:n],
                    w_col[:, kc:kc + 1], b_col[:, kc:kc + 1], ALU.mult, ALU.add),
                    inc=True)
        else:
            if w_col is None:
                p.emit("vector", lambda e, kc=kc: e.tensor_tensor(
                    out_sb[:, kc, 0:n], x_sb[:, kc, 0:n], a_bc[:, 0:n], ALU.mult),
                    inc=True)
            else:
                p.emit("vector", lambda e, kc=kc: e.scalar_tensor_tensor(
                    out_sb[:, kc, 0:n], x_sb[:, kc, 0:n], w_col[:, kc:kc + 1],
                    a_bc[:, 0:n], ALU.mult, ALU.mult), inc=True)
    return p.t("qv")


def mix_proj_pass(p, src_d, mix, wT, dst, sig, xc2, xm2, xsq, osb2, Tl,
                  evac_extra=None):
    ntt = Tl // TT
    use_tk = {}
    st_tk = {}
    mt = {}

    def prep(it, wi):
        t0 = it * TT
        p.w("sync", use_tk.get(wi))
        p.w("vector", use_tk.get(wi))
        tl1 = dma_in(p, xc2[wi], src_d[:, :, t0:t0 + TT].rearrange("c p t -> p c t"))
        if t0 == 0:
            tl2 = dma_in(p, xm2[wi][:, :, 1:TT],
                         src_d[:, :, 0:TT - 1].rearrange("c p t -> p c t"))
            p.w("vector", tl1)
            p.w("vector", tl2)
            p.emit("vector", lambda e, wi=wi: e.memset(xm2[wi][:, :, 0:1], 0.0),
                   inc=True)
        else:
            tl2 = dma_in(p, xm2[wi],
                         src_d[:, :, t0 - 1:t0 + TT - 1].rearrange("c p t -> p c t"))
            p.w("vector", tl1)
            p.w("vector", tl2)
        for kc in range(NCH):
            p.w("vector", p.t("qv"))
            p.emit("vector", lambda e, kc=kc, wi=wi: e.scalar_tensor_tensor(
                xsq[:, kc, :], xm2[wi][:, kc, :], mix[:, kc:kc + 1],
                xm2[wi][:, kc, :], ALU.mult, ALU.subtract), inc=True)
            p.w("vector", p.t("qv"))
            p.emit("vector", lambda e, kc=kc, wi=wi: e.tensor_scalar(
                xm2[wi][:, kc, :], xc2[wi][:, kc, :], mix[:, kc:kc + 1], None,
                ALU.mult), inc=True)
            p.w("vector", p.t("qv"))
            p.emit("vector", lambda e, kc=kc, wi=wi: e.tensor_tensor(
                xm2[wi][:, kc, :], xm2[wi][:, kc, :], xsq[:, kc, :], ALU.subtract),
                inc=True)
        return p.t("qv")

    mt[0] = prep(0, 0)
    for it in range(ntt):
        wi = it % 2
        if it + 1 < ntt:
            mt[1 - wi] = prep(it + 1, 1 - wi)
        p.w("tensor", mt[wi])

        def evac(m, b, pet, wi=wi, sig=sig):
            eng = "scalar" if sig else "vector"
            p.w(eng, pet)
            p.w(eng, st_tk.get(wi))
            if sig:
                t = p.emit("scalar", lambda e, m=m, b=b, wi=wi: e.activation(
                    osb2[wi][:, m, :], p.banks[b][:, 0:TT], AF.Sigmoid), inc=True)
            else:
                t = p.emit("vector", lambda e, m=m, b=b, wi=wi: e.tensor_copy(
                    osb2[wi][:, m, :], p.banks[b][:, 0:TT]), inc=True)
            return t
        proj_fm(p, evac, xm2[wi], wT, TT)
        use_tk[wi] = p.t("qt")
        p.w("sync", p.t("qv"))
        p.w("sync", p.t("qs"))
        t0 = it * TT
        st_tk[wi] = dma_out(p, dst[:, :, t0:t0 + TT].rearrange("c p t -> p c t"),
                            osb2[wi])
    p.barrier()


def emit_stage1(p, Tl=T):
    nc = p.nc
    p.push_scope()
    ntt = Tl // TT

    D = lambda name, shape, kind: p.dram(
        ("s1_" + name) if (kind == "Internal" or p.fused) else name, shape, kind)
    x0 = D("x0", [NCH, 128, Tl], "ExternalInput")
    Wk = D("Wk", [C, C], "ExternalInput")
    Wv = D("Wv", [C, C], "ExternalInput")
    Wr = D("Wr", [C, C], "ExternalInput")
    Wo = D("Wo", [C, C], "ExternalInput")
    vecs = D("vecs", [128, 8 * NCH], "ExternalInput")
    consts = D("consts", [128, 8], "ExternalInput")
    idc = D("identity", [128, 128], "ExternalInput")
    xn0 = D("xn0", [NCH, 128, Tl], "ExternalOutput")
    a0 = D("a0", [NCH, 128, Tl], "ExternalOutput")
    xt_d = D("xt_d", [NCH, 128, Tl], "Internal")
    K_d = D("K_d", [NCH, 128, Tl], "Internal")
    V_d = D("V_d", [NCH, 128, Tl], "Internal")
    R_d = D("R_d", [NCH, 128, Tl], "Internal")
    Z_d = D("Z_d", [NCH, 128, Tl], "Internal")

    A = lambda name, shape, dt=F32: p.sbuf("s1_" + name, shape, dt)
    ident = A("ident", [128, 128])
    ones_col = A("ones_col", [128, 1])
    ones_row = A("ones_row", [1, 128])
    sb_vec = A("sb_vec", [128, 8 * NCH])
    sb_cst = A("sb_cst", [128, 8])
    fcol = A("fcol", [128, NCH])
    wT = A("wT", [128, NCH, C])
    wnat = A("wnat", [128, NCH, C])
    xa = A("xa", [128, NCH, TT])
    xb = A("xb", [128, NCH, TT])
    xc = A("xc", [128, NCH, TT])
    xm = A("xm", [128, NCH, TT])
    xsq = A("xsq", [128, NCH, TT])
    osb = A("osb", [128, NCH, TT])
    a_bc = A("a_bc", [128, TT])
    b_bc = A("b_bc", [128, TT])
    rows = A("rows", [1, 2048])

    xcB = A("xcB", [128, NCH, TT])
    xmB = A("xmB", [128, NCH, TT])
    osbB = wnat[:, :, 0:TT]
    kc_t = xa.rearrange("p a b -> p (a b)")[:, 0:Tl]
    vc_t = xb.rearrange("p a b -> p (a b)")[:, 0:Tl]
    rc_t = xc.rearrange("p a b -> p (a b)")[:, 0:Tl]
    sc_t = xm.rearrange("p a b -> p (a b)")[:, 0:Tl]

    dma_in(p, ident, idc)
    t = dma_in(p, sb_vec, vecs)
    dma_in(p, sb_cst, consts)
    mixk = sb_vec[:, 0:NCH]
    mixv = sb_vec[:, NCH:2 * NCH]
    mixr = sb_vec[:, 2 * NCH:3 * NCH]
    tdec = sb_vec[:, 3 * NCH:4 * NCH]
    tfirst = sb_vec[:, 4 * NCH:5 * NCH]
    lnw = sb_vec[:, 5 * NCH:6 * NCH]
    lnb = sb_vec[:, 6 * NCH:7 * NCH]
    dcol = sb_vec[:, 7 * NCH:8 * NCH]
    eps_ln = sb_cst[:, 0:1]
    p.w("scalar", t)
    p.emit("scalar", lambda e: e.activation(dcol, tdec, AF.Sigmoid), inc=True)
    p.emit("scalar", lambda e: e.activation(fcol, tfirst, AF.Exp), inc=True)
    p.emit("vector", lambda e: e.memset(ones_col, 1.0), inc=True)
    p.emit("vector", lambda e: e.memset(ones_row, 1.0), inc=True)
    p.barrier()

    for it in range(ntt):
        t0 = it * TT
        dma_in(p, xa, x0[:, :, t0:t0 + TT].rearrange("c p t -> p c t"))
        p.barrier()
        tv = ln_tile(p, xa, TT, xsq, a_bc, b_bc, rows, ones_col, ones_row, eps_ln,
                     "ln", xb)
        p.w("sync", tv)
        dma_out(p, xn0[:, :, t0:t0 + TT].rearrange("c p t -> p c t"), xb)
        p.barrier()
        tv = ln_tile(p, xb, TT, xsq, a_bc, b_bc, rows, ones_col, ones_row, eps_ln,
                     "ln", xc, w_col=lnw, b_col=lnb)
        p.w("sync", tv)
        dma_out(p, xt_d[:, :, t0:t0 + TT].rearrange("c p t -> p c t"), xc)
        p.barrier()

    for (W_dram, mix, dst, sig) in ((Wk, mixk, K_d, False), (Wv, mixv, V_d, False),
                                    (Wr, mixr, R_d, True)):
        load_weight_T(p, W_dram, wT, wnat, ident)
        p.barrier()
        mix_proj_pass(p, xt_d, mix, wT, dst, sig, [xc, xcB], [xm, xmB], xsq,
                      [osb, osbB], Tl)

    load_weight_T(p, Wo, wT, wnat, ident)
    for kc in range(NCH):
        dma_in(p, kc_t, K_d[kc])
        dma_in(p, vc_t, V_d[kc])
        dma_in(p, rc_t, R_d[kc])
        p.barrier()
        t = p.emit("vector", lambda e, kc=kc: e.tensor_tensor_scan(
            sc_t, dcol[:, kc:kc + 1].to_broadcast([128, Tl]), kc_t,
            0.0, ALU.mult, ALU.add), inc=True)
        p.w("vector", t)
        t = p.emit("vector", lambda e, kc=kc: e.tensor_scalar(
            sc_t, sc_t, fcol[:, kc:kc + 1], None, ALU.mult), inc=True)
        p.w("vector", t)
        t = p.emit("vector", lambda e: e.tensor_tensor(sc_t, sc_t, kc_t, ALU.add),
                   inc=True)
        p.w("vector", t)
        t = p.emit("vector", lambda e: e.tensor_tensor(sc_t, sc_t, vc_t, ALU.mult),
                   inc=True)
        p.w("vector", t)
        tv = p.emit("vector", lambda e: e.tensor_tensor(sc_t, sc_t, rc_t, ALU.mult),
                    inc=True)
        p.w("sync", tv)
        dma_out(p, Z_d[kc], sc_t)
        p.barrier()

    p.barrier()
    for it in range(ntt):
        t0 = it * TT
        dma_in(p, xa, Z_d[:, :, t0:t0 + TT].rearrange("c p t -> p c t"))
        dma_in(p, xb, xn0[:, :, t0:t0 + TT].rearrange("c p t -> p c t"))
        p.barrier()

        def evac(m, b, pet):
            p.w("vector", pet)
            return p.emit("vector", lambda e, m=m, b=b: e.tensor_tensor(
                osb[:, m, :], p.banks[b][:, 0:TT], xb[:, m, :], ALU.add), inc=True)
        proj_fm(p, evac, xa, wT, TT)
        p.barrier(("sync",))
        dma_out(p, a0[:, :, t0:t0 + TT].rearrange("c p t -> p c t"), osb)
        p.barrier()

    p.barrier()
    p.pop_scope()


def emit_dec_qkv(p, pfx, S):
    nc = p.nc
    p.push_scope()
    ntt = S // TT

    D = lambda n, s, k: p.dram(pfx + n, s, k)
    hs = D("hs", [NCH, 128, S], "ExternalInput")
    wq = D("wq", [C, C], "ExternalInput")
    wk = D("wk", [C, C], "ExternalInput")
    wv = D("wv", [C, C], "ExternalInput")
    vec = D("vec", [128, NCH], "ExternalInput")
    cst = D("cst", [128, 8], "ExternalInput")
    cos_d = D("cos", [S, 32], "ExternalInput")
    sin_d = D("sin", [S, 32], "ExternalInput")
    QT = D("QT", [NCH, 128, S], "ExternalOutput")
    KT = D("KT", [NCH, 128, S], "ExternalOutput")
    Vtok = D("Vtok", [S, C], "ExternalOutput")
    idc = D("identity", [128, 128], "ExternalInput")

    A = lambda n, s, dt=F32: p.sbuf(pfx + n, s, dt)
    ident = A("ident", [128, 128])
    ones_col = A("ones_col", [128, 1])
    ones_row = A("ones_row", [1, 128])
    sb_vec = A("sb_vec", [128, NCH])
    sb_cst = A("sb_cst", [128, 8])
    w1 = A("w1", [128, NCH, C])
    w2 = A("w2", [128, NCH, C])
    w3 = A("w3", [128, NCH, C])
    xa = A("xa", [128, NCH, TT])
    xh = A("xh", [128, NCH, TT])
    xsq = A("xsq", [128, NCH, TT])
    ptile = A("ptile", [128, C])
    ptile2 = A("ptile2", [128, C])
    ptile3 = A("ptile3", [128, 512])
    oT = A("oT", [128, NCH, TT])
    cs = A("cs", [128, 64])
    a_bc = A("a_bc", [128, TT])
    b_bc = A("b_bc", [128, TT])
    rows = A("rows", [1, 2048])

    dma_in(p, ident, idc)
    t = dma_in(p, sb_vec, vec)
    dma_in(p, sb_cst, cst)
    dma_in(p, w1, wq.rearrange("(a p) b -> p a b", p=128))
    dma_in(p, w2, wk.rearrange("(a p) b -> p a b", p=128))
    dma_in(p, w3, wv.rearrange("(a p) b -> p a b", p=128))
    p.emit("vector", lambda e: e.memset(ones_col, 1.0), inc=True)
    p.emit("vector", lambda e: e.memset(ones_row, 1.0), inc=True)
    p.barrier()
    eps_rms = sb_cst[:, 1:2]

    for it in range(ntt):
        t0 = it * TT
        dma_in(p, xa, hs[:, :, t0:t0 + TT].rearrange("c p t -> p c t"))
        p.barrier()
        ln_tile(p, xa, TT, xsq, a_bc, b_bc, rows, ones_col, ones_row, eps_rms,
                "rms", xh, w_col=sb_vec)
        p.barrier()
        for wi, (wsb, dst, rope) in enumerate(((w1, QT, True), (w2, KT, True),
                                               (w3, Vtok, False))):
            for st in range(TT // 128):
                tok0 = t0 + st * 128
                p.w("vector", p.last_store)
                for half in range(2):
                    b = p.bank()
                    for kc in range(NCH):
                        tpe = p.emit("tensor",
                                     lambda e, b=b, kc=kc, st=st, half=half, wsb=wsb:
                                     e.matmul(p.banks[b][:, 0:512],
                                              xh[:, kc, st * 128:(st + 1) * 128],
                                              wsb[:, kc, half * 512:(half + 1) * 512],
                                              start=(kc == 0), stop=(kc == NCH - 1)),
                                     inc=(kc == NCH - 1))
                    p.w("vector", tpe)
                    tev = p.emit("vector", lambda e, b=b, half=half: e.tensor_copy(
                        ptile[:, half * 512:(half + 1) * 512], p.banks[b][:, 0:512]),
                        inc=True)
                    p.bank_done(b, tev)
                src = ptile
                if rope:
                    p.w("sync", p.t("qv"))
                    dma_in(p, cs[:, 0:32], cos_d[tok0:tok0 + 128, :])
                    dma_in(p, cs[:, 32:64], sin_d[tok0:tok0 + 128, :])
                    p.barrier(("vector",))
                    v = ptile.rearrange("p (h d) -> p h d", h=16)
                    o = ptile2.rearrange("p (h d) -> p h d", h=16)
                    sq16 = ptile3.rearrange("p (h d) -> p h d", h=16)
                    cosb = cs[:, 0:32].rearrange("p (o t) -> p o t", o=1).to_broadcast([128, 16, 32])
                    sinb = cs[:, 32:64].rearrange("p (o t) -> p o t", o=1).to_broadcast([128, 16, 32])
                    p.emit("vector", lambda e: e.tensor_tensor(
                        o[:, :, 0:32], v[:, :, 0:32], cosb, ALU.mult), inc=True)
                    t2 = p.emit("vector", lambda e: e.tensor_tensor(
                        sq16, v[:, :, 32:64], sinb, ALU.mult), inc=True)
                    p.w("vector", t2)
                    p.emit("vector", lambda e: e.tensor_tensor(
                        o[:, :, 0:32], o[:, :, 0:32], sq16, ALU.subtract), inc=True)
                    t4 = p.emit("vector", lambda e: e.tensor_tensor(
                        o[:, :, 32:64], v[:, :, 32:64], cosb, ALU.mult), inc=True)
                    p.w("vector", t4)
                    t5 = p.emit("vector", lambda e: e.tensor_tensor(
                        sq16, v[:, :, 0:32], sinb, ALU.mult), inc=True)
                    p.w("vector", t5)
                    p.emit("vector", lambda e: e.tensor_tensor(
                        o[:, :, 32:64], o[:, :, 32:64], sq16, ALU.add), inc=True)
                    src = ptile2
                    p.barrier(("tensor",))
                if dst is Vtok:
                    p.barrier(("sync",))
                    p.last_store = dma_out(p, Vtok[tok0:tok0 + 128, :], src)
                else:
                    for j in range(NCH):
                        b = p.bank()
                        tpe = p.emit("tensor", lambda e, b=b, j=j, src=src: e.transpose(
                            p.banks[b][:, 0:128], src[:, j * 128:(j + 1) * 128], ident),
                            inc=True)
                        p.w("vector", tpe)
                        tev = p.emit("vector", lambda e, b=b, j=j, st=st: e.tensor_copy(
                            oT[:, j, st * 128:(st + 1) * 128], p.banks[b][:, 0:128]),
                            inc=True)
                        p.bank_done(b, tev)
            if dst is not Vtok:
                p.barrier(("sync",))
                p.last_store = dma_out(
                    p, dst[:, :, t0:t0 + TT].rearrange("c p t -> p c t"), oT)
            p.barrier()
    p.barrier()
    p.pop_scope()


def emit_dec_attn(p, pfx, S):
    nc = p.nc
    p.push_scope()
    nq = S // 128

    D = lambda n, s, k: p.dram(pfx + n, s, k)
    QT = D("QT", [NCH, 128, S], "ExternalInput")
    KT = D("KT", [NCH, 128, S], "ExternalInput")
    Vtok = D("Vtok", [S, C], "ExternalInput")
    maskn = D("maskn", [128, 128], "ExternalInput")
    cst = D("cst", [128, 8], "ExternalInput")
    idc = D("identity", [128, 128], "ExternalInput")
    Otok = D("Otok", [S, C], "ExternalOutput")

    A = lambda n, s, dt=F32: p.sbuf(pfx + n, s, dt)
    ident = A("ident", [128, 128])
    qp2 = [A(f"qp{i}", [128, S]) for i in range(2)]
    kp2 = [A(f"kp{i}", [128, S]) for i in range(2)]
    vp2 = [A(f"vp{i}", [128, S // 128, 128]) for i in range(2)]
    op = A("op", [128, S // 128, 128])
    arow = [A(f"arow{i}", [128, S]) for i in range(2)]
    msk = A("msk", [128, 128])
    at4 = A("at4", [128, 512])
    lsum = [A(f"lsum{i}", [128, 4]) for i in range(2)]
    sb_cst = A("sb_cst", [128, 8])

    dma_in(p, msk, maskn)
    dma_in(p, ident, idc)
    dma_in(p, sb_cst, cst)
    c125 = sb_cst[:, 2:3]

    def emit_qk(h2, qt, bi):
        qp, kp = qkv_cur[0], qkv_cur[1]
        hb = h2 * 64
        W = (qt + 1) * 128
        tev = None
        for s0 in range(0, W, 512):
            n = min(512, W - s0)
            b = p.bank()
            tpe = p.emit("tensor", lambda e, b=b, qt=qt, s0=s0, n=n, hb=hb:
                         e.matmul(p.banks[b][:, 0:n],
                                  qp[hb:hb + 64, qt * 128:(qt + 1) * 128],
                                  kp[hb:hb + 64, s0:s0 + n],
                                  start=True, stop=True), inc=True)
            p.w("vector", tpe)
            tev = p.emit("vector", lambda e, b=b, s0=s0, n=n, bi=bi: e.tensor_copy(
                arow[bi][:, s0:s0 + n], p.banks[b][:, 0:n]), inc=True)
            p.bank_done(b, tev)
        return tev

    def emit_softmax(qt, bi, tev):
        W = (qt + 1) * 128
        p.w("vector", tev)
        t = p.emit("vector", lambda e, qt=qt, W=W, bi=bi: e.tensor_tensor(
            arow[bi][:, qt * 128:W], arow[bi][:, qt * 128:W], msk, ALU.add),
            inc=True)
        p.w("vector", t)
        t = p.emit("vector", lambda e, W=W, bi=bi: e.reduce_max(
            lsum[bi][:, 1:2], arow[bi][:, 0:W], axis=AX.X, negate=True), inc=True)
        p.w("vector", t)
        t = p.emit("vector", lambda e, bi=bi: e.tensor_scalar(
            lsum[bi][:, 1:2], lsum[bi][:, 1:2], c125, None, ALU.mult), inc=True)
        p.w("scalar", t)
        te = p.emit("scalar", lambda e, W=W, bi=bi: e.activation(
            arow[bi][:, 0:W], arow[bi][:, 0:W], AF.Exp, bias=lsum[bi][:, 1:2],
            scale=c125, accum_out=lsum[bi][:, 0:1]), inc=True)
        p.w("vector", te)
        tl = p.emit("vector", lambda e, bi=bi: e.reciprocal(
            lsum[bi][:, 2:3], lsum[bi][:, 0:1]), inc=True)
        return te, tl

    def emit_av(h2, qt, bi, te, tl):
        vp = qkv_cur[2]
        hb = h2 * 64
        ob = p.bank7()
        nkt = qt + 1
        n_av = 0
        for g0 in range(0, nkt, 4):
            gn = min(4, nkt - g0)
            tb = p.bank()
            p.w("tensor", te)
            for i in range(gn):
                tpe = p.emit("tensor", lambda e, tb=tb, g0=g0, i=i, bi=bi:
                             e.transpose(
                                 p.banks[tb][:, i * 128:(i + 1) * 128],
                                 arow[bi][:, (g0 + i) * 128:(g0 + i + 1) * 128],
                                 ident), inc=(i == gn - 1))
            p.w("vector", tpe)
            tev = p.emit("vector", lambda e, tb=tb, gn=gn: e.tensor_copy(
                at4[:, 0:gn * 128], p.banks[tb][:, 0:gn * 128]), inc=True)
            p.bank_done(tb, tev)
            p.w("tensor", tev)
            for i in range(gn):
                p.emit("tensor", lambda e, ob=ob, g0=g0, i=i, hb=hb,
                       first=(n_av == 0 and i == 0):
                       e.matmul(p.banks[ob][:, 0:64],
                                at4[:, i * 128:(i + 1) * 128],
                                vp[:, g0 + i, hb:hb + 64],
                                start=first, stop=False,
                                skip_group_check=True),
                       inc=(i == gn - 1))
            n_av += gn
        tpe = p.t("qt")
        p.w("vector", tpe)
        p.w("vector", tl)
        tev = p.emit("vector", lambda e, ob=ob, qt=qt, hb=hb, bi=bi: e.tensor_scalar(
            op[:, qt, hb:hb + 64], p.banks[ob][:, 0:64], lsum[bi][:, 2:3], None,
            ALU.mult), inc=True)
        p.bank_done(ob, tev)

    use_j = {}
    ld_j = {}
    qkv_cur = [None, None, None]

    def ld_pair(j, pi):
        p.w("sync", use_j.get(pi))
        t1 = dma_in(p, qp2[pi], QT[j])
        t2 = dma_in(p, kp2[pi], KT[j])
        t3 = dma_in(p, vp2[pi], Vtok[:, j * 128:(j + 1) * 128].rearrange(
            "(c p) f -> p c f", p=128))
        ld_j[pi] = (t1, t2, t3)

    p.barrier()
    ld_pair(0, 0)
    ost = None
    for j in range(NCH):
        pi = j % 2
        if j + 1 < NCH:
            ld_pair(j + 1, 1 - pi)
        qkv_cur[0], qkv_cur[1], qkv_cur[2] = qp2[pi], kp2[pi], vp2[pi]
        for tt in ld_j[pi]:
            p.w("tensor", tt)
        p.w("vector", ost)
        jobs = [(h2, qt) for h2 in range(2) for qt in range(nq)]
        pend = None
        for ji, (h2, qt) in enumerate(jobs):
            bi = ji % 2
            tev = emit_qk(h2, qt, bi)
            if pend is not None:
                emit_av(*pend)
            te, tl = emit_softmax(qt, bi, tev)
            pend = (h2, qt, bi, te, tl)
        emit_av(*pend)
        use_j[pi] = p.t("qt")
        p.w("sync", p.t("qv"))
        ost = dma_out(p, Otok[:, j * 128:(j + 1) * 128].rearrange(
            "(c p) f -> p c f", p=128), op)
    p.barrier()
    p.barrier()
    p.pop_scope()


def emit_dec_tail(p, pfx, S):
    nc = p.nc
    p.push_scope()
    ntt = S // TT
    FB = 512
    NFB = 4096 // FB

    D = lambda n, s, k: p.dram(pfx + n, s, k)
    hs = D("hs", [NCH, 128, S], "ExternalInput")
    Otok = D("Otok", [S, C], "ExternalInput")
    wo = D("wo", [C, C], "ExternalInput")
    wg = D("wg", [C, 4 * C], "ExternalInput")
    wu = D("wu", [C, 4 * C], "ExternalInput")
    wd = D("wd", [4 * C, C], "ExternalInput")
    vec = D("vec", [128, NCH], "ExternalInput")
    cst = D("cst", [128, 8], "ExternalInput")
    idc = D("identity", [128, 128], "ExternalInput")
    proc = D("proc", [NCH, 128, S], "ExternalOutput")
    x2_d = D("x2_d", [NCH, 128, S], "Internal")

    A = lambda n, s, dt=F32: p.sbuf(pfx + n, s, dt)
    ident = A("ident", [128, 128])
    ones_col = A("ones_col", [128, 1])
    ones_row = A("ones_row", [1, 128])
    sb_vec = A("sb_vec", [128, NCH])
    sb_cst = A("sb_cst", [128, 8])
    xa = A("xa", [128, NCH, TT])
    xb = A("xb", [128, NCH, TT])
    xsq = A("xsq", [128, NCH, TT])
    osb = A("osb", [128, NCH, TT])
    gsb = A("gsb", [128, 4, TT])
    husb = A("husb", [128, 4, TT])
    a_bc = A("a_bc", [128, TT])
    b_bc = A("b_bc", [128, TT])
    rows = A("rows", [1, 2048])
    inner = ExitStack()
    Ai = lambda n, s, dt=F32: inner.enter_context(
        nc.sbuf_tensor(pfx + n, s, dt)).ap()
    wosb = Ai("wosb", [128, NCH, C])

    dma_in(p, ident, idc)
    dma_in(p, sb_vec, vec)
    dma_in(p, sb_cst, cst)
    dma_in(p, wosb, wo.rearrange("(a p) b -> p a b", p=128))
    p.emit("vector", lambda e: e.memset(ones_col, 1.0), inc=True)
    p.emit("vector", lambda e: e.memset(ones_row, 1.0), inc=True)
    p.barrier()
    eps_rms = sb_cst[:, 1:2]

    otile = Ai("otile", [128, C])
    for it in range(ntt):
        t0 = it * TT
        dma_in(p, xa, hs[:, :, t0:t0 + TT].rearrange("c p t -> p c t"))
        for st in range(TT // 128):
            p.w("sync", p.t("qt"))
            tld = dma_in(p, otile, Otok[t0 + st * 128:t0 + st * 128 + 128, :])
            p.w("tensor", tld)
            for j in range(NCH):
                b = p.bank()
                tpe = p.emit("tensor", lambda e, b=b, j=j: e.transpose(
                    p.banks[b][:, 0:128], otile[:, j * 128:(j + 1) * 128], ident),
                    inc=True)
                p.w("vector", tpe)
                tev = p.emit("vector", lambda e, b=b, j=j, st=st: e.tensor_copy(
                    xb[:, j, st * 128:(st + 1) * 128], p.banks[b][:, 0:128]), inc=True)
                p.bank_done(b, tev)
        p.barrier(("tensor",))

        def evac(m, b, pet):
            p.w("vector", pet)
            return p.emit("vector", lambda e, m=m, b=b: e.tensor_tensor(
                osb[:, m, :], p.banks[b][:, 0:TT], xa[:, m, :], ALU.add), inc=True)
        proj_fm(p, evac, xb, wosb, TT)
        p.barrier(("sync",))
        dma_out(p, x2_d[:, :, t0:t0 + TT].rearrange("c p t -> p c t"), osb)
        p.barrier()

    inner.close()
    wgb = [A(f"wgb{i}", [128, NCH, FB]) for i in range(2)]
    wub = [A(f"wub{i}", [128, NCH, FB]) for i in range(2)]
    wdb = [A(f"wdb{i}", [128, FB // 128, C]) for i in range(2)]
    wuse = {}
    wld = {}

    def ld_fb(fb, wi):
        f0 = fb * FB
        p.w("sync", wuse.get(wi))
        t1 = dma_in(p, wgb[wi], wg[:, f0:f0 + FB].rearrange("(a p) b -> p a b",
                                                            p=128))
        t2 = dma_in(p, wub[wi], wu[:, f0:f0 + FB].rearrange("(a p) b -> p a b",
                                                            p=128))
        t3 = dma_in(p, wdb[wi], wd[f0:f0 + FB, :].rearrange("(a p) b -> p a b",
                                                            p=128))
        wld[wi] = (t1, t2, t3)

    for it in range(ntt):
        t0 = it * TT
        dma_in(p, xa, x2_d[:, :, t0:t0 + TT].rearrange("c p t -> p c t"))
        p.barrier()
        ln_tile(p, xa, TT, xsq, a_bc, b_bc, rows, ones_col, ones_row, eps_rms,
                "rms", xb, w_col=sb_vec)
        p.barrier()
        ld_fb(0, 0)
        for fb in range(NFB):
            wi = fb % 2
            if fb + 1 < NFB:
                ld_fb(fb + 1, 1 - wi)
            for tt_ in wld[wi]:
                p.w("tensor", tt_)
            for m in range(FB // 128):
                b = p.bank()
                for kc in range(NCH):
                    tpe = p.emit("tensor", lambda e, b=b, m=m, kc=kc, wi=wi: e.matmul(
                        p.banks[b][:, 0:TT], wgb[wi][:, kc, m * 128:(m + 1) * 128],
                        xb[:, kc, 0:TT], start=(kc == 0), stop=(kc == NCH - 1)),
                        inc=(kc == NCH - 1))
                p.w("scalar", tpe)
                tev = p.emit("scalar", lambda e, b=b, m=m: e.activation(
                    gsb[:, m, :], p.banks[b][:, 0:TT], AF.Silu), inc=True)
                p.bank_done(b, tev)
            for m in range(FB // 128):
                b = p.bank()
                for kc in range(NCH):
                    tpe = p.emit("tensor", lambda e, b=b, m=m, kc=kc, wi=wi: e.matmul(
                        p.banks[b][:, 0:TT], wub[wi][:, kc, m * 128:(m + 1) * 128],
                        xb[:, kc, 0:TT], start=(kc == 0), stop=(kc == NCH - 1)),
                        inc=(kc == NCH - 1))
                p.w("vector", tpe)
                p.w("vector", p.t("qs"))
                tev = p.emit("vector", lambda e, b=b, m=m: e.tensor_tensor(
                    husb[:, m, :], p.banks[b][:, 0:TT], gsb[:, m, :], ALU.mult),
                    inc=True)
                p.bank_done(b, tev)
            p.barrier(("tensor",))
            for m in range(NCH):
                b = p.bank()
                for kf in range(FB // 128):
                    tpe = p.emit("tensor", lambda e, b=b, m=m, kf=kf, wi=wi: e.matmul(
                        p.banks[b][:, 0:TT], wdb[wi][:, kf, m * 128:(m + 1) * 128],
                        husb[:, kf, 0:TT], start=(kf == 0), stop=(kf == FB // 128 - 1)),
                        inc=(kf == FB // 128 - 1))
                p.w("vector", tpe)
                p.w("vector", p.t("qv"))
                if fb == 0:
                    tev = p.emit("vector", lambda e, b=b, m=m: e.tensor_tensor(
                        osb[:, m, :], p.banks[b][:, 0:TT], xa[:, m, :], ALU.add),
                        inc=True)
                else:
                    tev = p.emit("vector", lambda e, b=b, m=m: e.tensor_tensor(
                        osb[:, m, :], p.banks[b][:, 0:TT], osb[:, m, :], ALU.add),
                        inc=True)
                p.bank_done(b, tev)
            wuse[wi] = p.t("qt")
        p.barrier(("sync",))
        dma_out(p, proc[:, :, t0:t0 + TT].rearrange("c p t -> p c t"), osb)
        p.barrier()
    p.barrier()
    p.pop_scope()


def emit_stage3(p, Tl=T):
    nc = p.nc
    p.push_scope()
    ntt = Tl // TT

    D = lambda n, s, k: p.dram("s3_" + n, s, k)
    xn0 = D("xn0", [NCH, 128, Tl], "ExternalInput")
    proc = D("proc", [NCH, 128, Tl], "ExternalInput")
    pw = D("pw", [C, 2 * C], "ExternalInput")
    vec = D("vec", [128, NCH], "ExternalInput")
    cst = D("cst", [128, 8], "ExternalInput")
    idc = D("identity", [128, 128], "ExternalInput")
    dst_d = D("dst", [1, Tl], "ExternalOutput")
    dch_d = D("dch", [1, Tl], "ExternalOutput")
    parts = D("parts", [1, 2], "ExternalOutput")
    ptok = D("ptok", [Tl, C], "ExternalOutput")
    f0tok = D("f0tok", [Tl, C], "ExternalOutput")

    A = lambda n, s, dt=F32: p.sbuf("s3_" + n, s, dt)
    ident = A("ident", [128, 128])
    ones_col = A("ones_col", [128, 1])
    ones_row = A("ones_row", [1, 128])
    sb_vec = A("sb_vec", [128, NCH])
    sb_cst = A("sb_cst", [128, 8])
    pwsb = A("pwsb", [128, NCH, 2 * C])
    xa = A("xa", [128, NCH, TT])
    xb = A("xb", [128, NCH, TT])
    xc = A("xc", [128, NCH, TT])
    xd = A("xd", [128, NCH, TT])
    xsq = A("xsq", [128, NCH, TT])
    musb = A("musb", [128, NCH, TT])
    lvsb = A("lvsb", [128, NCH, TT])
    otile = A("otile", [128, C])
    a_bc = A("a_bc", [128, TT])
    b_bc = A("b_bc", [128, TT])
    rows = A("rows", [1, 2048])
    racc = A("racc", [1, 4])

    dma_in(p, ident, idc)
    dma_in(p, sb_vec, vec)
    dma_in(p, sb_cst, cst)
    dma_in(p, pwsb, pw.rearrange("(a p) b -> p a b", p=128))
    p.emit("vector", lambda e: e.memset(ones_col, 1.0), inc=True)
    p.emit("vector", lambda e: e.memset(ones_row, 1.0), inc=True)
    p.emit("vector", lambda e: e.memset(racc, 0.0), inc=True)
    p.barrier()
    eps_rms = sb_cst[:, 1:2]
    c_st = sb_cst[:, 3:4]
    c_ch = sb_cst[:, 4:5]
    c_nh = sb_cst[:, 5:6]

    for it in range(ntt):
        t0 = it * TT
        dma_in(p, xa, xn0[:, :, t0:t0 + TT].rearrange("c p t -> p c t"))
        dma_in(p, xb, proc[:, :, t0:t0 + TT].rearrange("c p t -> p c t"))
        p.barrier()
        for st in range(TT // 128):
            for j in range(NCH):
                b = p.bank()
                tpe = p.emit("tensor", lambda e, b=b, j=j, st=st: e.transpose(
                    p.banks[b][:, 0:128], xb[:, j, st * 128:(st + 1) * 128], ident),
                    inc=True)
                p.w("vector", tpe)
                tev = p.emit("vector", lambda e, b=b, j=j: e.tensor_copy(
                    otile[:, j * 128:(j + 1) * 128], p.banks[b][:, 0:128]), inc=True)
                p.bank_done(b, tev)
            p.barrier(("sync",))
            t1 = dma_out(p, ptok[t0 + st * 128:t0 + st * 128 + 128, :], otile)
            t2 = dma_out(p, f0tok[t0 + st * 128:t0 + st * 128 + 128, :], otile)
            p.w("vector", t1)
            p.w("vector", t2)
        ln_tile(p, xa, TT, xsq, a_bc, b_bc, rows, ones_col, ones_row, eps_rms,
                "rms", xc, w_col=sb_vec)
        p.barrier(("tensor",))
        for m in range(16):
            b = p.bank()
            for kc in range(NCH):
                tpe = p.emit("tensor", lambda e, b=b, m=m, kc=kc: e.matmul(
                    p.banks[b][:, 0:TT], pwsb[:, kc, m * 128:(m + 1) * 128],
                    xc[:, kc, 0:TT], start=(kc == 0), stop=(kc == NCH - 1)),
                    inc=(kc == NCH - 1))
            p.w("vector", tpe)
            dstt = musb if m < 8 else lvsb
            tev = p.emit("vector", lambda e, b=b, m=m, dstt=dstt: e.tensor_copy(
                dstt[:, m % 8, :], p.banks[b][:, 0:TT]), inc=True)
            p.bank_done(b, tev)
        for kc in range(NCH):
            p.emit("vector", lambda e, kc=kc: e.tensor_tensor(
                xd[:, kc, :], xb[:, kc, :], xa[:, kc, :], ALU.subtract), inc=True)
        tres = p.t("qv")
        p.w("scalar", tres)
        t = p.emit("scalar", lambda e: e.activation(xsq, xd, AF.Square), inc=True)
        b = p.bank()
        p.w("tensor", t)
        for kc in range(NCH):
            tpe = p.emit("tensor", lambda e, b=b, kc=kc: e.matmul(
                p.banks[b][0:1, 0:TT], ones_col, xsq[:, kc, 0:TT],
                start=(kc == 0), stop=(kc == NCH - 1)), inc=(kc == NCH - 1))
        p.w("vector", tpe)
        trow = p.emit("vector", lambda e, b=b: e.tensor_scalar(
            rows[0:1, 0:TT], p.banks[b][0:1, 0:TT], 1.0 / 1024.0, None, ALU.mult),
            inc=True)
        p.bank_done(b, trow)
        p.w("sync", trow)
        dma_out(p, dst_d[:, t0:t0 + TT], rows[0:1, 0:TT])
        p.w("vector", trow)
        t = p.emit("vector", lambda e: e.reduce_sum(rows[0:1, 1024:1025],
                                                    rows[0:1, 0:TT], axis=AX.X),
                   inc=True)
        p.w("vector", t)
        t = p.emit("vector", lambda e: e.tensor_tensor(
            racc[0:1, 0:1], racc[0:1, 0:1], rows[0:1, 1024:1025], ALU.add), inc=True)
        for kc in range(NCH):
            tt1 = p.emit("vector", lambda e, kc=kc: e.tensor_tensor(
                musb[:, kc, :], xd[:, kc, :], musb[:, kc, :], ALU.subtract), inc=True)
        tdm = p.t("qv")
        p.w("scalar", tdm)
        t = p.emit("scalar", lambda e: e.activation(xsq, musb, AF.Square), inc=True)
        p.w("scalar", t)
        t = p.emit("scalar", lambda e: e.activation(musb, lvsb, AF.Copy, scale=-1.0),
                   inc=True)
        p.w("scalar", t)
        t = p.emit("scalar", lambda e: e.activation(musb, musb, AF.Exp), inc=True)
        p.w("vector", t)
        for kc in range(NCH):
            t = p.emit("vector", lambda e, kc=kc: e.tensor_scalar(
                xsq[:, kc, :], xsq[:, kc, :], 1.0, None, ALU.add), inc=True)
            p.w("vector", t)
            t = p.emit("vector", lambda e, kc=kc: e.tensor_tensor(
                xsq[:, kc, :], xsq[:, kc, :], musb[:, kc, :], ALU.mult), inc=True)
            p.w("vector", t)
            t = p.emit("vector", lambda e, kc=kc: e.tensor_tensor(
                xsq[:, kc, :], xsq[:, kc, :], lvsb[:, kc, :], ALU.add), inc=True)
        tch = p.t("qv")
        b = p.bank()
        p.w("tensor", tch)
        for kc in range(NCH):
            tpe = p.emit("tensor", lambda e, b=b, kc=kc: e.matmul(
                p.banks[b][0:1, 0:TT], ones_col, xsq[:, kc, 0:TT],
                start=(kc == 0), stop=(kc == NCH - 1)), inc=(kc == NCH - 1))
        p.w("vector", tpe)
        trow = p.emit("vector", lambda e, b=b: e.tensor_scalar(
            rows[0:1, 512:512 + TT], p.banks[b][0:1, 0:TT], 0.5 / 1024.0, -0.5,
            ALU.mult, ALU.add), inc=True)
        p.bank_done(b, trow)
        p.w("sync", trow)
        dma_out(p, dch_d[:, t0:t0 + TT], rows[0:1, 512:512 + TT])
        p.w("vector", trow)
        t = p.emit("vector", lambda e: e.reduce_sum(rows[0:1, 1030:1031],
                                                    rows[0:1, 512:512 + TT],
                                                    axis=AX.X), inc=True)
        p.w("vector", t)
        t = p.emit("vector", lambda e: e.tensor_tensor(
            racc[0:1, 1:2], racc[0:1, 1:2], rows[0:1, 1030:1031], ALU.add), inc=True)
        p.barrier()
    p.w("sync", p.t("qv"))
    dma_out(p, parts, racc[0:1, 0:2])
    p.barrier()
    p.pop_scope()


def emit_stage4(p, Tl=T):
    nc = p.nc
    p.push_scope()
    K = Tl // 4
    NT = Tl // 128
    KC = K // 128

    D = lambda n, s, k, dt=F32: p.dram("s4_" + n, s, k, dt)
    dst_d = D("dst", [1, Tl], "ExternalInput")
    dch_d = D("dch", [1, Tl], "ExternalInput")
    if p.fused:
        parts = p.dram("G_ccdst", [128, 8], "Internal")
    else:
        parts = D("parts", [NCORES, 2], "ExternalInput")
    ptok = D("ptok", [Tl, C], "ExternalInput")
    cos_d = D("cos", [Tl, 32], "ExternalInput")
    sin_d = D("sin", [Tl, 32], "ExternalInput")
    iosb_d = D("iota", [128, Tl], "ExternalInput")
    ivals_d = D("ivals", [128, NT], "ExternalInput")
    prem_d = D("prem", [128, 128], "ExternalInput")
    cst = D("cst", [128, 8], "ExternalInput")
    idc = D("identity", [128, 128], "ExternalInput")

    g2_d = D("g2", [128, NT], "Internal")
    isc_d = D("isc", [K, 8], "ExternalOutput")
    seltok = D("seltok", [K, C], "ExternalOutput")
    selfm = D("selfm", [NCH, 128, K], "ExternalOutput")
    cossel = D("cossel", [K, 32], "ExternalOutput")
    sinsel = D("sinsel", [K, 32], "ExternalOutput")

    A = lambda n, s, dt=F32: p.sbuf("s4_" + n, s, dt)
    ident = A("ident", [128, 128])
    ones_col = A("ones_col", [128, 1])
    ones_row = A("ones_row", [1, 128])
    sb_cst = A("sb_cst", [128, 8])
    grow = A("grow", [1, Tl])
    drow = A("drow", [1, Tl])
    gbc = A("gbc", [128, Tl])
    iobc = A("iobc", [128, Tl])
    ivals = A("ivals_sb", [128, NT])
    prem = A("prem_sb", [128, 128])
    gcol = A("gcol", [128, NT])
    rank = A("rank", [128, NT])
    sel = A("sel", [128, NT])
    slot = A("slot", [128, NT])
    s2 = A("s2", [128, NT])
    tmp1 = A("tmp1", [128, Tl])
    tmp2 = A("tmp2", [128, Tl])
    tmp3 = A("tmp3", [128, Tl])
    psb = A("psb", [NCORES, 2])
    mrow = A("mrow", [1, 2])
    u32c = A("u32c", [128, NT], U32)
    idxu = A("idxu", [128, KC], U32)
    combo = A("combo", [128, NT, 8])
    iscsb = A("iscsb", [128, KC, 8])
    stile = A("stile", [128, C])
    sfmt = A("sfmt", [128, NCH, 128])
    cstile = A("cstile", [128, 64])

    dma_in(p, ident, idc)
    dma_in(p, sb_cst, cst)
    dma_in(p, grow, dst_d)
    dma_in(p, drow, dch_d)
    if p.fused:
        dma_in(p, mrow, parts[0:1, 0:2])
    else:
        dma_in(p, psb, parts)
    dma_in(p, iobc, iosb_d)
    dma_in(p, ivals, ivals_d)
    dma_in(p, prem, prem_d)
    p.emit("vector", lambda e: e.memset(ones_col, 1.0), inc=True)
    p.emit("vector", lambda e: e.memset(ones_row, 1.0), inc=True)
    p.barrier()
    cK = sb_cst[:, 6:7]
    cBIG = sb_cst[:, 7:8]

    if p.fused:
        t = p.emit("vector", lambda e: e.tensor_scalar(
            mrow, mrow, 1.0 / (NCORES * Tl), None, ALU.mult), inc=True)
        p.w("vector", t)
    else:
        b = p.bank()
        t = p.emit("tensor", lambda e, b=b: e.matmul(p.banks[b][0:1, 0:2],
                                                     ones_col[0:8, :], psb,
                                                     start=True, stop=True), inc=True)
        p.w("vector", t)
        t = p.emit("vector", lambda e, b=b: e.tensor_scalar(
            mrow, p.banks[b][0:1, 0:2], 1.0 / (NCORES * Tl), None, ALU.mult),
            inc=True)
        p.bank_done(b, t)
        p.w("vector", t)
    t = p.emit("vector", lambda e: e.tensor_tensor(
        mrow[0:1, 0:1], mrow[0:1, 0:1], mrow[0:1, 1:2], ALU.add), inc=True)
    p.w("vector", t)
    t = p.emit("vector", lambda e: e.tensor_tensor(grow, grow, drow, ALU.add),
               inc=True)
    p.w("vector", t)
    t = p.emit("vector", lambda e: e.tensor_scalar(
        grow, grow, mrow[0:1, 0:1], None, ALU.subtract), inc=True)
    p.w("scalar", t)
    t = p.emit("scalar", lambda e: e.activation(grow, grow, AF.Sigmoid), inc=True)
    p.w("sync", t)
    tg = dma_out(p, g2_d.rearrange("p t -> (p t)").rearrange("(o f) -> o f", o=1), grow)
    p.w("sync", tg)
    tgl = dma_in(p, gcol, g2_d)
    p.w("tensor", t)
    for s0 in range(0, Tl, 512):
        b = p.bank()
        tpe = p.emit("tensor", lambda e, b=b, s0=s0: e.matmul(
            p.banks[b][:, 0:512], ones_row, grow[0:1, s0:s0 + 512],
            start=True, stop=True), inc=True)
        p.w("vector", tpe)
        tev = p.emit("vector", lambda e, b=b, s0=s0: e.tensor_copy(
            gbc[:, s0:s0 + 512], p.banks[b][:, 0:512]), inc=True)
        p.bank_done(b, tev)
    p.w("vector", tgl)
    p.barrier(("vector",))
    for tc in range(NT):
        t1 = p.emit("vector", lambda e, tc=tc: e.tensor_scalar(
            tmp1, gbc, gcol[:, tc:tc + 1], None, ALU.is_gt), inc=True)
        t2 = p.emit("vector", lambda e, tc=tc: e.tensor_scalar(
            tmp2, gbc, gcol[:, tc:tc + 1], None, ALU.is_equal), inc=True)
        t3 = p.emit("vector", lambda e, tc=tc: e.tensor_scalar(
            tmp3, iobc, ivals[:, tc:tc + 1], None, ALU.is_lt), inc=True)
        p.w("vector", t3)
        t4 = p.emit("vector", lambda e: e.tensor_tensor(tmp2, tmp2, tmp3, ALU.mult),
                    inc=True)
        p.w("vector", t4)
        t5 = p.emit("vector", lambda e: e.tensor_tensor(tmp1, tmp1, tmp2, ALU.add),
                    inc=True)
        p.w("vector", t5)
        t6 = p.emit("vector", lambda e, tc=tc: e.reduce_sum(
            rank[:, tc:tc + 1], tmp1, axis=AX.X), inc=True)
        p.w("vector", t6)
    t = p.emit("vector", lambda e: e.tensor_scalar(sel, rank, cK, None, ALU.is_lt),
               inc=True)
    p.w("vector", t)
    t = p.emit("vector", lambda e: e.tensor_tensor_scan(
        slot, ones_col.to_broadcast([128, NT]), sel, 0.0, ALU.mult, ALU.add), inc=True)
    p.w("vector", t)
    p.w("tensor", t)
    b = p.bank()
    tpe = p.emit("tensor", lambda e, b=b: e.matmul(
        p.banks[b][:, 0:1], prem, slot[:, NT - 1:NT], start=True, stop=True), inc=True)
    p.w("vector", tpe)
    toffs = p.emit("vector", lambda e, b=b: e.tensor_copy(
        s2[:, 0:1], p.banks[b][:, 0:1]), inc=True)
    p.bank_done(b, toffs)
    p.w("vector", toffs)
    t = p.emit("vector", lambda e: e.tensor_tensor(slot, slot, sel, ALU.subtract),
               inc=True)
    p.w("vector", t)
    t = p.emit("vector", lambda e: e.tensor_scalar(
        slot, slot, s2[:, 0:1], None, ALU.add), inc=True)
    p.w("vector", t)
    t = p.emit("vector", lambda e: e.tensor_scalar(
        tmp1[:, 0:NT], slot, 128.0, None, ALU.is_ge), inc=True)
    for j in range(2, KC):
        p.w("vector", t)
        t = p.emit("vector", lambda e, j=j: e.tensor_scalar(
            tmp2[:, 0:NT], slot, float(j) * 128.0, None, ALU.is_ge), inc=True)
        p.w("vector", t)
        t = p.emit("vector", lambda e: e.tensor_tensor(
            tmp1[:, 0:NT], tmp1[:, 0:NT], tmp2[:, 0:NT], ALU.add), inc=True)
    p.w("vector", t)
    t = p.emit("vector", lambda e: e.scalar_tensor_tensor(
        s2, tmp1[:, 0:NT], -128.0, slot, ALU.mult, ALU.add), inc=True)
    p.w("vector", t)
    t = p.emit("vector", lambda e: e.scalar_tensor_tensor(
        s2, s2, float(KC), tmp1[:, 0:NT], ALU.mult, ALU.add), inc=True)
    p.w("vector", t)
    t = p.emit("vector", lambda e: e.tensor_scalar(
        tmp1[:, 0:NT], sel, -1.0, 1.0, ALU.mult, ALU.add), inc=True)
    p.w("vector", t)
    t = p.emit("vector", lambda e: e.tensor_scalar(
        tmp1[:, 0:NT], tmp1[:, 0:NT], cBIG, None, ALU.mult), inc=True)
    p.w("vector", t)
    t = p.emit("vector", lambda e: e.tensor_tensor(s2, s2, sel, ALU.mult), inc=True)
    p.w("vector", t)
    t = p.emit("vector", lambda e: e.tensor_tensor(s2, s2, tmp1[:, 0:NT], ALU.add),
               inc=True)
    p.w("vector", t)
    tu = p.emit("vector", lambda e: e.tensor_copy(u32c, s2), inc=True)
    p.w("gpsimd", tu)
    t = p.emit("vector", lambda e: e.memset(combo, 0.0), inc=True)
    p.w("vector", t)
    t = p.emit("vector", lambda e: e.tensor_copy(combo[:, :, 0:1],
                                                 ivals.rearrange(
                                                     "p (t o) -> p t o", o=1)),
               inc=True)
    p.w("vector", t)
    t = p.emit("vector", lambda e: e.tensor_copy(combo[:, :, 1:2],
                                                 gcol.rearrange(
                                                     "p (t o) -> p t o", o=1)),
               inc=True)
    p.w("gpsimd", t)
    tprev = None
    for tc in range(NT):
        p.w("gpsimd", tprev)
        tprev = p.dma("gpsimd", lambda e, tc=tc: e.indirect_dma_start(
            out=isc_d, out_offset=bass.IndirectOffsetOnAxis(u32c[:, tc:tc + 1], 0),
            in_=combo[:, tc, :], in_offset=None,
            bounds_check=K - 1, oob_is_err=False))
    p.barrier()
    dma_in(p, iscsb, isc_d.rearrange("(p c) e -> p c e", p=128))
    p.barrier(("vector",))
    tu = p.emit("vector", lambda e: e.tensor_copy(
        idxu, iscsb[:, :, 0:1].rearrange("p c o -> p (c o)")), inc=True)
    p.w("gpsimd", tu)
    for c2 in range(KC):
        tg = p.dma("gpsimd", lambda e, c2=c2: e.indirect_dma_start(
            out=stile, out_offset=None, in_=ptok,
            in_offset=bass.IndirectOffsetOnAxis(idxu[:, c2:c2 + 1], 0)))
        p.w("sync", tg)
        dma_out(p, seltok[c2 * 128:(c2 + 1) * 128, :], stile)
        p.w("tensor", tg)
        for j in range(NCH):
            b = p.bank()
            tpe = p.emit("tensor", lambda e, b=b, j=j: e.transpose(
                p.banks[b][:, 0:128], stile[:, j * 128:(j + 1) * 128], ident),
                inc=True)
            p.w("vector", tpe)
            tev = p.emit("vector", lambda e, b=b, j=j: e.tensor_copy(
                sfmt[:, j, :], p.banks[b][:, 0:128]), inc=True)
            p.bank_done(b, tev)
        p.barrier(("sync",))
        dma_out(p, selfm[:, :, c2 * 128:(c2 + 1) * 128].rearrange("c p t -> p c t"),
                sfmt)
        tg = p.dma("gpsimd", lambda e, c2=c2: e.indirect_dma_start(
            out=cstile[:, 0:32], out_offset=None, in_=cos_d,
            in_offset=bass.IndirectOffsetOnAxis(idxu[:, c2:c2 + 1], 0)))
        tg2 = p.dma("gpsimd", lambda e, c2=c2: e.indirect_dma_start(
            out=cstile[:, 32:64], out_offset=None, in_=sin_d,
            in_offset=bass.IndirectOffsetOnAxis(idxu[:, c2:c2 + 1], 0)))
        p.w("sync", tg)
        p.w("sync", tg2)
        dma_out(p, cossel[c2 * 128:(c2 + 1) * 128, :], cstile[:, 0:32])
        tls = dma_out(p, sinsel[c2 * 128:(c2 + 1) * 128, :], cstile[:, 32:64])
        p.w("gpsimd", tls)
        p.barrier(("gpsimd",))
    p.barrier()
    p.pop_scope()


def emit_stage5(p, Tl=T):
    nc = p.nc
    p.push_scope()
    K = Tl // 4
    KC = K // 128

    D = lambda n, s, k, dt=F32: p.dram("s5_" + n, s, k, dt)
    proc2 = D("proc2", [NCH, 128, K], "ExternalInput")
    seltok = D("seltok", [K, C], "ExternalInput")
    isc_d = D("isc", [K, 8], "ExternalInput")
    f0tok = D("f0tok", [Tl, C], "ExternalInput")
    x0 = D("x0", [NCH, 128, Tl], "ExternalInput")
    a0 = D("a0", [NCH, 128, Tl], "ExternalInput")
    idc = D("identity", [128, 128], "ExternalInput")
    x1_d = D("x1", [NCH, 128, Tl], "ExternalOutput")
    f0w = D("f0w", [Tl, C], "Internal")

    A = lambda n, s, dt=F32: p.sbuf("s5_" + n, s, dt)
    ident = A("ident", [128, 128])
    ytile = A("ytile", [128, C])
    stile = A("stile", [128, C])
    ptile = A("ptile", [128, NCH, 128])
    iscsb = A("iscsb", [128, KC, 8])
    idxu = A("idxu", [128, KC], U32)
    xa = A("xa", [128, NCH, TT])
    xb = A("xb", [128, NCH, TT])
    xc = A("xc", [128, NCH, TT])

    dma_in(p, ident, idc)
    dma_in(p, iscsb, isc_d.rearrange("(p c) e -> p c e", p=128))
    p.barrier()
    tu = p.emit("vector", lambda e: e.tensor_copy(
        idxu, iscsb[:, :, 0:1].rearrange("p c o -> p (c o)")), inc=True)
    Q4 = Tl // 4
    for c2 in range(4):
        p.dma("sync", lambda e, c2=c2: e.dma_start(
            out=f0w[c2 * Q4:(c2 + 1) * Q4, :], in_=f0tok[c2 * Q4:(c2 + 1) * Q4, :]))
    p.barrier()
    for c2 in range(KC):
        dma_in(p, ptile, proc2[:, :, c2 * 128:(c2 + 1) * 128].rearrange(
            "c p t -> p c t"))
        tld = dma_in(p, stile, seltok[c2 * 128:(c2 + 1) * 128, :])
        p.barrier(("tensor", "vector"))
        for j in range(NCH):
            b = p.bank()
            tpe = p.emit("tensor", lambda e, b=b, j=j: e.transpose(
                p.banks[b][:, 0:128], ptile[:, j, :], ident), inc=True)
            p.w("vector", tpe)
            tev = p.emit("vector", lambda e, b=b, j=j: e.tensor_copy(
                ytile[:, j * 128:(j + 1) * 128], p.banks[b][:, 0:128]), inc=True)
            p.bank_done(b, tev)
        tev = p.t("qv")
        p.w("vector", tev)
        t = p.emit("vector", lambda e: e.tensor_tensor(ytile, ytile, stile,
                                                       ALU.subtract), inc=True)
        p.w("vector", t)
        t = p.emit("vector", lambda e, c2=c2: e.tensor_scalar(
            ytile, ytile, iscsb[:, c2, 1:2], None, ALU.mult), inc=True)
        p.w("vector", t)
        t = p.emit("vector", lambda e: e.tensor_tensor(ytile, ytile, stile, ALU.add),
                   inc=True)
        p.w("gpsimd", t)
        tsc = p.dma("gpsimd", lambda e, c2=c2: e.indirect_dma_start(
            out=f0w, out_offset=bass.IndirectOffsetOnAxis(idxu[:, c2:c2 + 1], 0),
            in_=ytile, in_offset=None))
        p.w("sync", tsc)
        p.w("vector", tsc)
        p.barrier(("gpsimd",))
    p.barrier()
    for it in range(Tl // TT):
        t0 = it * TT
        dma_in(p, xa, x0[:, :, t0:t0 + TT].rearrange("c p t -> p c t"))
        dma_in(p, xb, a0[:, :, t0:t0 + TT].rearrange("c p t -> p c t"))
        p.barrier()
        for st in range(TT // 128):
            tld = dma_in(p, ytile, f0w[t0 + st * 128:t0 + st * 128 + 128, :])
            p.w("tensor", tld)
            for j in range(NCH):
                b = p.bank()
                tpe = p.emit("tensor", lambda e, b=b, j=j: e.transpose(
                    p.banks[b][:, 0:128], ytile[:, j * 128:(j + 1) * 128], ident),
                    inc=True)
                p.w("vector", tpe)
                tev = p.emit("vector", lambda e, b=b, j=j, st=st: e.tensor_copy(
                    xc[:, j, st * 128:(st + 1) * 128], p.banks[b][:, 0:128]),
                    inc=True)
                p.bank_done(b, tev)
            p.w("sync", p.t("qt"))
        tev = p.t("qv")
        p.w("vector", tev)
        for kc in range(NCH):
            t = p.emit("vector", lambda e, kc=kc: e.tensor_tensor(
                xa[:, kc, :], xa[:, kc, :], xb[:, kc, :], ALU.add), inc=True)
            p.w("vector", t)
            t = p.emit("vector", lambda e, kc=kc: e.tensor_tensor(
                xa[:, kc, :], xa[:, kc, :], xc[:, kc, :], ALU.add), inc=True)
        p.w("sync", p.t("qv"))
        dma_out(p, x1_d[:, :, t0:t0 + TT].rearrange("c p t -> p c t"), xa)
        p.barrier()
    p.barrier()
    p.pop_scope()


SC = 16


def emit_stage6(p, Tl=T):
    nc = p.nc
    p.push_scope()
    ntt = Tl // TT
    ntt128 = Tl // 128

    D = lambda n, s, k: p.dram("s6_" + n, s, k)
    x1 = D("x1", [NCH, 128, Tl], "ExternalInput")
    Wr = D("Wr", [C, C], "ExternalInput")
    Wk = D("Wk", [C, C], "ExternalInput")
    Wv = D("Wv", [C, C], "ExternalInput")
    Ww = D("Ww", [C, C], "ExternalInput")
    Wg = D("Wg", [C, C], "ExternalInput")
    Wo = D("Wo", [C, C], "ExternalInput")
    vecs = D("vecs", [128, 8 * NCH], "ExternalInput")
    cst = D("cst", [128, 8], "ExternalInput")
    idc = D("identity", [128, 128], "ExternalInput")
    rmask_d = D("rmask", [128, Tl], "ExternalInput")
    bar_d = D("bar", [128, Tl], "ExternalInput")
    bmask_d = D("bmask", [128, 128], "ExternalInput")
    xn1 = D("xn1", [NCH, 128, Tl], "ExternalOutput")
    a1 = D("a1", [NCH, 128, Tl], "ExternalOutput")
    xl_d = D("xl_d", [NCH, 128, Tl], "Internal")
    r_d = D("r_d", [NCH, 128, Tl], "Internal")
    k_d = D("k_d", [NCH, 128, Tl], "Internal")
    v_d = D("v_d", [NCH, 128, Tl], "Internal")
    w_d = D("w_d", [NCH, 128, Tl], "Internal")
    g_d = D("g_d", [NCH, 128, Tl], "Internal")
    e_d = D("e_d", [NCH, 128, Tl], "Internal")
    C_d = D("C_d", [NCH, 128, Tl], "Internal")
    rt_d = D("rt_d", [NCH, 128, Tl], "Internal")
    kh_d = D("kh_d", [NCH, 128, Tl], "Internal")
    dE_d = D("dE_d", [NCH, 128, Tl // SC], "Internal")
    khtok = D("khtok", [Tl, C], "Internal")
    vtok = D("vtok", [Tl, C], "Internal")
    gy_d = D("gy_d", [NCH, 128, Tl], "Internal")

    A = lambda n, s, dt=F32: p.sbuf("s6_" + n, s, dt)
    ident = A("ident", [128, 128])
    ones_col = A("ones_col", [128, 1])
    ones_row = A("ones_row", [1, 128])
    sb_vec = A("sb_vec", [128, 8 * NCH])
    sb_cst = A("sb_cst", [128, 8])
    sigm = A("sigm", [128, 4 * NCH])
    wT = A("wT", [128, NCH, C])
    wnat = A("wnat", [128, NCH, C])
    xa = A("xa", [128, NCH, TT])
    xb = A("xb", [128, NCH, TT])
    xc = A("xc", [128, NCH, TT])
    xm = A("xm", [128, NCH, TT])
    xsq = A("xsq", [128, NCH, TT])
    osb = A("osb", [128, NCH, TT])
    a_bc = A("a_bc", [128, TT])
    b_bc = A("b_bc", [128, TT])
    rows = A("rows", [1, 2048])

    dma_in(p, ident, idc)
    tv = dma_in(p, sb_vec, vecs)
    dma_in(p, sb_cst, cst)
    mixs = [sb_vec[:, i * NCH:(i + 1) * NCH] for i in range(4)]
    tdec = sb_vec[:, 4 * NCH:5 * NCH]
    lnw = sb_vec[:, 5 * NCH:6 * NCH]
    lnb = sb_vec[:, 6 * NCH:7 * NCH]
    ucol = sb_vec[:, 7 * NCH:8 * NCH]
    eps_ln = sb_cst[:, 0:1]
    p.w("scalar", tv)
    p.emit("scalar", lambda e: e.activation(sigm, sb_vec[:, 0:4 * NCH], AF.Sigmoid),
           inc=True)
    p.emit("vector", lambda e: e.memset(ones_col, 1.0), inc=True)
    p.emit("vector", lambda e: e.memset(ones_row, 1.0), inc=True)
    p.barrier()

    for it in range(ntt):
        t0 = it * TT
        dma_in(p, xa, x1[:, :, t0:t0 + TT].rearrange("c p t -> p c t"))
        p.barrier()
        tvv = ln_tile(p, xa, TT, xsq, a_bc, b_bc, rows, ones_col, ones_row, eps_ln,
                      "ln", xb)
        p.w("sync", tvv)
        dma_out(p, xn1[:, :, t0:t0 + TT].rearrange("c p t -> p c t"), xb)
        p.barrier()
        tvv = ln_tile(p, xb, TT, xsq, a_bc, b_bc, rows, ones_col, ones_row, eps_ln,
                      "ln", xc, w_col=lnw, b_col=lnb)
        p.w("sync", tvv)
        dma_out(p, xl_d[:, :, t0:t0 + TT].rearrange("c p t -> p c t"), xc)
        p.barrier()

    inner6 = ExitStack()
    Ai6 = lambda n, sh: inner6.enter_context(nc.sbuf_tensor("s6_" + n, sh, F32)).ap()
    xcB = Ai6("xcB", [128, NCH, TT])
    xmB = Ai6("xmB", [128, NCH, TT])
    osbB = wnat[:, :, 0:TT]
    for (W_dram, mi, dst, sig) in ((Wr, 2, r_d, True), (Wg, 2, g_d, True),
                                   (Wk, 0, k_d, False), (Wv, 1, v_d, False),
                                   (Ww, 3, w_d, False), (Wo, None, None, False)):
        if dst is None:
            break
        mix = sigm[:, mi * NCH:(mi + 1) * NCH]
        load_weight_T(p, W_dram, wT, wnat, ident)
        p.barrier()
        mix_proj_pass(p, xl_d, mix, wT, dst, sig, [xc, xcB], [xm, xmB], xsq,
                      [osb, osbB], Tl)

    inner6.close()
    wc_t = wT.rearrange("p a b -> p (a b)")[:, 0:Tl]
    cc_t = wT.rearrange("p a b -> p (a b)")[:, Tl:2 * Tl] if Tl <= 4096 else None
    rm_t = wnat.rearrange("p a b -> p (a b)")[:, 0:Tl]
    sc_t = wnat.rearrange("p a b -> p (a b)")[:, Tl:2 * Tl]
    for kc in range(NCH):
        dma_in(p, wc_t, w_d[kc])
        dma_in(p, rm_t, rmask_d)
        p.barrier()
        t = p.emit("vector", lambda e, kc=kc: e.tensor_scalar(
            wc_t, wc_t, tdec[:, kc:kc + 1], None, ALU.add), inc=True)
        p.w("vector", t)
        t = p.emit("vector", lambda e: e.tensor_scalar(
            wc_t, wc_t, -10.0, None, ALU.max), inc=True)
        p.w("vector", t)
        t = p.emit("vector", lambda e: e.tensor_scalar(
            wc_t, wc_t, 10.0, None, ALU.min), inc=True)
        p.w("scalar", t)
        t = p.emit("scalar", lambda e: e.activation(sc_t, wc_t, AF.Exp), inc=True)
        p.w("sync", t)
        dma_out(p, e_d[kc], sc_t)
        p.w("vector", t)
        t = p.emit("vector", lambda e: e.tensor_tensor_scan(
            cc_t, rm_t, sc_t, 0.0, ALU.mult, ALU.add), inc=True)
        p.w("sync", t)
        dma_out(p, C_d[kc], cc_t)
        p.barrier()
    for kc in range(NCH):
        dma_in(p, wc_t, C_d[kc])
        dma_in(p, rm_t, e_d[kc])
        dma_in(p, sc_t, r_d[kc])
        p.barrier()
        t = p.emit("vector", lambda e: e.tensor_tensor(rm_t, wc_t, rm_t,
                                                       ALU.subtract), inc=True)
        p.w("scalar", t)
        t = p.emit("scalar", lambda e: e.activation(rm_t, rm_t, AF.Exp, scale=-1.0),
                   inc=True)
        p.w("vector", t)
        tmul = p.emit("vector", lambda e: e.tensor_tensor(sc_t, sc_t, rm_t,
                                                          ALU.mult), inc=True)
        p.w("sync", tmul)
        tso = dma_out(p, rt_d[kc], sc_t)
        cv = wc_t.rearrange("p (s j) -> p s j", j=SC)
        ends = cv[:, :, SC - 1:SC]
        p.w("scalar", tmul)
        t = p.emit("scalar", lambda e: e.activation(
            rm_t[:, 0:Tl // SC], ends.rearrange("p s o -> p (s o)"), AF.Exp,
            scale=-1.0), inc=True)
        p.w("sync", t)
        dma_out(p, dE_d[kc], rm_t[:, 0:Tl // SC])
        p.w("vector", t)
        t = p.emit("vector", lambda e: e.tensor_tensor(
            cv, cv, ends.to_broadcast([128, Tl // SC, SC]), ALU.subtract), inc=True)
        p.w("scalar", t)
        t = p.emit("scalar", lambda e: e.activation(wc_t, wc_t, AF.Exp), inc=True)
        p.w("sync", tso)
        tld = dma_in(p, sc_t, k_d[kc])
        p.w("vector", t)
        p.w("vector", tld)
        t = p.emit("vector", lambda e: e.tensor_tensor(sc_t, sc_t, wc_t, ALU.mult),
                   inc=True)
        p.w("sync", t)
        dma_out(p, kh_d[kc], sc_t)
        p.barrier()
    xm_flat = xm.rearrange("p a b -> p (a b)")
    otile = xm_flat[:, 0:C]
    o2 = xm_flat[:, C:2 * C]
    for st in range(ntt128):
        dma_in(p, xa[:, :, 0:128], kh_d[:, :, st * 128:(st + 1) * 128].rearrange(
            "c p t -> p c t"))
        dma_in(p, xb[:, :, 0:128], v_d[:, :, st * 128:(st + 1) * 128].rearrange(
            "c p t -> p c t"))
        p.barrier()
        for (src, dst_t, ot) in ((xa, khtok, otile), (xb, vtok, o2)):
            for j in range(NCH):
                b = p.bank()
                tpe = p.emit("tensor", lambda e, b=b, j=j, src=src: e.transpose(
                    p.banks[b][:, 0:128], src[:, j, 0:128], ident), inc=True)
                p.w("vector", tpe)
                tev = p.emit("vector", lambda e, b=b, j=j, ot=ot: e.tensor_copy(
                    ot[:, j * 128:(j + 1) * 128], p.banks[b][:, 0:128]), inc=True)
                p.bank_done(b, tev)
            p.barrier(("sync",))
            dma_out(p, dst_t[st * 128:(st + 1) * 128, :], ot)
        p.barrier()

    S_t = [A(f"S{c}", [128, 128]) for c in range(NCH)]
    khsg = wT[0:16, :, :]
    vsg = wnat[0:16, :, :]
    rtt = xa[:, :, 0:128]
    rft = xa[:, :, 128:256]
    kft = xa[:, :, 256:384]
    eft = xa[:, :, 384:512]
    vft = xb[:, :, 0:128]
    yt = xb[:, :, 128:256]
    Dt = xb[:, :, 256:384]
    Et = xb[:, :, 384:512]
    Pt = xc[:, :, 0:128]
    gt_ = xc[:, :, 128:256]
    bart = A("bart", [128, 128])
    rmt4 = A("rmt4", [128, 128])
    bmask = A("bmask_sb", [128, 128])
    dEt = A("dEt", [128, NCH, 8])

    dma_in(p, bmask, bmask_d)
    for c in range(NCH):
        p.emit("vector", lambda e, c=c: e.memset(S_t[c], 0.0), inc=True)
    p.barrier()
    for it in range(ntt128):
        t0 = it * 128
        dma_in(p, rtt, rt_d[:, :, t0:t0 + 128].rearrange("c p t -> p c t"))
        dma_in(p, rft, r_d[:, :, t0:t0 + 128].rearrange("c p t -> p c t"))
        dma_in(p, kft, k_d[:, :, t0:t0 + 128].rearrange("c p t -> p c t"))
        dma_in(p, eft, e_d[:, :, t0:t0 + 128].rearrange("c p t -> p c t"))
        dma_in(p, vft, v_d[:, :, t0:t0 + 128].rearrange("c p t -> p c t"))
        dma_in(p, bart, bar_d[:, t0:t0 + 128])
        dma_in(p, rmt4, rmask_d[:, t0:t0 + 128])
        dma_in(p, khsg, khtok[t0:t0 + 128, :].rearrange("(s o) f -> o s f", o=SC))
        dma_in(p, vsg, vtok[t0:t0 + 128, :].rearrange("(s o) f -> o s f", o=SC))
        dma_in(p, dEt, dE_d[:, :, it * 8:(it + 1) * 8].rearrange("c p s -> p c s"))
        p.barrier()
        t = p.emit("vector", lambda e: e.tensor_tensor(
            eft, eft, bart.rearrange("p (o t) -> p o t", o=1).to_broadcast(
                [128, NCH, 128]), ALU.add), inc=True)
        tz = p.emit("vector", lambda e: e.memset(yt, 0.0), inc=True)
        tz2 = p.emit("vector", lambda e: e.memset(Dt, 0.0), inc=True)
        p.barrier(("vector", "scalar", "tensor"))
        for dl in range(SC):
            p.w("vector", p.t("qt"))
            p.w("scalar", p.t("qv"))
            if dl == 0:
                t = p.emit("vector", lambda e: e.tensor_tensor(Pt, rft, kft,
                                                               ALU.mult), inc=True)
                p.w("vector", t)
                t = p.emit("vector", lambda e: e.tensor_tensor(
                    Pt, Pt, ucol.rearrange("p (c o) -> p c o", o=1).to_broadcast(
                        [128, NCH, 128]), ALU.mult), inc=True)
                nn = 128
            else:
                if dl > 1:
                    t = p.emit("vector", lambda e, dl=dl: e.tensor_tensor(
                        Dt[:, :, dl:], Dt[:, :, dl:], eft[:, :, 1:128 - dl + 2 - 1],
                        ALU.add), inc=True)
                    p.w("vector", t)
                    p.w("scalar", t)
                t = p.emit("scalar", lambda e, dl=dl: e.activation(
                    Et[:, :, dl:], Dt[:, :, dl:], AF.Exp, scale=-1.0), inc=True)
                p.w("vector", t)
                t = p.emit("vector", lambda e, dl=dl: e.tensor_tensor(
                    Et[:, :, dl:], Et[:, :, dl:], rft[:, :, dl:], ALU.mult), inc=True)
                p.w("vector", t)
                t = p.emit("vector", lambda e, dl=dl: e.tensor_tensor(
                    Pt[:, :, dl:], Et[:, :, dl:], kft[:, :, 0:128 - dl], ALU.mult),
                    inc=True)
                if dl == 1:
                    p.w("vector", t)
                    t = p.emit("vector", lambda e: e.tensor_tensor(
                        Pt[:, :, 1:], Pt[:, :, 1:],
                        rmt4.rearrange("p (o t) -> p o t", o=1).to_broadcast(
                            [128, NCH, 128])[:, :, 1:], ALU.mult), inc=True)
                nn = 128 - dl
            p.w("vector", t)
            p.w("tensor", t)
            for hh in range(2):
                b = p.bank()
                tpe = p.emit("tensor", lambda e, b=b, hh=hh, dl=dl, nn=nn: e.matmul(
                    p.banks[b][:, 0:4 * nn],
                    bmask, Pt[:, 4 * hh:4 * (hh + 1), 128 - nn:],
                    start=True, stop=True), inc=True)
                p.w("vector", tpe)
                t = p.emit("vector", lambda e, b=b, hh=hh, dl=dl, nn=nn:
                           e.tensor_tensor(
                               Et[:, 4 * hh:4 * (hh + 1), 128 - nn:],
                               p.banks[b][:, 0:4 * nn].rearrange(
                                   "p (c t) -> p c t", c=4),
                               vft[:, 4 * hh:4 * (hh + 1), 0:nn], ALU.mult),
                           inc=True)
                p.bank_done(b, t)
                p.w("vector", t)
                t = p.emit("vector", lambda e, hh=hh, dl=dl, nn=nn: e.tensor_tensor(
                    yt[:, 4 * hh:4 * (hh + 1), 128 - nn:],
                    yt[:, 4 * hh:4 * (hh + 1), 128 - nn:],
                    Et[:, 4 * hh:4 * (hh + 1), 128 - nn:], ALU.add), inc=True)
                p.w("vector", t)
        for sc in range(8):
            c0 = sc * SC
            p.w("tensor", p.t("qv"))
            b = p.bank()
            for c in range(NCH):
                tpe = p.emit("tensor", lambda e, b=b, c=c, c0=c0: e.matmul(
                    p.banks[b][:, c * SC:(c + 1) * SC], S_t[c],
                    rtt[:, c, c0:c0 + SC], start=True, stop=True),
                    inc=(c == NCH - 1))
            p.w("vector", tpe)
            t = p.emit("vector", lambda e, b=b, c0=c0: e.tensor_tensor(
                yt[:, :, c0:c0 + SC],
                yt[:, :, c0:c0 + SC],
                p.banks[b][:, 0:128].rearrange("p (c t) -> p c t", c=NCH), ALU.add),
                inc=True)
            p.bank_done(b, t)
            for c in range(NCH):
                b = p.bank()
                tpe = p.emit("tensor", lambda e, b=b, c=c, sc=sc: e.matmul(
                    p.banks[b][:, 0:128], khsg[:, sc, c * 128:(c + 1) * 128],
                    vsg[:, sc, c * 128:(c + 1) * 128], start=True, stop=True),
                    inc=True)
                p.w("vector", tpe)
                t = p.emit("vector", lambda e, b=b: e.tensor_tensor(
                    Pt[:, 0, :], p.banks[b][:, 0:128], bmask, ALU.mult), inc=True)
                p.bank_done(b, t)
                p.w("vector", t)
                t = p.emit("vector", lambda e, c=c, sc=sc: e.scalar_tensor_tensor(
                    S_t[c], S_t[c], dEt[:, c, sc:sc + 1], Pt[:, 0, :],
                    ALU.mult, ALU.add), inc=True)
                p.w("vector", t)
        tld = dma_in(p, gt_, g_d[:, :, t0:t0 + 128].rearrange("c p t -> p c t"))
        p.w("vector", tld)
        t = p.emit("vector", lambda e: e.tensor_tensor(yt, yt, gt_, ALU.mult),
                   inc=True)
        p.w("sync", t)
        dma_out(p, gy_d[:, :, t0:t0 + 128].rearrange("c p t -> p c t"), yt)
        p.barrier()

    load_weight_T(p, Wo, wT, wnat, ident)
    p.barrier()
    for it in range(ntt):
        t0 = it * TT
        dma_in(p, xa, gy_d[:, :, t0:t0 + TT].rearrange("c p t -> p c t"))
        p.barrier()

        def evac(m, b, pet):
            p.w("vector", pet)
            return p.emit("vector", lambda e, m=m, b=b: e.tensor_copy(
                osb[:, m, :], p.banks[b][:, 0:TT]), inc=True)
        proj_fm(p, evac, xa, wT, TT)
        p.barrier(("sync",))
        dma_out(p, a1[:, :, t0:t0 + TT].rearrange("c p t -> p c t"), osb)
        p.barrier()
    p.barrier()
    p.pop_scope()


def emit_stage7(p, Tl=T, e_lo=0, e_hi=8, first=True):
    nc = p.nc
    p.push_scope()
    pfx = f"s7{e_lo}_"
    ntt = Tl // TT
    FB = 512
    NFB = 4096 // FB

    D = lambda n, s, k: p.dram(pfx + n, s, k)
    xn1 = D("xn1", [NCH, 128, Tl], "ExternalInput")
    if first:
        x1_ = D("x1", [NCH, 128, Tl], "ExternalInput")
        a1_ = D("a1", [NCH, 128, Tl], "ExternalInput")
    else:
        base = D("base", [NCH, 128, Tl], "ExternalInput")
    Wg = D("Wg", [C, 8], "ExternalInput")
    bg = D("bg", [1, 8], "ExternalInput")
    W1 = D("W1", [8, C, 4 * C], "ExternalInput")
    b1 = D("b1", [8, 128, 32], "ExternalInput")
    W2 = D("W2", [8, 4 * C, C], "ExternalInput")
    b2 = D("b2", [128, 64], "ExternalInput")
    emask_d = D("emask", [128, 64], "ExternalInput")
    idc = D("identity", [128, 128], "ExternalInput")
    out_d = D("out", [NCH, 128, Tl], "ExternalOutput")
    ct_d = D("ct", [8, Tl], "Internal")

    A = lambda n, s, dt=F32: p.sbuf(pfx + n, s, dt)
    ident = A("ident", [128, 128])
    ones_row = A("ones_row", [1, 128])
    wgsb = A("wgsb", [128, NCH, 8])
    bgsb = A("bgsb", [1, 8])
    emask = A("emask_sb", [128, 64])
    b2sb = A("b2sb", [128, 64])
    xa2 = [A(f"xa{i}", [128, NCH, TT]) for i in range(2)]
    ob2 = [A(f"osb{i}", [128, NCH, TT]) for i in range(2)]
    xc2 = [None, None]
    gsb = A("gsb", [128, 4, TT])
    csb = [A(f"csb{i}", [128, TT]) for i in range(4)]
    ptile = A("ptile", [128, 8])
    prank = A("prank", [128, 8])
    pw1f = A("pw1f", [128, NCH, FB])
    pw2f = A("pw2f", [128, FB // 128, C])
    pw1b = [A(f"pw1b_{i}", [128, NCH, FB], BF16) for i in range(2)]
    pw2b = [A(f"pw2b_{i}", [128, FB // 128, C], BF16) for i in range(2)]
    xab2 = [A(f"xab{i}", [128, NCH, TT], BF16) for i in range(2)]
    gsbb = A("gsbb", [128, 4, TT], BF16)
    b1sb = [A(f"b1sb_{i}", [128, 32]) for i in range(2)]
    lsum = A("lsum", [128, 4])
    ct_sb = A("ct_sb", [8, 128])
    if first:
        xc2 = [A(f"xc7{i}", [128, NCH, TT]) for i in range(2)]
    crow = [A(f"crow{i}", [1, TT]) for i in range(4)]

    dma_in(p, ident, idc)
    dma_in(p, wgsb, Wg.rearrange("(a p) b -> p a b", p=128))
    dma_in(p, bgsb, bg)
    dma_in(p, emask, emask_d)
    dma_in(p, b2sb, b2)
    p.emit("vector", lambda e: e.memset(ones_row, 1.0), inc=True)
    p.barrier()

    if first:
        for st in range(Tl // 128):
            t0 = st * 128
            dma_in(p, xa2[0][:, :, 0:128], xn1[:, :, t0:t0 + 128].rearrange(
                "c p t -> p c t"))
            p.barrier(("tensor", "vector", "scalar"))
            b = p.bank()
            for kc in range(NCH):
                p.emit("tensor", lambda e, b=b, kc=kc: e.matmul(
                    p.banks[b][:, 0:8], xa2[0][:, kc, 0:128], wgsb[:, kc, :],
                    start=(kc == 0), stop=False), inc=False)
            tpe = p.emit("tensor", lambda e, b=b: e.matmul(
                p.banks[b][:, 0:8], ones_row, bgsb, start=False, stop=True),
                inc=True)
            p.w("vector", tpe)
            t = p.emit("vector", lambda e, b=b: e.tensor_copy(ptile,
                                                              p.banks[b][:, 0:8]),
                       inc=True)
            p.bank_done(b, t)
            p.w("vector", t)
            t = p.emit("vector", lambda e: e.reduce_max(lsum[:, 0:1], ptile,
                                                        axis=AX.X, negate=True),
                       inc=True)
            p.w("scalar", t)
            t = p.emit("scalar", lambda e: e.activation(
                ptile, ptile, AF.Exp, bias=lsum[:, 0:1], accum_out=lsum[:, 1:2]),
                inc=True)
            p.w("vector", t)
            t = p.emit("vector", lambda e: e.reciprocal(lsum[:, 2:3], lsum[:, 1:2]),
                       inc=True)
            p.w("vector", t)
            t = p.emit("vector", lambda e: e.tensor_scalar(
                ptile, ptile, lsum[:, 2:3], None, ALU.mult), inc=True)
            p.w("vector", t)
            for ee in range(8):
                t1 = p.emit("vector", lambda e, ee=ee: e.tensor_scalar(
                    prank, ptile, ptile[:, ee:ee + 1], None, ALU.is_gt), inc=True)
                t2 = p.emit("scalar", lambda e, ee=ee: e.activation(
                    lsum[:, 3:4], ptile[:, ee:ee + 1], AF.Copy), inc=True) \
                    if False else None
                t2 = p.emit("vector", lambda e, ee=ee: e.tensor_scalar(
                    gsb[:, 0, 0:8], ptile, ptile[:, ee:ee + 1], None, ALU.is_equal),
                    inc=True)
                p.w("vector", t2)
                t3 = p.emit("vector", lambda e, ee=ee: e.tensor_tensor(
                    gsb[:, 0, 0:8], gsb[:, 0, 0:8], emask[:, ee * 8:(ee + 1) * 8],
                    ALU.mult), inc=True)
                p.w("vector", t3)
                t4 = p.emit("vector", lambda e: e.tensor_tensor(
                    prank, prank, gsb[:, 0, 0:8], ALU.add), inc=True)
                p.w("vector", t4)
                t5 = p.emit("vector", lambda e, ee=ee: e.reduce_sum(
                    gsb[:, 1, ee:ee + 1], prank, axis=AX.X), inc=True)
                p.w("vector", t5)
            t = p.emit("vector", lambda e: e.tensor_scalar(
                gsb[:, 1, 0:8], gsb[:, 1, 0:8], 2.0, None, ALU.is_lt), inc=True)
            p.w("vector", t)
            t = p.emit("vector", lambda e: e.tensor_tensor(
                ptile, ptile, gsb[:, 1, 0:8], ALU.mult), inc=True)
            p.w("vector", t)
            t = p.emit("vector", lambda e: e.reduce_sum(lsum[:, 0:1], ptile,
                                                        axis=AX.X), inc=True)
            p.w("vector", t)
            t = p.emit("vector", lambda e: e.tensor_scalar(
                lsum[:, 0:1], lsum[:, 0:1], 1e-8, None, ALU.add), inc=True)
            p.w("vector", t)
            t = p.emit("vector", lambda e: e.reciprocal(lsum[:, 1:2], lsum[:, 0:1]),
                       inc=True)
            p.w("vector", t)
            t = p.emit("vector", lambda e: e.tensor_scalar(
                ptile, ptile, lsum[:, 1:2], None, ALU.mult), inc=True)
            p.w("vector", t)
            p.w("tensor", t)
            b = p.bank()
            tpe = p.emit("tensor", lambda e, b=b: e.transpose(
                p.banks[b][0:8, 0:128], ptile, ident), inc=True)
            p.w("vector", tpe)
            t = p.emit("vector", lambda e, b=b: e.tensor_copy(
                ct_sb, p.banks[b][0:8, 0:128]), inc=True)
            p.bank_done(b, t)
            p.w("sync", t)
            tst = dma_out(p, ct_d[:, t0:t0 + 128], ct_sb)
            p.w("vector", tst)
            p.w("tensor", tst)
    else:
        pass

    p.barrier()
    ct_in = D("ct_in", [8, Tl], "ExternalInput") if not first else ct_d
    ct_out = D("ct_out", [8, Tl], "ExternalOutput")
    tile_use = {}

    def ld_tile(it, ti):
        t0 = it * TT
        p.w("sync", tile_use.get(ti))
        t1 = dma_in(p, xa2[ti], xn1[:, :, t0:t0 + TT].rearrange("c p t -> p c t"))
        if first:
            t2 = dma_in(p, ob2[ti], x1_[:, :, t0:t0 + TT].rearrange("c p t -> p c t"))
            t3 = dma_in(p, xc2[ti], a1_[:, :, t0:t0 + TT].rearrange("c p t -> p c t"))
            return (t1, t2, t3)
        t2 = dma_in(p, ob2[ti], base[:, :, t0:t0 + TT].rearrange("c p t -> p c t"))
        return (t1, t2)

    grps = [list(range(g, min(g + 2, ntt))) for g in range(0, ntt, 2)]
    tile_ld = {}
    for grp in grps:
        nu = len(grp)
        for u, it in enumerate(grp):
            tile_ld[u] = ld_tile(it, u)
        for u in range(nu):
            for tt in tile_ld[u]:
                p.w("tensor", tt)
                p.w("vector", tt)
            p.w("scalar", tile_ld[u][0])
            p.w("scalar", getattr(p, "xab_use", None))
            txab = p.emit("scalar", lambda e, u=u: e.activation(
                xab2[u][:, :, :], xa2[u][:, :, :], AF.Copy), inc=True)
            p.w("tensor", txab)
            if first:
                for kc in range(NCH):
                    p.w("vector", p.t("qv"))
                    p.emit("vector", lambda e, kc=kc, u=u: e.tensor_tensor(
                        ob2[u][:, kc, :], ob2[u][:, kc, :], xc2[u][:, kc, :],
                        ALU.add), inc=True)
        if not hasattr(p, "wld"):
            p.wld = {}
            p.wuse = {}
            p.wcast = {}
            p.stg_use = None
            p.crow_use = {}

        def ld_fb(job, wi):
            ee2, fb = job
            f0 = fb * FB
            p.w("sync", p.stg_use)
            t1 = dma_in(p, pw1f, W1[ee2][:, f0:f0 + FB].rearrange(
                "(a p) b -> p a b", p=128))
            p.w("gpsimd", p.stg_use)
            t2 = dma_in(p, pw2f, W2[ee2][f0:f0 + FB, :].rearrange(
                "(a p) b -> p a b", p=128), eng="gpsimd")
            p.w("sync", p.wuse.get(wi))
            t3 = dma_in(p, b1sb[wi][:, 0:FB // 128],
                        b1[ee2][:, fb * (FB // 128):(fb + 1) * (FB // 128)])
            p.wld[wi] = (t1, t2, t3)

        def cast_fb(wi):
            t1, t2, t3 = p.wld[wi]
            p.w("scalar", t1)
            p.w("scalar", p.wuse.get(wi))
            c1 = p.emit("scalar", lambda e, wi=wi: e.activation(
                pw1b[wi][:, :, :], pw1f[:, :, :], AF.Copy), inc=True)
            p.w("scalar", t2)
            c2 = p.emit("scalar", lambda e, wi=wi: e.activation(
                pw2b[wi][:, :, :], pw2f[:, :, :], AF.Copy), inc=True)
            p.wcast[wi] = (c1, c2, t3)
            p.stg_use = c2

        def ld_crow(ee2, u, ci):
            t0u = grp[u] * TT
            p.w("sync", p.crow_use.get(ci))
            return dma_in(p, crow[ci], ct_in[ee2:ee2 + 1, t0u:t0u + TT])

        def ld_crows(ee2):
            return [ld_crow(ee2, u, (ee2 % 2) * 2 + u) for u in range(nu)]

        jobs = [(ee2, fb) for ee2 in range(e_lo, e_hi) for fb in range(NFB)]
        ld_fb(jobs[0], 0)
        cast_fb(0)
        tcr = {jobs[0][0]: ld_crows(jobs[0][0])}
        tcsb = {}
        for ji, (ee, fb) in enumerate(jobs):
            wi = ji % 2
            if ji + 1 < len(jobs):
                ld_fb(jobs[ji + 1], 1 - wi)
                ne = jobs[ji + 1][0]
                if ne != ee and ne not in tcr:
                    tcr[ne] = ld_crows(ne)
            if fb == 0:
                tl = []
                for u in range(nu):
                    ci = (ee % 2) * 2 + u
                    b = p.bank()
                    p.w("tensor", tcr[ee][u])
                    tpe = p.emit("tensor", lambda e, b=b, ci=ci: e.matmul(
                        p.banks[b][:, 0:TT], ones_row, crow[ci],
                        start=True, stop=True), inc=True)
                    p.w("vector", tpe)
                    p.w("vector", p.t("qv"))
                    tc = p.emit("vector", lambda e, b=b, ci=ci: e.tensor_copy(
                        csb[ci], p.banks[b][:, 0:TT]), inc=True)
                    p.bank_done(b, tc)
                    tl.append(tc)
                tcsb[ee] = tl
                for u in range(nu):
                    p.crow_use[(ee % 2) * 2 + u] = p.t("qt")
            c1, c2, t3 = p.wcast[wi]
            p.w("tensor", c1)
            p.w("tensor", c2)
            p.w("scalar", t3)
            for u in range(nu):
                ci = (ee % 2) * 2 + u
                for m in range(FB // 128):
                    b = p.bank()
                    for kc in range(NCH):
                        tpe = p.emit("tensor", lambda e, b=b, m=m, kc=kc, wi=wi,
                                     u=u: e.matmul(
                            p.banks[b][:, 0:TT],
                            pw1b[wi][:, kc, m * 128:(m + 1) * 128],
                            xab2[u][:, kc, 0:TT], start=(kc == 0),
                            stop=(kc == NCH - 1)), inc=(kc == NCH - 1))
                    p.w("scalar", tpe)
                    tev = p.emit("scalar", lambda e, b=b, m=m, wi=wi: e.activation(
                        gsb[:, m, :], p.banks[b][:, 0:TT], AF.Gelu_apprx_tanh,
                        bias=b1sb[wi][:, m:m + 1]), inc=True)
                    p.bank_done(b, tev)
                    p.w("vector", tev)
                    p.w("vector", tcsb[ee][u])
                    t = p.emit("vector", lambda e, m=m, ci=ci: e.tensor_tensor(
                        gsbb[:, m, :], gsb[:, m, :], csb[ci], ALU.mult), inc=True)
                p.w("tensor", p.t("qv"))
                for m in range(NCH):
                    b = p.bank()
                    for kf in range(FB // 128):
                        tpe = p.emit("tensor", lambda e, b=b, m=m, kf=kf, wi=wi:
                                     e.matmul(
                            p.banks[b][:, 0:TT],
                            pw2b[wi][:, kf, m * 128:(m + 1) * 128],
                            gsbb[:, kf, 0:TT], start=(kf == 0),
                            stop=(kf == FB // 128 - 1)), inc=(kf == FB // 128 - 1))
                    p.w("vector", tpe)
                    p.w("vector", p.t("qv"))
                    tev = p.emit("vector", lambda e, b=b, m=m, u=u: e.tensor_tensor(
                        ob2[u][:, m, :], p.banks[b][:, 0:TT], ob2[u][:, m, :],
                        ALU.add), inc=True)
                    p.bank_done(b, tev)
            p.wuse[wi] = p.t("qt")
            if ji + 1 < len(jobs):
                cast_fb(1 - wi)
            if fb == NFB - 1:
                for u in range(nu):
                    ci = (ee % 2) * 2 + u
                    for m in range(NCH):
                        p.w("vector", p.t("qv"))
                        p.emit("vector", lambda e, m=m, ee=ee, ci=ci, u=u:
                               e.scalar_tensor_tensor(
                            ob2[u][:, m, :], csb[ci],
                            b2sb[:, ee * 8 + m:ee * 8 + m + 1],
                            ob2[u][:, m, :], ALU.mult, ALU.add), inc=True)
                    p.crow_use[ci] = p.t("qv")
        del tcr
        p.xab_use = p.t("qt")
        p.w("sync", p.t("qv"))
        for u, it in enumerate(grp):
            tile_use[u] = dma_out(
                p, out_d[:, :, it * TT:it * TT + TT].rearrange("c p t -> p c t"),
                ob2[u])
    p.barrier()
    if first:
        for st in range(Tl // 512):
            tld = dma_in(p, csb[0][0:8, :], ct_d[:, st * 512:(st + 1) * 512])
            p.w("sync", tld)
            tst = dma_out(p, ct_out[:, st * 512:(st + 1) * 512], csb[0][0:8, :])
            p.w("sync", tst)
    else:
        for st in range(Tl // 512):
            tld = dma_in(p, csb[0][0:8, :], ct_in[:, st * 512:(st + 1) * 512])
            p.w("sync", tld)
            tst = dma_out(p, ct_out[:, st * 512:(st + 1) * 512], csb[0][0:8, :])
            p.w("sync", tst)
    p.barrier()
    p.pop_scope()


def fm(x):
    return np.ascontiguousarray(np.asarray(x).T.reshape(NCH, 128, -1))


def fm_inv(x):
    return np.ascontiguousarray(np.asarray(x).reshape(C, -1).T)


def vec_col(v):
    return np.ascontiguousarray(np.asarray(v).reshape(NCH, 128).T)


def stage1_inputs(x_row_fm, w):
    vecs = np.concatenate([
        vec_col(w["L0_attn_time_mix_k"]), vec_col(w["L0_attn_time_mix_v"]),
        vec_col(w["L0_attn_time_mix_r"]), vec_col(w["L0_attn_time_decay"]),
        vec_col(w["L0_attn_time_first"]), vec_col(w["L0_attn_ln_w"]),
        vec_col(w["L0_attn_ln_b"]), np.zeros((128, NCH), np.float32)], axis=1)
    consts = np.zeros((128, 8), np.float32)
    consts[:, 0] = LN_EPS
    consts[:, 1] = RMS_EPS
    return {
        "x0": x_row_fm,
        "Wk": np.ascontiguousarray(w["L0_attn_Wk"]),
        "Wv": np.ascontiguousarray(w["L0_attn_Wv"]),
        "Wr": np.ascontiguousarray(w["L0_attn_Wr"]),
        "Wo": np.ascontiguousarray(w["L0_attn_Wo"]),
        "vecs": np.ascontiguousarray(vecs.astype(np.float32)), "consts": consts,
        "identity": np.eye(128, dtype=np.float32),
    }


def _mk(pfx, emit, *a, **kw):
    es = ExitStack()
    nc = bass.Bass("TRN2", target_bir_lowering=False)
    p = Prog(nc, es, pfx=pfx)
    emit(p, *a, **kw)
    p.finalize()
    es.close()
    return nc


def build_stage1(Tl=T):
    return _mk("s1_", emit_stage1, Tl)


def build_dec_qkv(pfx, S):
    return _mk(pfx, emit_dec_qkv, pfx, S)


def build_dec_attn(pfx, S):
    return _mk(pfx, emit_dec_attn, pfx, S)


def build_dec_tail(pfx, S):
    return _mk(pfx, emit_dec_tail, pfx, S)


def build_stage3(Tl=T):
    return _mk("s3_", emit_stage3, Tl)


def build_stage4(Tl=T):
    return _mk("s4_", emit_stage4, Tl)


def build_stage5(Tl=T):
    return _mk("s5_", emit_stage5, Tl)


def build_stage6(Tl=T):
    return _mk("s6_", emit_stage6, Tl)


def build_stage7(Tl=T, e_lo=0, e_hi=8, first=True):
    return _mk(f"s7{e_lo}_", emit_stage7, Tl, e_lo, e_hi, first)


def fused_aliases(Tl):
    K = Tl // 4
    al = {}
    for st in ("dA_", "dB_", "dC_", "dD_", "dE_", "dF_", "s3_", "s4_", "s5_",
               "s6_", "s70_", "s74_"):
        al[st + "identity"] = "identity"
        al[st + "cst"] = "G_cst"
    al["consts"] = "G_cst8" if False else "consts"
    al.update({
        "s1_identity": "identity", "s1_consts": "G_cst",
        "dA_hs": "s1_xn0", "dA_cos": "G_cos", "dA_sin": "G_sin",
        "dB_QT": "dA_QT", "dB_KT": "dA_KT", "dB_Vtok": "dA_Vtok",
        "dB_maskn": "G_maskn",
        "dC_hs": "s1_xn0", "dC_Otok": "dB_Otok",
        "s3_xn0": "s1_xn0", "s3_proc": "dC_proc",
        "s4_dst": "s3_dst", "s4_dch": "s3_dch", "s4_ptok": "s3_ptok",
        "s4_cos": "G_cos", "s4_sin": "G_sin",
        "dD_hs": "s4_selfm", "dD_cos": "s4_cossel", "dD_sin": "s4_sinsel",
        "dE_QT": "dD_QT", "dE_KT": "dD_KT", "dE_Vtok": "dD_Vtok",
        "dE_maskn": "G_maskn",
        "dF_hs": "s4_selfm", "dF_Otok": "dE_Otok",
        "s5_proc2": "dF_proc", "s5_seltok": "s4_seltok", "s5_isc": "s4_isc",
        "s5_f0tok": "s3_f0tok", "s5_x0": "s1_x0", "s5_a0": "s1_a0",
        "s6_x1": "s5_x1",
        "s70_xn1": "s6_xn1", "s70_x1": "s5_x1", "s70_a1": "s6_a1",
        "s74_xn1": "s6_xn1", "s74_base": "s70_out", "s74_ct_in": "s70_ct_out",
    })
    for nm in ("Wg", "bg", "W1", "b1", "W2", "b2", "emask"):
        al["s74_" + nm] = "s70_" + nm
    return al


def build_fused(Tl=T):
    es = ExitStack()
    nc = bass.Bass("TRN2", target_bir_lowering=False)
    p = Prog(nc, es, pfx="f_", fused=True, final_outputs={"s74_out"},
             alias=fused_aliases(Tl))
    p.dram("identity", [128, 128], "ExternalInput")
    p.dram("G_cst", [128, 8], "ExternalInput")
    p.dram("G_cos", [Tl, 32], "ExternalInput")
    p.dram("G_sin", [Tl, 32], "ExternalInput")
    p.dram("G_maskn", [128, 128], "ExternalInput")
    ccsrc = p.dram("G_ccsrc", [128, 8], "Internal")
    ccdst = p.dram("G_ccdst", [128, 8], "Internal")

    K = Tl // 4
    emit_stage1(p, Tl)
    emit_dec_qkv(p, "dA_", Tl)
    emit_dec_attn(p, "dB_", Tl)
    emit_dec_tail(p, "dC_", Tl)
    emit_stage3(p, Tl)
    p.push_scope()
    padt = p.sbuf("f_pad", [128, 8])
    t = p.emit("vector", lambda e: e.memset(padt, 0.0), inc=True)
    parts_ap = p.dram_map["s3_parts"]
    p.w("sync", t)
    tld = dma_in(p, padt[0:1, 0:2], parts_ap)
    p.w("sync", tld)
    if COLLECTIVE:
        tst = dma_out(p, ccsrc, padt)
        p.w("gpsimd", tst)
        p.emit("gpsimd", lambda e: e.collective_compute(
            "AllReduce", ALU.add, replica_groups=[list(range(NCORES))],
            ins=[ccsrc.opt()], outs=[ccdst.opt()]), inc=True)
    else:
        p.w("vector", t)
        p.w("vector", tld)
        t2 = p.emit("vector", lambda e: e.tensor_scalar(
            padt, padt, 8.0, None, ALU.mult), inc=True)
        p.w("sync", t2)
        dma_out(p, ccdst, padt)
    p.barrier()
    p.pop_scope()
    emit_stage4(p, Tl)
    emit_dec_qkv(p, "dD_", K)
    emit_dec_attn(p, "dE_", K)
    emit_dec_tail(p, "dF_", K)
    emit_stage5(p, Tl)
    emit_stage6(p, Tl)
    emit_stage7(p, Tl, 0, 4, True)
    emit_stage7(p, Tl, 4, 8, False)
    p.finalize()
    es.close()
    return nc


_NC_CACHE = {}
COLLECTIVE = True
SIM = False
FUSED = True
RUN_CORES = NCORES


def get_nc(name, builder):
    if name not in _NC_CACHE:
        _NC_CACHE[name] = builder()
    return _NC_CACHE[name]


def run_stage(nc, in_maps, out_names):
    import time as _time
    _t0 = _time.time()
    ncore = len(in_maps)
    if SIM:
        from concourse import bass_interp
        res = []
        for m in in_maps:
            sim = bass_interp.MultiCoreSim(nc, 1)
            for k, v in m.items():
                sim.cores[0].tensor(k)[:] = np.asarray(v).reshape(
                    sim.cores[0].tensor(k).shape)
            sim.simulate()
            res.append({o: np.array(sim.cores[0].mem_tensor(o)) for o in out_names})
        return res
    r = run_bass_kernel_spmd(nc, in_maps, core_ids=list(range(ncore)))
    out = [{o: np.asarray(r.results[b][o]) for o in out_names}
           for b in range(ncore)]
    print(f"[stage {out_names[0]}] {_time.time() - _t0:.0f}s", flush=True)
    return out


def cos_sin_tables(S):
    half = 32
    inv = (1.0 / (np.float32(10000.0) ** (np.arange(half, dtype=np.float32)
                                          / np.float32(half)))).astype(np.float32)
    pos = np.arange(S, dtype=np.float32)
    ang = pos[:, None] * inv
    return np.cos(ang).astype(np.float32), np.sin(ang).astype(np.float32)


def module_consts(Tl):
    K = Tl // 4
    cst = np.zeros((128, 8), np.float32)
    cst[:, 0] = LN_EPS
    cst[:, 1] = RMS_EPS
    cst[:, 2] = 0.125
    cst[:, 3] = 1.0 / 1024
    cst[:, 4] = 0.5 / 1024
    cst[:, 5] = -0.5
    cst[:, 6] = float(K)
    cst[:, 7] = 1e9
    ident = np.eye(128, dtype=np.float32)
    maskn = np.where(np.tril(np.ones((128, 128), bool)), 0.0,
                     -3.0e38).astype(np.float32)
    cos, sin = cos_sin_tables(Tl)
    iota = np.broadcast_to(np.arange(Tl, dtype=np.float32), (128, Tl)).copy()
    ivals = (np.arange(128)[:, None] * (Tl // 128)
             + np.arange(Tl // 128)[None, :]).astype(np.float32)
    prem = np.triu(np.ones((128, 128), np.float32), 1)
    tau = np.arange(Tl)
    rmask = np.broadcast_to((tau % SC != 0).astype(np.float32), (128, Tl)).copy()
    bar = np.broadcast_to((1e6 * ((tau % SC == 0) | (tau % SC == SC - 1))
                           ).astype(np.float32), (128, Tl)).copy()
    bmask = np.zeros((128, 128), np.float32)
    bmask[:64, :64] = 1
    bmask[64:, 64:] = 1
    emask = np.zeros((128, 64), np.float32)
    for e in range(8):
        emask[:, e * 8:e * 8 + e] = 1.0
    return dict(cst=cst, ident=ident, maskn=maskn, cos=cos, sin=sin, iota=iota,
                ivals=ivals, prem=prem, rmask=rmask, bar=bar, bmask=bmask,
                emask=emask)


def kernel(**inputs):
    Tl = T
    w = {k: np.asarray(v) for k, v in inputs.items()}
    x = w["L0_attn_x"]
    nb = x.shape[0]
    CC = module_consts(Tl)
    ccos2, csin2 = cos_sin_tables(Tl)
    ar = lambda v: np.ascontiguousarray(v.astype(np.float32))

    if FUSED:
        nc = get_nc("fused", lambda: build_fused(Tl))
        b1cols = np.stack([np.ascontiguousarray(
            np.asarray(w["L1_ffn_b1"][e]).reshape(32, 128).T) for e in range(8)]
            ).astype(np.float32)
        b2cols = np.concatenate([vec_col(w["L1_ffn_b2"][e]) for e in range(8)],
                                axis=1).astype(np.float32)
        vecs6 = np.concatenate([
            vec_col(w["L1_attn_mix_k"].reshape(-1)),
            vec_col(w["L1_attn_mix_v"].reshape(-1)),
            vec_col(w["L1_attn_mix_r"].reshape(-1)),
            vec_col(w["L1_attn_mix_w"].reshape(-1)),
            vec_col(w["L1_attn_time_decay"].reshape(-1)),
            vec_col(w["L1_attn_ln_w"]), vec_col(w["L1_attn_ln_b"]),
            vec_col(w["L1_attn_time_first"].reshape(-1))], axis=1
            ).astype(np.float32)
        maps = []
        for b in range(nb):
            s1m = stage1_inputs(fm(np.ascontiguousarray(x[b])), w)
            m = {
                "identity": CC["ident"], "G_cst": CC["cst"], "G_cos": CC["cos"],
                "G_sin": CC["sin"], "G_maskn": CC["maskn"],
                "s1_x0": s1m["x0"], "s1_Wk": s1m["Wk"], "s1_Wv": s1m["Wv"],
                "s1_Wr": s1m["Wr"], "s1_Wo": s1m["Wo"], "s1_vecs": s1m["vecs"],
                "dA_wq": ar(w["L0_ffn_wq1"]), "dA_wk": ar(w["L0_ffn_wk1"]),
                "dA_wv": ar(w["L0_ffn_wv1"]), "dA_vec": vec_col(w["L0_ffn_ln1_1"]),
                "dC_wo": ar(w["L0_ffn_wo1"]), "dC_wg": ar(w["L0_ffn_wg1"]),
                "dC_wu": ar(w["L0_ffn_wu1"]), "dC_wd": ar(w["L0_ffn_wd1"]),
                "dC_vec": vec_col(w["L0_ffn_ln2_1"]),
                "s3_pw": ar(w["L0_ffn_prior_w"]),
                "s3_vec": vec_col(w["L0_ffn_prior_norm_w"]),
                "s4_iota": CC["iota"], "s4_ivals": CC["ivals"],
                "s4_prem": CC["prem"],
                "dD_wq": ar(w["L0_ffn_wq2"]), "dD_wk": ar(w["L0_ffn_wk2"]),
                "dD_wv": ar(w["L0_ffn_wv2"]), "dD_vec": vec_col(w["L0_ffn_ln1_2"]),
                "dF_wo": ar(w["L0_ffn_wo2"]), "dF_wg": ar(w["L0_ffn_wg2"]),
                "dF_wu": ar(w["L0_ffn_wu2"]), "dF_wd": ar(w["L0_ffn_wd2"]),
                "dF_vec": vec_col(w["L0_ffn_ln2_2"]),
                "s6_Wr": ar(w["L1_attn_Wr"]), "s6_Wk": ar(w["L1_attn_Wk"]),
                "s6_Wv": ar(w["L1_attn_Wv"]), "s6_Ww": ar(w["L1_attn_Ww"]),
                "s6_Wg": ar(w["L1_attn_Wg"]), "s6_Wo": ar(w["L1_attn_Wo"]),
                "s6_vecs": vecs6, "s6_rmask": CC["rmask"], "s6_bar": CC["bar"],
                "s6_bmask": CC["bmask"],
                "s70_Wg": ar(w["L1_ffn_Wg"]),
                "s70_bg": ar(w["L1_ffn_bg"]).reshape(1, 8),
                "s70_W1": ar(w["L1_ffn_W1"]), "s70_b1": b1cols,
                "s70_W2": ar(w["L1_ffn_W2"]), "s70_b2": b2cols,
                "s70_emask": CC["emask"],
            }
            maps.append(m)
        r = run_stage(nc, maps, ["s74_out"])
        out = np.stack([fm_inv(np.asarray(r[b]["s74_out"]).reshape(NCH, 128, Tl))
                        for b in range(nb)])
        return out.astype(np.float32)

    nc = get_nc("s1", lambda: build_stage1(Tl))
    maps = [stage1_inputs(fm(np.ascontiguousarray(x[b])), w) for b in range(nb)]
    r1 = run_stage(nc, maps, ["xn0", "a0"])

    nc = get_nc("dA", lambda: build_dec_qkv("dA_", Tl))
    maps = [{"dA_hs": r1[b]["xn0"], "dA_wq": ar(w["L0_ffn_wq1"]),
             "dA_wk": ar(w["L0_ffn_wk1"]), "dA_wv": ar(w["L0_ffn_wv1"]),
             "dA_vec": vec_col(w["L0_ffn_ln1_1"]), "dA_cst": CC["cst"],
             "dA_cos": CC["cos"], "dA_sin": CC["sin"],
             "dA_identity": CC["ident"]} for b in range(nb)]
    r2 = run_stage(nc, maps, ["dA_QT", "dA_KT", "dA_Vtok"])

    nc = get_nc("dB", lambda: build_dec_attn("dB_", Tl))
    maps = [{"dB_QT": r2[b]["dA_QT"], "dB_KT": r2[b]["dA_KT"],
             "dB_Vtok": r2[b]["dA_Vtok"], "dB_maskn": CC["maskn"],
             "dB_cst": CC["cst"], "dB_identity": CC["ident"]} for b in range(nb)]
    r3 = run_stage(nc, maps, ["dB_Otok"])

    nc = get_nc("dC", lambda: build_dec_tail("dC_", Tl))
    maps = [{"dC_hs": r1[b]["xn0"], "dC_Otok": r3[b]["dB_Otok"],
             "dC_wo": ar(w["L0_ffn_wo1"]), "dC_wg": ar(w["L0_ffn_wg1"]),
             "dC_wu": ar(w["L0_ffn_wu1"]), "dC_wd": ar(w["L0_ffn_wd1"]),
             "dC_vec": vec_col(w["L0_ffn_ln2_1"]), "dC_cst": CC["cst"],
             "dC_identity": CC["ident"]} for b in range(nb)]
    r4 = run_stage(nc, maps, ["dC_proc"])

    nc = get_nc("s3", lambda: build_stage3(Tl))
    maps = [{"s3_xn0": r1[b]["xn0"], "s3_proc": r4[b]["dC_proc"],
             "s3_pw": ar(w["L0_ffn_prior_w"]),
             "s3_vec": vec_col(w["L0_ffn_prior_norm_w"]), "s3_cst": CC["cst"],
             "s3_identity": CC["ident"]} for b in range(nb)]
    r5 = run_stage(nc, maps, ["s3_dst", "s3_dch", "s3_parts", "s3_ptok",
                              "s3_f0tok"])

    parts_all = np.concatenate([np.asarray(r5[b]["s3_parts"]).reshape(1, 2)
                                for b in range(nb)], axis=0).astype(np.float32)
    if nb < NCORES:
        parts_all = np.tile(parts_all[0:1], (NCORES, 1)) if nb == 1 else             np.concatenate([parts_all] * (NCORES // nb), axis=0)

    nc = get_nc("s4", lambda: build_stage4(Tl))
    maps = [{"s4_dst": r5[b]["s3_dst"], "s4_dch": r5[b]["s3_dch"],
             "s4_parts": parts_all, "s4_ptok": r5[b]["s3_ptok"],
             "s4_cos": CC["cos"], "s4_sin": CC["sin"], "s4_iota": CC["iota"],
             "s4_ivals": CC["ivals"], "s4_prem": CC["prem"], "s4_cst": CC["cst"],
             "s4_identity": CC["ident"]} for b in range(nb)]
    r6 = run_stage(nc, maps, ["s4_isc", "s4_seltok", "s4_selfm", "s4_cossel",
                              "s4_sinsel"])

    K = Tl // 4
    nc = get_nc("dD", lambda: build_dec_qkv("dD_", K))
    maps = [{"dD_hs": r6[b]["s4_selfm"], "dD_wq": ar(w["L0_ffn_wq2"]),
             "dD_wk": ar(w["L0_ffn_wk2"]), "dD_wv": ar(w["L0_ffn_wv2"]),
             "dD_vec": vec_col(w["L0_ffn_ln1_2"]), "dD_cst": CC["cst"],
             "dD_cos": r6[b]["s4_cossel"], "dD_sin": r6[b]["s4_sinsel"],
             "dD_identity": CC["ident"]} for b in range(nb)]
    r7 = run_stage(nc, maps, ["dD_QT", "dD_KT", "dD_Vtok"])

    nc = get_nc("dE", lambda: build_dec_attn("dE_", K))
    maps = [{"dE_QT": r7[b]["dD_QT"], "dE_KT": r7[b]["dD_KT"],
             "dE_Vtok": r7[b]["dD_Vtok"], "dE_maskn": CC["maskn"],
             "dE_cst": CC["cst"], "dE_identity": CC["ident"]} for b in range(nb)]
    r8 = run_stage(nc, maps, ["dE_Otok"])

    nc = get_nc("dF", lambda: build_dec_tail("dF_", K))
    maps = [{"dF_hs": r6[b]["s4_selfm"], "dF_Otok": r8[b]["dE_Otok"],
             "dF_wo": ar(w["L0_ffn_wo2"]), "dF_wg": ar(w["L0_ffn_wg2"]),
             "dF_wu": ar(w["L0_ffn_wu2"]), "dF_wd": ar(w["L0_ffn_wd2"]),
             "dF_vec": vec_col(w["L0_ffn_ln2_2"]), "dF_cst": CC["cst"],
             "dF_identity": CC["ident"]} for b in range(nb)]
    r9 = run_stage(nc, maps, ["dF_proc"])

    nc = get_nc("s5", lambda: build_stage5(Tl))
    maps = [{"s5_proc2": r9[b]["dF_proc"], "s5_seltok": r6[b]["s4_seltok"],
             "s5_isc": r6[b]["s4_isc"], "s5_f0tok": r5[b]["s3_f0tok"],
             "s5_x0": fm(np.ascontiguousarray(x[b])), "s5_a0": r1[b]["a0"],
             "s5_identity": CC["ident"]} for b in range(nb)]
    r10 = run_stage(nc, maps, ["s5_x1"])

    vecs6 = np.concatenate([
        vec_col(w["L1_attn_mix_k"].reshape(-1)),
        vec_col(w["L1_attn_mix_v"].reshape(-1)),
        vec_col(w["L1_attn_mix_r"].reshape(-1)),
        vec_col(w["L1_attn_mix_w"].reshape(-1)),
        vec_col(w["L1_attn_time_decay"].reshape(-1)),
        vec_col(w["L1_attn_ln_w"]), vec_col(w["L1_attn_ln_b"]),
        vec_col(w["L1_attn_time_first"].reshape(-1))], axis=1).astype(np.float32)
    nc = get_nc("s6", lambda: build_stage6(Tl))
    maps = [{"s6_x1": r10[b]["s5_x1"], "s6_Wr": ar(w["L1_attn_Wr"]),
             "s6_Wk": ar(w["L1_attn_Wk"]), "s6_Wv": ar(w["L1_attn_Wv"]),
             "s6_Ww": ar(w["L1_attn_Ww"]), "s6_Wg": ar(w["L1_attn_Wg"]),
             "s6_Wo": ar(w["L1_attn_Wo"]), "s6_vecs": vecs6, "s6_cst": CC["cst"],
             "s6_identity": CC["ident"], "s6_rmask": CC["rmask"],
             "s6_bar": CC["bar"], "s6_bmask": CC["bmask"]} for b in range(nb)]
    r11 = run_stage(nc, maps, ["s6_xn1", "s6_a1"])

    b1cols = np.stack([np.ascontiguousarray(
        np.asarray(w["L1_ffn_b1"][e]).reshape(32, 128).T) for e in range(8)]
        ).astype(np.float32)
    b2cols = np.concatenate([vec_col(w["L1_ffn_b2"][e]) for e in range(8)],
                            axis=1).astype(np.float32)
    com = {"Wg": ar(w["L1_ffn_Wg"]), "bg": ar(w["L1_ffn_bg"]).reshape(1, 8),
           "W1": ar(w["L1_ffn_W1"]), "b1": b1cols, "W2": ar(w["L1_ffn_W2"]),
           "b2": b2cols, "emask": CC["emask"], "identity": CC["ident"]}
    nc = get_nc("s70", lambda: build_stage7(Tl, 0, 4, True))
    maps = [{**{f"s70_{k}": v for k, v in com.items()},
             "s70_xn1": r11[b]["s6_xn1"], "s70_x1": r10[b]["s5_x1"],
             "s70_a1": r11[b]["s6_a1"]} for b in range(nb)]
    r12 = run_stage(nc, maps, ["s70_out", "s70_ct_out"])

    nc = get_nc("s74", lambda: build_stage7(Tl, 4, 8, False))
    maps = [{**{f"s74_{k}": v for k, v in com.items()},
             "s74_xn1": r11[b]["s6_xn1"], "s74_base": r12[b]["s70_out"],
             "s74_ct_in": r12[b]["s70_ct_out"]} for b in range(nb)]
    r13 = run_stage(nc, maps, ["s74_out"])

    out = np.stack([fm_inv(np.asarray(r13[b]["s74_out"]).reshape(NCH, 128, Tl))
                    for b in range(nb)])
    return out.astype(np.float32)


# version 08
# speedup vs baseline: 1.5516x; 1.1113x over previous
import numpy as np
from contextlib import ExitStack

import concourse.bass as bass
import concourse.mybir as mybir
from concourse.bass_utils import run_bass_kernel_spmd

F32 = mybir.dt.float32
BF16 = mybir.dt.bfloat16
U32 = mybir.dt.uint32
I32 = mybir.dt.int32
AF = mybir.ActivationFunctionType
ALU = mybir.AluOpType
AX = mybir.AxisListType

T = 4096
C = 1024
NCH = 8
NCORES = 8
TT = 512

LN_EPS = 1e-5
RMS_EPS = 1e-6

QSEM = {"tensor": "qt", "vector": "qv", "scalar": "qs", "gpsimd": "qg", "sync": "qy"}


class Prog:
    ENGINES = ("tensor", "vector", "scalar", "gpsimd", "sync")

    def __init__(self, nc, es, pfx="", fused=False, final_outputs=(), alias=None):
        self.nc = nc
        self.es = es
        self.pfx = pfx
        self.fused = fused
        self.final_outputs = set(final_outputs)
        self.alias = dict(alias or {})
        self.dram_map = {}
        self.scope = es
        self.ops = {e: [] for e in self.ENGINES}
        self.sems = {}
        self.cnt = {}
        self.banks = [nc.alloc_psum_tensor(f"{pfx}bank{i}", [128, 512], F32).ap()
                      for i in range(8)]
        self.bank_release = [None] * 8
        self._rr = 0
        self.last_store = None

    def resolve(self, name):
        seen = set()
        while name in self.alias and name not in seen:
            seen.add(name)
            name = self.alias[name]
        return name

    def dram(self, name, shape, kind, dt=F32):
        name = self.resolve(name)
        if name in self.dram_map:
            return self.dram_map[name]
        k = kind
        if self.fused and kind == "ExternalOutput" and name not in self.final_outputs:
            k = "Internal"
        ap = self.nc.dram_tensor(name, shape, dt, kind=k).ap()
        self.dram_map[name] = ap
        return ap

    def sbuf(self, name, shape, dt=F32):
        return self.scope.enter_context(self.nc.sbuf_tensor(name, shape, dt)).ap()

    def push_scope(self):
        self.scope = ExitStack()
        return self.scope

    def pop_scope(self):
        self.scope.close()
        self.scope = self.es

    def sem(self, name):
        if name not in self.sems:
            self.sems[name] = self.es.enter_context(self.nc.semaphore(self.pfx + name))
            self.cnt[name] = 0
        return self.sems[name]

    def emit(self, engine, fn, inc=False, k=1):
        s = v = None
        if inc:
            name = QSEM[engine]
            s = self.sem(name)
            self.cnt[name] += k
            v = k

        def thunk(eng, fn=fn, s=s, v=v):
            ins = fn(eng)
            if s is not None:
                ins.then_inc(s, v)
        self.ops[engine].append(thunk)
        return (QSEM[engine], self.cnt[QSEM[engine]]) if inc else None

    DMA_POOL = {"sync": 10, "gpsimd": 4}

    def dma(self, engine, fn):
        npool = self.DMA_POOL[engine]
        key = f"_dmarr_{engine}"
        idx = getattr(self, key, 0)
        setattr(self, key, idx + 1)
        name = f"d{engine[0]}{idx % npool}"
        s = self.sem(name)
        prev = self.cnt[name]
        if prev > 0:
            self.ops[engine].append(lambda eng, s=s, v=prev: eng.wait_ge(s, v))
        self.cnt[name] += 16

        def thunk(eng, fn=fn, s=s):
            fn(eng).then_inc(s, 16)
        self.ops[engine].append(thunk)
        return (name, self.cnt[name])

    def t(self, name):
        self.sem(name)
        return (name, self.cnt[name])

    def w(self, engine, ticket):
        if ticket is not None and ticket[1] > 0:
            s = self.sem(ticket[0])
            v = ticket[1]
            self.ops[engine].append(lambda eng, s=s, v=v: eng.wait_ge(s, v))

    def barrier(self, engines=None):
        snap = [(n, c) for n, c in self.cnt.items() if c > 0]
        for e in (engines or self.ENGINES):
            for n, c in snap:
                self.w(e, (n, c))

    def bank(self):
        i = self._rr % 7
        self._rr += 1
        self.w("tensor", self.bank_release[i])
        return i

    def bank7(self):
        self.w("tensor", self.bank_release[7])
        return 7

    def bank_done(self, i, ticket):
        self.bank_release[i] = ticket

    def finalize(self):
        with self.nc.Block() as block:
            for e in self.ENGINES:
                thunks = self.ops[e]

                def runner(eng, thunks=thunks):
                    for th in thunks:
                        th(eng)
                getattr(block, e)(runner)


def dma_in(p, sb_ap, dram_ap, eng="sync"):
    return p.dma(eng, lambda e, o=sb_ap, i=dram_ap: e.dma_start(out=o, in_=i))


def dma_out(p, dram_ap, sb_ap, eng="sync"):
    return p.dma(eng, lambda e, o=dram_ap, i=sb_ap: e.dma_start(out=o, in_=i))


def load_weight_T(p, w_dram, wT_sb, wnat_sb, ident, d_out=C, d_in=C):
    t = dma_in(p, wnat_sb, w_dram.rearrange("(m p) c -> p m c", p=128))
    p.w("tensor", t)
    for m in range(d_out // 128):
        for kc in range(d_in // 128):
            b = p.bank()
            tpe = p.emit("tensor",
                         lambda e, b=b, m=m, kc=kc: e.transpose(
                             p.banks[b][:, 0:128],
                             wnat_sb[:, m, kc * 128:(kc + 1) * 128], ident),
                         inc=True)
            p.w("vector", tpe)
            tev = p.emit("vector",
                         lambda e, b=b, m=m, kc=kc: e.tensor_copy(
                             wT_sb[:, kc, m * 128:(m + 1) * 128], p.banks[b][:, 0:128]),
                         inc=True)
            p.bank_done(b, tev)
    return p.t("qv")


def proj_fm(p, out_cb, xt_sb, w_sb, n, n_m=NCH, n_k=NCH):
    for m in range(n_m):
        b = p.bank()
        tpe = None
        for kc in range(n_k):
            tpe = p.emit("tensor",
                         lambda e, b=b, m=m, kc=kc: e.matmul(
                             p.banks[b][:, 0:n], w_sb[:, kc, m * 128:(m + 1) * 128],
                             xt_sb[:, kc, 0:n],
                             start=(kc == 0), stop=(kc == n_k - 1)),
                         inc=(kc == n_k - 1))
        tev = out_cb(m, b, tpe)
        p.bank_done(b, tev)


def ln_tile(p, x_sb, n, sq_sb, a_bc, b_bc, rows, ones_col, ones_row,
            eps_col, kind, out_sb, w_col=None, b_col=None, n_ch=NCH):
    r_sum = rows[0:1, 0:n]
    r_sq = rows[0:1, 512:512 + n]
    r_a = rows[0:1, 1024:1024 + n]
    r_b = rows[0:1, 1536:1536 + n]
    nf = float(n_ch * 128)

    tsq = p.emit("scalar", lambda e: e.activation(
        sq_sb[:, 0:n_ch, 0:n], x_sb[:, 0:n_ch, 0:n], AF.Square), inc=True)
    if kind == "ln":
        b1 = p.bank()
        for kc in range(n_ch):
            t1 = p.emit("tensor",
                        lambda e, b1=b1, kc=kc: e.matmul(
                            p.banks[b1][0:1, 0:n], ones_col, x_sb[:, kc, 0:n],
                            start=(kc == 0), stop=(kc == n_ch - 1)),
                        inc=(kc == n_ch - 1))
    b2 = p.bank()
    p.w("tensor", tsq)
    for kc in range(n_ch):
        t2 = p.emit("tensor",
                    lambda e, b2=b2, kc=kc: e.matmul(
                        p.banks[b2][0:1, 0:n], ones_col, sq_sb[:, kc, 0:n],
                        start=(kc == 0), stop=(kc == n_ch - 1)),
                    inc=(kc == n_ch - 1))
    p.w("vector", t2)
    if kind == "ln":
        p.emit("vector", lambda e: e.tensor_scalar(r_sum, p.banks[b1][0:1, 0:n],
                                                   1.0 / nf, None, ALU.mult), inc=True)
        tb = p.emit("vector", lambda e: e.tensor_scalar(r_sq, p.banks[b2][0:1, 0:n],
                                                        1.0 / nf, None, ALU.mult),
                    inc=True)
        p.bank_done(b1, tb)
        p.bank_done(b2, tb)
        p.w("vector", tb)
        p.emit("vector", lambda e: e.tensor_tensor(r_a, r_sum, r_sum, ALU.mult),
               inc=True)
        p.w("vector", p.t("qv"))
        tv = p.emit("vector", lambda e: e.tensor_tensor(r_a, r_sq, r_a, ALU.subtract),
                    inc=True)
        p.w("scalar", tv)
        ts = p.emit("scalar", lambda e: e.activation(r_b, r_a, AF.Sqrt,
                                                     bias=eps_col[0:1, 0:1]), inc=True)
        p.w("vector", ts)
        p.emit("vector", lambda e: e.reciprocal(r_a, r_b), inc=True)
        p.w("vector", p.t("qv"))
        p.emit("vector", lambda e: e.tensor_tensor(r_b, r_sum, r_a, ALU.mult), inc=True)
        p.w("vector", p.t("qv"))
        tv = p.emit("vector", lambda e: e.tensor_scalar(r_b, r_b, -1.0, None, ALU.mult),
                    inc=True)
    else:
        tb = p.emit("vector", lambda e: e.tensor_scalar(r_a, p.banks[b2][0:1, 0:n],
                                                        1.0 / nf, None, ALU.mult),
                    inc=True)
        p.bank_done(b2, tb)
        p.w("scalar", tb)
        ts = p.emit("scalar", lambda e: e.activation(r_b, r_a, AF.Sqrt,
                                                     bias=eps_col[0:1, 0:1]), inc=True)
        p.w("vector", ts)
        tv = p.emit("vector", lambda e: e.reciprocal(r_a, r_b), inc=True)
    b3 = p.bank()
    p.w("tensor", tv)
    t3 = p.emit("tensor", lambda e, b3=b3: e.matmul(p.banks[b3][:, 0:n], ones_row, r_a,
                                                    start=True, stop=True), inc=True)
    if kind == "ln":
        b4 = p.bank()
        t3 = p.emit("tensor", lambda e, b4=b4: e.matmul(p.banks[b4][:, 0:n], ones_row,
                                                        r_b, start=True, stop=True),
                    inc=True)
    p.w("vector", t3)
    ta = p.emit("vector", lambda e, b3=b3: e.tensor_copy(a_bc[:, 0:n],
                                                         p.banks[b3][:, 0:n]), inc=True)
    p.bank_done(b3, ta)
    if kind == "ln":
        ta = p.emit("vector", lambda e, b4=b4: e.tensor_copy(b_bc[:, 0:n],
                                                             p.banks[b4][:, 0:n]),
                    inc=True)
        p.bank_done(b4, ta)
    p.w("vector", ta)
    for kc in range(n_ch):
        if kind == "ln":
            t = p.emit("vector", lambda e, kc=kc: e.tensor_tensor(
                out_sb[:, kc, 0:n], x_sb[:, kc, 0:n], a_bc[:, 0:n], ALU.mult), inc=True)
            p.w("vector", t)
            t = p.emit("vector", lambda e, kc=kc: e.tensor_tensor(
                out_sb[:, kc, 0:n], out_sb[:, kc, 0:n], b_bc[:, 0:n], ALU.add),
                inc=True)
            if w_col is not None:
                p.w("vector", t)
                p.emit("vector", lambda e, kc=kc: e.tensor_scalar(
                    out_sb[:, kc, 0:n], out_sb[:, kc, 0:n],
                    w_col[:, kc:kc + 1], b_col[:, kc:kc + 1], ALU.mult, ALU.add),
                    inc=True)
        else:
            if w_col is None:
                p.emit("vector", lambda e, kc=kc: e.tensor_tensor(
                    out_sb[:, kc, 0:n], x_sb[:, kc, 0:n], a_bc[:, 0:n], ALU.mult),
                    inc=True)
            else:
                p.emit("vector", lambda e, kc=kc: e.scalar_tensor_tensor(
                    out_sb[:, kc, 0:n], x_sb[:, kc, 0:n], w_col[:, kc:kc + 1],
                    a_bc[:, 0:n], ALU.mult, ALU.mult), inc=True)
    return p.t("qv")


def mix_proj_pass(p, src_d, mix, wT, dst, sig, xc2, xm2, xsq, osb2, Tl,
                  evac_extra=None):
    ntt = Tl // TT
    use_tk = {}
    st_tk = {}
    mt = {}

    def prep(it, wi):
        t0 = it * TT
        p.w("sync", use_tk.get(wi))
        p.w("vector", use_tk.get(wi))
        tl1 = dma_in(p, xc2[wi], src_d[:, :, t0:t0 + TT].rearrange("c p t -> p c t"))
        if t0 == 0:
            tl2 = dma_in(p, xm2[wi][:, :, 1:TT],
                         src_d[:, :, 0:TT - 1].rearrange("c p t -> p c t"))
            p.w("vector", tl1)
            p.w("vector", tl2)
            p.emit("vector", lambda e, wi=wi: e.memset(xm2[wi][:, :, 0:1], 0.0),
                   inc=True)
        else:
            tl2 = dma_in(p, xm2[wi],
                         src_d[:, :, t0 - 1:t0 + TT - 1].rearrange("c p t -> p c t"))
            p.w("vector", tl1)
            p.w("vector", tl2)
        for kc in range(NCH):
            p.w("vector", p.t("qv"))
            p.emit("vector", lambda e, kc=kc, wi=wi: e.scalar_tensor_tensor(
                xsq[:, kc, :], xm2[wi][:, kc, :], mix[:, kc:kc + 1],
                xm2[wi][:, kc, :], ALU.mult, ALU.subtract), inc=True)
            p.w("vector", p.t("qv"))
            p.emit("vector", lambda e, kc=kc, wi=wi: e.tensor_scalar(
                xm2[wi][:, kc, :], xc2[wi][:, kc, :], mix[:, kc:kc + 1], None,
                ALU.mult), inc=True)
            p.w("vector", p.t("qv"))
            p.emit("vector", lambda e, kc=kc, wi=wi: e.tensor_tensor(
                xm2[wi][:, kc, :], xm2[wi][:, kc, :], xsq[:, kc, :], ALU.subtract),
                inc=True)
        return p.t("qv")

    mt[0] = prep(0, 0)
    for it in range(ntt):
        wi = it % 2
        if it + 1 < ntt:
            mt[1 - wi] = prep(it + 1, 1 - wi)
        p.w("tensor", mt[wi])

        def evac(m, b, pet, wi=wi, sig=sig):
            eng = "scalar" if sig else "vector"
            p.w(eng, pet)
            p.w(eng, st_tk.get(wi))
            if sig:
                t = p.emit("scalar", lambda e, m=m, b=b, wi=wi: e.activation(
                    osb2[wi][:, m, :], p.banks[b][:, 0:TT], AF.Sigmoid), inc=True)
            else:
                t = p.emit("vector", lambda e, m=m, b=b, wi=wi: e.tensor_copy(
                    osb2[wi][:, m, :], p.banks[b][:, 0:TT]), inc=True)
            return t
        proj_fm(p, evac, xm2[wi], wT, TT)
        use_tk[wi] = p.t("qt")
        p.w("sync", p.t("qv"))
        p.w("sync", p.t("qs"))
        t0 = it * TT
        st_tk[wi] = dma_out(p, dst[:, :, t0:t0 + TT].rearrange("c p t -> p c t"),
                            osb2[wi])
    p.barrier()


def emit_stage1(p, Tl=T):
    nc = p.nc
    p.push_scope()
    ntt = Tl // TT

    D = lambda name, shape, kind: p.dram(
        ("s1_" + name) if (kind == "Internal" or p.fused) else name, shape, kind)
    x0 = D("x0", [NCH, 128, Tl], "ExternalInput")
    Wk = D("Wk", [C, C], "ExternalInput")
    Wv = D("Wv", [C, C], "ExternalInput")
    Wr = D("Wr", [C, C], "ExternalInput")
    Wo = D("Wo", [C, C], "ExternalInput")
    vecs = D("vecs", [128, 8 * NCH], "ExternalInput")
    consts = D("consts", [128, 8], "ExternalInput")
    idc = D("identity", [128, 128], "ExternalInput")
    xn0 = D("xn0", [NCH, 128, Tl], "ExternalOutput")
    a0 = D("a0", [NCH, 128, Tl], "ExternalOutput")
    xt_d = D("xt_d", [NCH, 128, Tl], "Internal")
    K_d = D("K_d", [NCH, 128, Tl], "Internal")
    V_d = D("V_d", [NCH, 128, Tl], "Internal")
    R_d = D("R_d", [NCH, 128, Tl], "Internal")
    Z_d = D("Z_d", [NCH, 128, Tl], "Internal")

    A = lambda name, shape, dt=F32: p.sbuf("s1_" + name, shape, dt)
    ident = A("ident", [128, 128])
    ones_col = A("ones_col", [128, 1])
    ones_row = A("ones_row", [1, 128])
    sb_vec = A("sb_vec", [128, 8 * NCH])
    sb_cst = A("sb_cst", [128, 8])
    fcol = A("fcol", [128, NCH])
    wT = A("wT", [128, NCH, C])
    wnat = A("wnat", [128, NCH, C])
    xa = A("xa", [128, NCH, TT])
    xb = A("xb", [128, NCH, TT])
    xc = A("xc", [128, NCH, TT])
    xm = A("xm", [128, NCH, TT])
    xsq = A("xsq", [128, NCH, TT])
    osb = A("osb", [128, NCH, TT])
    a_bc = A("a_bc", [128, TT])
    b_bc = A("b_bc", [128, TT])
    rows = A("rows", [1, 2048])

    xcB = A("xcB", [128, NCH, TT])
    xmB = A("xmB", [128, NCH, TT])
    osbB = wnat[:, :, 0:TT]
    kc_t = xa.rearrange("p a b -> p (a b)")[:, 0:Tl]
    vc_t = xb.rearrange("p a b -> p (a b)")[:, 0:Tl]
    rc_t = xc.rearrange("p a b -> p (a b)")[:, 0:Tl]
    sc_t = xm.rearrange("p a b -> p (a b)")[:, 0:Tl]

    dma_in(p, ident, idc)
    t = dma_in(p, sb_vec, vecs)
    dma_in(p, sb_cst, consts)
    mixk = sb_vec[:, 0:NCH]
    mixv = sb_vec[:, NCH:2 * NCH]
    mixr = sb_vec[:, 2 * NCH:3 * NCH]
    tdec = sb_vec[:, 3 * NCH:4 * NCH]
    tfirst = sb_vec[:, 4 * NCH:5 * NCH]
    lnw = sb_vec[:, 5 * NCH:6 * NCH]
    lnb = sb_vec[:, 6 * NCH:7 * NCH]
    dcol = sb_vec[:, 7 * NCH:8 * NCH]
    eps_ln = sb_cst[:, 0:1]
    p.w("scalar", t)
    p.emit("scalar", lambda e: e.activation(dcol, tdec, AF.Sigmoid), inc=True)
    p.emit("scalar", lambda e: e.activation(fcol, tfirst, AF.Exp), inc=True)
    p.emit("vector", lambda e: e.memset(ones_col, 1.0), inc=True)
    p.emit("vector", lambda e: e.memset(ones_row, 1.0), inc=True)
    p.barrier()

    for it in range(ntt):
        t0 = it * TT
        dma_in(p, xa, x0[:, :, t0:t0 + TT].rearrange("c p t -> p c t"))
        p.barrier()
        tv = ln_tile(p, xa, TT, xsq, a_bc, b_bc, rows, ones_col, ones_row, eps_ln,
                     "ln", xb)
        p.w("sync", tv)
        dma_out(p, xn0[:, :, t0:t0 + TT].rearrange("c p t -> p c t"), xb)
        p.barrier()
        tv = ln_tile(p, xb, TT, xsq, a_bc, b_bc, rows, ones_col, ones_row, eps_ln,
                     "ln", xc, w_col=lnw, b_col=lnb)
        p.w("sync", tv)
        dma_out(p, xt_d[:, :, t0:t0 + TT].rearrange("c p t -> p c t"), xc)
        p.barrier()

    for (W_dram, mix, dst, sig) in ((Wk, mixk, K_d, False), (Wv, mixv, V_d, False),
                                    (Wr, mixr, R_d, True)):
        load_weight_T(p, W_dram, wT, wnat, ident)
        p.barrier()
        mix_proj_pass(p, xt_d, mix, wT, dst, sig, [xc, xcB], [xm, xmB], xsq,
                      [osb, osbB], Tl)

    load_weight_T(p, Wo, wT, wnat, ident)
    for kc in range(NCH):
        dma_in(p, kc_t, K_d[kc])
        dma_in(p, vc_t, V_d[kc])
        dma_in(p, rc_t, R_d[kc])
        p.barrier()
        t = p.emit("vector", lambda e, kc=kc: e.tensor_tensor_scan(
            sc_t, dcol[:, kc:kc + 1].to_broadcast([128, Tl]), kc_t,
            0.0, ALU.mult, ALU.add), inc=True)
        p.w("vector", t)
        t = p.emit("vector", lambda e, kc=kc: e.tensor_scalar(
            sc_t, sc_t, fcol[:, kc:kc + 1], None, ALU.mult), inc=True)
        p.w("vector", t)
        t = p.emit("vector", lambda e: e.tensor_tensor(sc_t, sc_t, kc_t, ALU.add),
                   inc=True)
        p.w("vector", t)
        t = p.emit("vector", lambda e: e.tensor_tensor(sc_t, sc_t, vc_t, ALU.mult),
                   inc=True)
        p.w("vector", t)
        tv = p.emit("vector", lambda e: e.tensor_tensor(sc_t, sc_t, rc_t, ALU.mult),
                    inc=True)
        p.w("sync", tv)
        dma_out(p, Z_d[kc], sc_t)
        p.barrier()

    p.barrier()
    for it in range(ntt):
        t0 = it * TT
        dma_in(p, xa, Z_d[:, :, t0:t0 + TT].rearrange("c p t -> p c t"))
        dma_in(p, xb, xn0[:, :, t0:t0 + TT].rearrange("c p t -> p c t"))
        p.barrier()

        def evac(m, b, pet):
            p.w("vector", pet)
            return p.emit("vector", lambda e, m=m, b=b: e.tensor_tensor(
                osb[:, m, :], p.banks[b][:, 0:TT], xb[:, m, :], ALU.add), inc=True)
        proj_fm(p, evac, xa, wT, TT)
        p.barrier(("sync",))
        dma_out(p, a0[:, :, t0:t0 + TT].rearrange("c p t -> p c t"), osb)
        p.barrier()

    p.barrier()
    p.pop_scope()


def emit_dec_qkv(p, pfx, S):
    nc = p.nc
    p.push_scope()
    ntt = S // TT

    D = lambda n, s, k: p.dram(pfx + n, s, k)
    hs = D("hs", [NCH, 128, S], "ExternalInput")
    wq = D("wq", [C, C], "ExternalInput")
    wk = D("wk", [C, C], "ExternalInput")
    wv = D("wv", [C, C], "ExternalInput")
    vec = D("vec", [128, NCH], "ExternalInput")
    cst = D("cst", [128, 8], "ExternalInput")
    cos_d = D("cos", [S, 32], "ExternalInput")
    sin_d = D("sin", [S, 32], "ExternalInput")
    QT = D("QT", [NCH, 128, S], "ExternalOutput")
    KT = D("KT", [NCH, 128, S], "ExternalOutput")
    Vtok = D("Vtok", [S, C], "ExternalOutput")
    idc = D("identity", [128, 128], "ExternalInput")

    A = lambda n, s, dt=F32: p.sbuf(pfx + n, s, dt)
    ident = A("ident", [128, 128])
    ones_col = A("ones_col", [128, 1])
    ones_row = A("ones_row", [1, 128])
    sb_vec = A("sb_vec", [128, NCH])
    sb_cst = A("sb_cst", [128, 8])
    w1 = A("w1", [128, NCH, C])
    w2 = A("w2", [128, NCH, C])
    w3 = A("w3", [128, NCH, C])
    xa = A("xa", [128, NCH, TT])
    xh = A("xh", [128, NCH, TT])
    xsq = A("xsq", [128, NCH, TT])
    ptile = A("ptile", [128, C])
    ptile2 = A("ptile2", [128, C])
    ptile3 = A("ptile3", [128, 512])
    oT = A("oT", [128, NCH, TT])
    cs = A("cs", [128, 64])
    a_bc = A("a_bc", [128, TT])
    b_bc = A("b_bc", [128, TT])
    rows = A("rows", [1, 2048])

    dma_in(p, ident, idc)
    t = dma_in(p, sb_vec, vec)
    dma_in(p, sb_cst, cst)
    dma_in(p, w1, wq.rearrange("(a p) b -> p a b", p=128))
    dma_in(p, w2, wk.rearrange("(a p) b -> p a b", p=128))
    dma_in(p, w3, wv.rearrange("(a p) b -> p a b", p=128))
    p.emit("vector", lambda e: e.memset(ones_col, 1.0), inc=True)
    p.emit("vector", lambda e: e.memset(ones_row, 1.0), inc=True)
    p.barrier()
    eps_rms = sb_cst[:, 1:2]

    for it in range(ntt):
        t0 = it * TT
        dma_in(p, xa, hs[:, :, t0:t0 + TT].rearrange("c p t -> p c t"))
        p.barrier()
        ln_tile(p, xa, TT, xsq, a_bc, b_bc, rows, ones_col, ones_row, eps_rms,
                "rms", xh, w_col=sb_vec)
        p.barrier()
        for wi, (wsb, dst, rope) in enumerate(((w1, QT, True), (w2, KT, True),
                                               (w3, Vtok, False))):
            for st in range(TT // 128):
                tok0 = t0 + st * 128
                p.w("vector", p.last_store)
                for half in range(2):
                    b = p.bank()
                    for kc in range(NCH):
                        tpe = p.emit("tensor",
                                     lambda e, b=b, kc=kc, st=st, half=half, wsb=wsb:
                                     e.matmul(p.banks[b][:, 0:512],
                                              xh[:, kc, st * 128:(st + 1) * 128],
                                              wsb[:, kc, half * 512:(half + 1) * 512],
                                              start=(kc == 0), stop=(kc == NCH - 1)),
                                     inc=(kc == NCH - 1))
                    p.w("vector", tpe)
                    tev = p.emit("vector", lambda e, b=b, half=half: e.tensor_copy(
                        ptile[:, half * 512:(half + 1) * 512], p.banks[b][:, 0:512]),
                        inc=True)
                    p.bank_done(b, tev)
                src = ptile
                if rope:
                    p.w("sync", p.t("qv"))
                    dma_in(p, cs[:, 0:32], cos_d[tok0:tok0 + 128, :])
                    dma_in(p, cs[:, 32:64], sin_d[tok0:tok0 + 128, :])
                    p.barrier(("vector",))
                    v = ptile.rearrange("p (h d) -> p h d", h=16)
                    o = ptile2.rearrange("p (h d) -> p h d", h=16)
                    sq16 = ptile3.rearrange("p (h d) -> p h d", h=16)
                    cosb = cs[:, 0:32].rearrange("p (o t) -> p o t", o=1).to_broadcast([128, 16, 32])
                    sinb = cs[:, 32:64].rearrange("p (o t) -> p o t", o=1).to_broadcast([128, 16, 32])
                    p.emit("vector", lambda e: e.tensor_tensor(
                        o[:, :, 0:32], v[:, :, 0:32], cosb, ALU.mult), inc=True)
                    t2 = p.emit("vector", lambda e: e.tensor_tensor(
                        sq16, v[:, :, 32:64], sinb, ALU.mult), inc=True)
                    p.w("vector", t2)
                    p.emit("vector", lambda e: e.tensor_tensor(
                        o[:, :, 0:32], o[:, :, 0:32], sq16, ALU.subtract), inc=True)
                    t4 = p.emit("vector", lambda e: e.tensor_tensor(
                        o[:, :, 32:64], v[:, :, 32:64], cosb, ALU.mult), inc=True)
                    p.w("vector", t4)
                    t5 = p.emit("vector", lambda e: e.tensor_tensor(
                        sq16, v[:, :, 0:32], sinb, ALU.mult), inc=True)
                    p.w("vector", t5)
                    p.emit("vector", lambda e: e.tensor_tensor(
                        o[:, :, 32:64], o[:, :, 32:64], sq16, ALU.add), inc=True)
                    src = ptile2
                    p.barrier(("tensor",))
                if dst is Vtok:
                    p.barrier(("sync",))
                    p.last_store = dma_out(p, Vtok[tok0:tok0 + 128, :], src)
                else:
                    for j in range(NCH):
                        b = p.bank()
                        tpe = p.emit("tensor", lambda e, b=b, j=j, src=src: e.transpose(
                            p.banks[b][:, 0:128], src[:, j * 128:(j + 1) * 128], ident),
                            inc=True)
                        p.w("vector", tpe)
                        tev = p.emit("vector", lambda e, b=b, j=j, st=st: e.tensor_copy(
                            oT[:, j, st * 128:(st + 1) * 128], p.banks[b][:, 0:128]),
                            inc=True)
                        p.bank_done(b, tev)
            if dst is not Vtok:
                p.barrier(("sync",))
                p.last_store = dma_out(
                    p, dst[:, :, t0:t0 + TT].rearrange("c p t -> p c t"), oT)
            p.barrier()
    p.barrier()
    p.pop_scope()


def emit_dec_attn(p, pfx, S):
    nc = p.nc
    p.push_scope()
    nq = S // 128

    D = lambda n, s, k: p.dram(pfx + n, s, k)
    QT = D("QT", [NCH, 128, S], "ExternalInput")
    KT = D("KT", [NCH, 128, S], "ExternalInput")
    Vtok = D("Vtok", [S, C], "ExternalInput")
    maskn = D("maskn", [128, 128], "ExternalInput")
    cst = D("cst", [128, 8], "ExternalInput")
    idc = D("identity", [128, 128], "ExternalInput")
    Otok = D("Otok", [S, C], "ExternalOutput")

    A = lambda n, s, dt=F32: p.sbuf(pfx + n, s, dt)
    ident = A("ident", [128, 128])
    qp2 = [A(f"qp{i}", [128, S]) for i in range(2)]
    kp2 = [A(f"kp{i}", [128, S]) for i in range(2)]
    vp2 = [A(f"vp{i}", [128, S // 128, 128]) for i in range(2)]
    op = A("op", [128, S // 128, 128])
    arow = [A(f"arow{i}", [128, S]) for i in range(2)]
    msk = A("msk", [128, 128])
    at4 = A("at4", [128, 512])
    lsum = [A(f"lsum{i}", [128, 4]) for i in range(2)]
    sb_cst = A("sb_cst", [128, 8])

    dma_in(p, msk, maskn)
    dma_in(p, ident, idc)
    dma_in(p, sb_cst, cst)
    c125 = sb_cst[:, 2:3]

    def emit_qk(h2, qt, bi):
        qp, kp = qkv_cur[0], qkv_cur[1]
        hb = h2 * 64
        W = (qt + 1) * 128
        tev = None
        for s0 in range(0, W, 512):
            n = min(512, W - s0)
            b = p.bank()
            tpe = p.emit("tensor", lambda e, b=b, qt=qt, s0=s0, n=n, hb=hb:
                         e.matmul(p.banks[b][:, 0:n],
                                  qp[hb:hb + 64, qt * 128:(qt + 1) * 128],
                                  kp[hb:hb + 64, s0:s0 + n],
                                  start=True, stop=True), inc=True)
            p.w("vector", tpe)
            tev = p.emit("vector", lambda e, b=b, s0=s0, n=n, bi=bi: e.tensor_copy(
                arow[bi][:, s0:s0 + n], p.banks[b][:, 0:n]), inc=True)
            p.bank_done(b, tev)
        return tev

    def emit_softmax(qt, bi, tev):
        W = (qt + 1) * 128
        p.w("vector", tev)
        t = p.emit("vector", lambda e, qt=qt, W=W, bi=bi: e.tensor_tensor(
            arow[bi][:, qt * 128:W], arow[bi][:, qt * 128:W], msk, ALU.add),
            inc=True)
        p.w("vector", t)
        t = p.emit("vector", lambda e, W=W, bi=bi: e.reduce_max(
            lsum[bi][:, 1:2], arow[bi][:, 0:W], axis=AX.X, negate=True), inc=True)
        p.w("vector", t)
        t = p.emit("vector", lambda e, bi=bi: e.tensor_scalar(
            lsum[bi][:, 1:2], lsum[bi][:, 1:2], c125, None, ALU.mult), inc=True)
        p.w("scalar", t)
        te = p.emit("scalar", lambda e, W=W, bi=bi: e.activation(
            arow[bi][:, 0:W], arow[bi][:, 0:W], AF.Exp, bias=lsum[bi][:, 1:2],
            scale=c125, accum_out=lsum[bi][:, 0:1]), inc=True)
        p.w("vector", te)
        tl = p.emit("vector", lambda e, bi=bi: e.reciprocal(
            lsum[bi][:, 2:3], lsum[bi][:, 0:1]), inc=True)
        return te, tl

    def emit_av(h2, qt, bi, te, tl):
        vp = qkv_cur[2]
        hb = h2 * 64
        ob = p.bank7()
        nkt = qt + 1
        n_av = 0
        for g0 in range(0, nkt, 4):
            gn = min(4, nkt - g0)
            tb = p.bank()
            p.w("tensor", te)
            for i in range(gn):
                tpe = p.emit("tensor", lambda e, tb=tb, g0=g0, i=i, bi=bi:
                             e.transpose(
                                 p.banks[tb][:, i * 128:(i + 1) * 128],
                                 arow[bi][:, (g0 + i) * 128:(g0 + i + 1) * 128],
                                 ident), inc=(i == gn - 1))
            p.w("vector", tpe)
            tev = p.emit("vector", lambda e, tb=tb, gn=gn: e.tensor_copy(
                at4[:, 0:gn * 128], p.banks[tb][:, 0:gn * 128]), inc=True)
            p.bank_done(tb, tev)
            p.w("tensor", tev)
            for i in range(gn):
                p.emit("tensor", lambda e, ob=ob, g0=g0, i=i, hb=hb,
                       first=(n_av == 0 and i == 0):
                       e.matmul(p.banks[ob][:, 0:64],
                                at4[:, i * 128:(i + 1) * 128],
                                vp[:, g0 + i, hb:hb + 64],
                                start=first, stop=False,
                                skip_group_check=True),
                       inc=(i == gn - 1))
            n_av += gn
        tpe = p.t("qt")
        p.w("vector", tpe)
        p.w("vector", tl)
        tev = p.emit("vector", lambda e, ob=ob, qt=qt, hb=hb, bi=bi: e.tensor_scalar(
            op[:, qt, hb:hb + 64], p.banks[ob][:, 0:64], lsum[bi][:, 2:3], None,
            ALU.mult), inc=True)
        p.bank_done(ob, tev)

    use_j = {}
    ld_j = {}
    qkv_cur = [None, None, None]

    def ld_pair(j, pi):
        p.w("sync", use_j.get(pi))
        t1 = dma_in(p, qp2[pi], QT[j])
        t2 = dma_in(p, kp2[pi], KT[j])
        t3 = dma_in(p, vp2[pi], Vtok[:, j * 128:(j + 1) * 128].rearrange(
            "(c p) f -> p c f", p=128))
        ld_j[pi] = (t1, t2, t3)

    p.barrier()
    ld_pair(0, 0)
    ost = None
    for j in range(NCH):
        pi = j % 2
        if j + 1 < NCH:
            ld_pair(j + 1, 1 - pi)
        qkv_cur[0], qkv_cur[1], qkv_cur[2] = qp2[pi], kp2[pi], vp2[pi]
        for tt in ld_j[pi]:
            p.w("tensor", tt)
        p.w("vector", ost)
        jobs = [(h2, qt) for h2 in range(2) for qt in range(nq)]
        pend = None
        for ji, (h2, qt) in enumerate(jobs):
            bi = ji % 2
            tev = emit_qk(h2, qt, bi)
            if pend is not None:
                emit_av(*pend)
            te, tl = emit_softmax(qt, bi, tev)
            pend = (h2, qt, bi, te, tl)
        emit_av(*pend)
        use_j[pi] = p.t("qt")
        p.w("sync", p.t("qv"))
        ost = dma_out(p, Otok[:, j * 128:(j + 1) * 128].rearrange(
            "(c p) f -> p c f", p=128), op)
    p.barrier()
    p.barrier()
    p.pop_scope()


def emit_dec_tail(p, pfx, S):
    nc = p.nc
    p.push_scope()
    ntt = S // TT
    FB = 512
    NFB = 4096 // FB

    D = lambda n, s, k: p.dram(pfx + n, s, k)
    hs = D("hs", [NCH, 128, S], "ExternalInput")
    Otok = D("Otok", [S, C], "ExternalInput")
    wo = D("wo", [C, C], "ExternalInput")
    wg = D("wg", [C, 4 * C], "ExternalInput")
    wu = D("wu", [C, 4 * C], "ExternalInput")
    wd = D("wd", [4 * C, C], "ExternalInput")
    vec = D("vec", [128, NCH], "ExternalInput")
    cst = D("cst", [128, 8], "ExternalInput")
    idc = D("identity", [128, 128], "ExternalInput")
    proc = D("proc", [NCH, 128, S], "ExternalOutput")
    x2_d = D("x2_d", [NCH, 128, S], "Internal")

    A = lambda n, s, dt=F32: p.sbuf(pfx + n, s, dt)
    ident = A("ident", [128, 128])
    ones_col = A("ones_col", [128, 1])
    ones_row = A("ones_row", [1, 128])
    sb_vec = A("sb_vec", [128, NCH])
    sb_cst = A("sb_cst", [128, 8])
    xa = A("xa", [128, NCH, TT])
    xb = A("xb", [128, NCH, TT])
    xsq = A("xsq", [128, NCH, TT])
    osb = A("osb", [128, NCH, TT])
    gsb = A("gsb", [128, 4, TT])
    husb = A("husb", [128, 4, TT])
    a_bc = A("a_bc", [128, TT])
    b_bc = A("b_bc", [128, TT])
    rows = A("rows", [1, 2048])
    inner = ExitStack()
    Ai = lambda n, s, dt=F32: inner.enter_context(
        nc.sbuf_tensor(pfx + n, s, dt)).ap()
    wosb_b = Ai("wosbb", [128, NCH, C], BF16)
    wstg = Ai("wstg", [128, 2, C])
    xbb = Ai("xbb", [128, NCH, TT], BF16)

    dma_in(p, ident, idc)
    dma_in(p, sb_vec, vec)
    dma_in(p, sb_cst, cst)
    wo_r = wo.rearrange("(a p) b -> p a b", p=128)
    cwo = None
    for q in range(4):
        p.w("sync", cwo)
        tq = dma_in(p, wstg, wo_r[:, 2 * q:2 * q + 2, :])
        p.w("scalar", tq)
        cwo = p.emit("scalar", lambda e, q=q: e.activation(
            wosb_b[:, 2 * q:2 * q + 2, :], wstg[:, :, :], AF.Copy), inc=True)
    p.emit("vector", lambda e: e.memset(ones_col, 1.0), inc=True)
    p.emit("vector", lambda e: e.memset(ones_row, 1.0), inc=True)
    p.barrier()
    eps_rms = sb_cst[:, 1:2]

    otile = Ai("otile", [128, C])
    for it in range(ntt):
        t0 = it * TT
        dma_in(p, xa, hs[:, :, t0:t0 + TT].rearrange("c p t -> p c t"))
        for st in range(TT // 128):
            p.w("sync", p.t("qt"))
            tld = dma_in(p, otile, Otok[t0 + st * 128:t0 + st * 128 + 128, :])
            p.w("tensor", tld)
            for j in range(NCH):
                b = p.bank()
                tpe = p.emit("tensor", lambda e, b=b, j=j: e.transpose(
                    p.banks[b][:, 0:128], otile[:, j * 128:(j + 1) * 128], ident),
                    inc=True)
                p.w("vector", tpe)
                tev = p.emit("vector", lambda e, b=b, j=j, st=st: e.tensor_copy(
                    xbb[:, j, st * 128:(st + 1) * 128], p.banks[b][:, 0:128]),
                    inc=True)
                p.bank_done(b, tev)
        p.barrier(("tensor",))

        def evac(m, b, pet):
            p.w("vector", pet)
            return p.emit("vector", lambda e, m=m, b=b: e.tensor_tensor(
                osb[:, m, :], p.banks[b][:, 0:TT], xa[:, m, :], ALU.add), inc=True)
        proj_fm(p, evac, xbb, wosb_b, TT)
        p.barrier(("sync",))
        dma_out(p, x2_d[:, :, t0:t0 + TT].rearrange("c p t -> p c t"), osb)
        p.barrier()

    inner.close()
    wgb = [A(f"wgb{i}", [128, NCH, FB]) for i in range(2)]
    wub = [A(f"wub{i}", [128, NCH, FB]) for i in range(2)]
    wdb = [A(f"wdb{i}", [128, FB // 128, C]) for i in range(2)]
    wuse = {}
    wld = {}

    def ld_fb(fb, wi):
        f0 = fb * FB
        p.w("sync", wuse.get(wi))
        t1 = dma_in(p, wgb[wi], wg[:, f0:f0 + FB].rearrange("(a p) b -> p a b",
                                                            p=128))
        t2 = dma_in(p, wub[wi], wu[:, f0:f0 + FB].rearrange("(a p) b -> p a b",
                                                            p=128))
        t3 = dma_in(p, wdb[wi], wd[f0:f0 + FB, :].rearrange("(a p) b -> p a b",
                                                            p=128))
        wld[wi] = (t1, t2, t3)

    for it in range(ntt):
        t0 = it * TT
        dma_in(p, xa, x2_d[:, :, t0:t0 + TT].rearrange("c p t -> p c t"))
        p.barrier()
        ln_tile(p, xa, TT, xsq, a_bc, b_bc, rows, ones_col, ones_row, eps_rms,
                "rms", xb, w_col=sb_vec)
        p.barrier()
        ld_fb(0, 0)
        for fb in range(NFB):
            wi = fb % 2
            if fb + 1 < NFB:
                ld_fb(fb + 1, 1 - wi)
            for tt_ in wld[wi]:
                p.w("tensor", tt_)
            for m in range(FB // 128):
                b = p.bank()
                for kc in range(NCH):
                    tpe = p.emit("tensor", lambda e, b=b, m=m, kc=kc, wi=wi: e.matmul(
                        p.banks[b][:, 0:TT], wgb[wi][:, kc, m * 128:(m + 1) * 128],
                        xb[:, kc, 0:TT], start=(kc == 0), stop=(kc == NCH - 1)),
                        inc=(kc == NCH - 1))
                p.w("scalar", tpe)
                tev = p.emit("scalar", lambda e, b=b, m=m: e.activation(
                    gsb[:, m, :], p.banks[b][:, 0:TT], AF.Silu), inc=True)
                p.bank_done(b, tev)
            for m in range(FB // 128):
                b = p.bank()
                for kc in range(NCH):
                    tpe = p.emit("tensor", lambda e, b=b, m=m, kc=kc, wi=wi: e.matmul(
                        p.banks[b][:, 0:TT], wub[wi][:, kc, m * 128:(m + 1) * 128],
                        xb[:, kc, 0:TT], start=(kc == 0), stop=(kc == NCH - 1)),
                        inc=(kc == NCH - 1))
                p.w("vector", tpe)
                p.w("vector", p.t("qs"))
                tev = p.emit("vector", lambda e, b=b, m=m: e.tensor_tensor(
                    husb[:, m, :], p.banks[b][:, 0:TT], gsb[:, m, :], ALU.mult),
                    inc=True)
                p.bank_done(b, tev)
            p.barrier(("tensor",))
            for m in range(NCH):
                b = p.bank()
                for kf in range(FB // 128):
                    tpe = p.emit("tensor", lambda e, b=b, m=m, kf=kf, wi=wi: e.matmul(
                        p.banks[b][:, 0:TT], wdb[wi][:, kf, m * 128:(m + 1) * 128],
                        husb[:, kf, 0:TT], start=(kf == 0), stop=(kf == FB // 128 - 1)),
                        inc=(kf == FB // 128 - 1))
                p.w("vector", tpe)
                p.w("vector", p.t("qv"))
                if fb == 0:
                    tev = p.emit("vector", lambda e, b=b, m=m: e.tensor_tensor(
                        osb[:, m, :], p.banks[b][:, 0:TT], xa[:, m, :], ALU.add),
                        inc=True)
                else:
                    tev = p.emit("vector", lambda e, b=b, m=m: e.tensor_tensor(
                        osb[:, m, :], p.banks[b][:, 0:TT], osb[:, m, :], ALU.add),
                        inc=True)
                p.bank_done(b, tev)
            wuse[wi] = p.t("qt")
        p.barrier(("sync",))
        dma_out(p, proc[:, :, t0:t0 + TT].rearrange("c p t -> p c t"), osb)
        p.barrier()
    p.barrier()
    p.pop_scope()


def emit_stage3(p, Tl=T):
    nc = p.nc
    p.push_scope()
    ntt = Tl // TT

    D = lambda n, s, k: p.dram("s3_" + n, s, k)
    xn0 = D("xn0", [NCH, 128, Tl], "ExternalInput")
    proc = D("proc", [NCH, 128, Tl], "ExternalInput")
    pw = D("pw", [C, 2 * C], "ExternalInput")
    vec = D("vec", [128, NCH], "ExternalInput")
    cst = D("cst", [128, 8], "ExternalInput")
    idc = D("identity", [128, 128], "ExternalInput")
    dst_d = D("dst", [1, Tl], "ExternalOutput")
    dch_d = D("dch", [1, Tl], "ExternalOutput")
    parts = D("parts", [1, 2], "ExternalOutput")
    ptok = D("ptok", [Tl, C], "ExternalOutput")
    f0tok = D("f0tok", [Tl, C], "ExternalOutput")

    A = lambda n, s, dt=F32: p.sbuf("s3_" + n, s, dt)
    ident = A("ident", [128, 128])
    ones_col = A("ones_col", [128, 1])
    ones_row = A("ones_row", [1, 128])
    sb_vec = A("sb_vec", [128, NCH])
    sb_cst = A("sb_cst", [128, 8])
    pwb = A("pwb", [128, NCH, 2 * C], BF16)
    xa = A("xa", [128, NCH, TT])
    xb = A("xb", [128, NCH, TT])
    xc = A("xc", [128, NCH, TT])
    xcb = A("xcb", [128, NCH, TT], BF16)
    xd = A("xd", [128, NCH, TT])
    xsq = A("xsq", [128, NCH, TT])
    musb = A("musb", [128, NCH, TT])
    lvsb = A("lvsb", [128, NCH, TT])
    otile = A("otile", [128, C])
    a_bc = A("a_bc", [128, TT])
    b_bc = A("b_bc", [128, TT])
    rows = A("rows", [1, 2048])
    racc = A("racc", [1, 4])

    dma_in(p, ident, idc)
    dma_in(p, sb_vec, vec)
    dma_in(p, sb_cst, cst)
    pw_r = pw.rearrange("(a p) b -> p a b", p=128)
    cpw = None
    for q in range(4):
        p.w("sync", cpw)
        tq = dma_in(p, musb, pw_r[:, :, q * 512:(q + 1) * 512])
        p.w("scalar", tq)
        cpw = p.emit("scalar", lambda e, q=q: e.activation(
            pwb[:, :, q * 512:(q + 1) * 512], musb[:, :, :], AF.Copy), inc=True)
    p.emit("vector", lambda e: e.memset(ones_col, 1.0), inc=True)
    p.emit("vector", lambda e: e.memset(ones_row, 1.0), inc=True)
    p.emit("vector", lambda e: e.memset(racc, 0.0), inc=True)
    p.barrier()
    eps_rms = sb_cst[:, 1:2]
    c_st = sb_cst[:, 3:4]
    c_ch = sb_cst[:, 4:5]
    c_nh = sb_cst[:, 5:6]

    for it in range(ntt):
        t0 = it * TT
        dma_in(p, xa, xn0[:, :, t0:t0 + TT].rearrange("c p t -> p c t"))
        dma_in(p, xb, proc[:, :, t0:t0 + TT].rearrange("c p t -> p c t"))
        p.barrier()
        for st in range(TT // 128):
            for j in range(NCH):
                b = p.bank()
                tpe = p.emit("tensor", lambda e, b=b, j=j, st=st: e.transpose(
                    p.banks[b][:, 0:128], xb[:, j, st * 128:(st + 1) * 128], ident),
                    inc=True)
                p.w("vector", tpe)
                tev = p.emit("vector", lambda e, b=b, j=j: e.tensor_copy(
                    otile[:, j * 128:(j + 1) * 128], p.banks[b][:, 0:128]), inc=True)
                p.bank_done(b, tev)
            p.barrier(("sync",))
            t1 = dma_out(p, ptok[t0 + st * 128:t0 + st * 128 + 128, :], otile)
            t2 = dma_out(p, f0tok[t0 + st * 128:t0 + st * 128 + 128, :], otile)
            p.w("vector", t1)
            p.w("vector", t2)
        tvx = ln_tile(p, xa, TT, xsq, a_bc, b_bc, rows, ones_col, ones_row,
                      eps_rms, "rms", xc, w_col=sb_vec)
        p.barrier(("tensor",))
        p.w("scalar", p.t("qt"))
        p.w("scalar", tvx)
        tcx = p.emit("scalar", lambda e: e.activation(
            xcb[:, :, :], xc[:, :, :], AF.Copy), inc=True)
        p.w("tensor", tcx)
        for m in range(16):
            b = p.bank()
            for kc in range(NCH):
                tpe = p.emit("tensor", lambda e, b=b, m=m, kc=kc: e.matmul(
                    p.banks[b][:, 0:TT], pwb[:, kc, m * 128:(m + 1) * 128],
                    xcb[:, kc, 0:TT], start=(kc == 0), stop=(kc == NCH - 1)),
                    inc=(kc == NCH - 1))
            p.w("vector", tpe)
            dstt = musb if m < 8 else lvsb
            tev = p.emit("vector", lambda e, b=b, m=m, dstt=dstt: e.tensor_copy(
                dstt[:, m % 8, :], p.banks[b][:, 0:TT]), inc=True)
            p.bank_done(b, tev)
        for kc in range(NCH):
            p.emit("vector", lambda e, kc=kc: e.tensor_tensor(
                xd[:, kc, :], xb[:, kc, :], xa[:, kc, :], ALU.subtract), inc=True)
        tres = p.t("qv")
        p.w("scalar", tres)
        t = p.emit("scalar", lambda e: e.activation(xsq, xd, AF.Square), inc=True)
        b = p.bank()
        p.w("tensor", t)
        for kc in range(NCH):
            tpe = p.emit("tensor", lambda e, b=b, kc=kc: e.matmul(
                p.banks[b][0:1, 0:TT], ones_col, xsq[:, kc, 0:TT],
                start=(kc == 0), stop=(kc == NCH - 1)), inc=(kc == NCH - 1))
        p.w("vector", tpe)
        trow = p.emit("vector", lambda e, b=b: e.tensor_scalar(
            rows[0:1, 0:TT], p.banks[b][0:1, 0:TT], 1.0 / 1024.0, None, ALU.mult),
            inc=True)
        p.bank_done(b, trow)
        p.w("sync", trow)
        dma_out(p, dst_d[:, t0:t0 + TT], rows[0:1, 0:TT])
        p.w("vector", trow)
        t = p.emit("vector", lambda e: e.reduce_sum(rows[0:1, 1024:1025],
                                                    rows[0:1, 0:TT], axis=AX.X),
                   inc=True)
        p.w("vector", t)
        t = p.emit("vector", lambda e: e.tensor_tensor(
            racc[0:1, 0:1], racc[0:1, 0:1], rows[0:1, 1024:1025], ALU.add), inc=True)
        for kc in range(NCH):
            tt1 = p.emit("vector", lambda e, kc=kc: e.tensor_tensor(
                musb[:, kc, :], xd[:, kc, :], musb[:, kc, :], ALU.subtract), inc=True)
        tdm = p.t("qv")
        p.w("scalar", tdm)
        t = p.emit("scalar", lambda e: e.activation(xsq, musb, AF.Square), inc=True)
        p.w("scalar", t)
        t = p.emit("scalar", lambda e: e.activation(musb, lvsb, AF.Copy, scale=-1.0),
                   inc=True)
        p.w("scalar", t)
        t = p.emit("scalar", lambda e: e.activation(musb, musb, AF.Exp), inc=True)
        p.w("vector", t)
        for kc in range(NCH):
            t = p.emit("vector", lambda e, kc=kc: e.tensor_scalar(
                xsq[:, kc, :], xsq[:, kc, :], 1.0, None, ALU.add), inc=True)
            p.w("vector", t)
            t = p.emit("vector", lambda e, kc=kc: e.tensor_tensor(
                xsq[:, kc, :], xsq[:, kc, :], musb[:, kc, :], ALU.mult), inc=True)
            p.w("vector", t)
            t = p.emit("vector", lambda e, kc=kc: e.tensor_tensor(
                xsq[:, kc, :], xsq[:, kc, :], lvsb[:, kc, :], ALU.add), inc=True)
        tch = p.t("qv")
        b = p.bank()
        p.w("tensor", tch)
        for kc in range(NCH):
            tpe = p.emit("tensor", lambda e, b=b, kc=kc: e.matmul(
                p.banks[b][0:1, 0:TT], ones_col, xsq[:, kc, 0:TT],
                start=(kc == 0), stop=(kc == NCH - 1)), inc=(kc == NCH - 1))
        p.w("vector", tpe)
        trow = p.emit("vector", lambda e, b=b: e.tensor_scalar(
            rows[0:1, 512:512 + TT], p.banks[b][0:1, 0:TT], 0.5 / 1024.0, -0.5,
            ALU.mult, ALU.add), inc=True)
        p.bank_done(b, trow)
        p.w("sync", trow)
        dma_out(p, dch_d[:, t0:t0 + TT], rows[0:1, 512:512 + TT])
        p.w("vector", trow)
        t = p.emit("vector", lambda e: e.reduce_sum(rows[0:1, 1030:1031],
                                                    rows[0:1, 512:512 + TT],
                                                    axis=AX.X), inc=True)
        p.w("vector", t)
        t = p.emit("vector", lambda e: e.tensor_tensor(
            racc[0:1, 1:2], racc[0:1, 1:2], rows[0:1, 1030:1031], ALU.add), inc=True)
        p.barrier()
    p.w("sync", p.t("qv"))
    dma_out(p, parts, racc[0:1, 0:2])
    p.barrier()
    p.pop_scope()


def emit_stage4(p, Tl=T):
    nc = p.nc
    p.push_scope()
    K = Tl // 4
    NT = Tl // 128
    KC = K // 128

    D = lambda n, s, k, dt=F32: p.dram("s4_" + n, s, k, dt)
    dst_d = D("dst", [1, Tl], "ExternalInput")
    dch_d = D("dch", [1, Tl], "ExternalInput")
    if p.fused:
        parts = p.dram("G_ccdst", [128, 8], "Internal")
    else:
        parts = D("parts", [NCORES, 2], "ExternalInput")
    ptok = D("ptok", [Tl, C], "ExternalInput")
    cos_d = D("cos", [Tl, 32], "ExternalInput")
    sin_d = D("sin", [Tl, 32], "ExternalInput")
    iosb_d = D("iota", [128, Tl], "ExternalInput")
    ivals_d = D("ivals", [128, NT], "ExternalInput")
    prem_d = D("prem", [128, 128], "ExternalInput")
    cst = D("cst", [128, 8], "ExternalInput")
    idc = D("identity", [128, 128], "ExternalInput")

    g2_d = D("g2", [128, NT], "Internal")
    isc_d = D("isc", [K, 8], "ExternalOutput")
    seltok = D("seltok", [K, C], "ExternalOutput")
    selfm = D("selfm", [NCH, 128, K], "ExternalOutput")
    cossel = D("cossel", [K, 32], "ExternalOutput")
    sinsel = D("sinsel", [K, 32], "ExternalOutput")

    A = lambda n, s, dt=F32: p.sbuf("s4_" + n, s, dt)
    ident = A("ident", [128, 128])
    ones_col = A("ones_col", [128, 1])
    ones_row = A("ones_row", [1, 128])
    sb_cst = A("sb_cst", [128, 8])
    grow = A("grow", [1, Tl])
    drow = A("drow", [1, Tl])
    gbc = A("gbc", [128, Tl])
    iobc = A("iobc", [128, Tl])
    ivals = A("ivals_sb", [128, NT])
    prem = A("prem_sb", [128, 128])
    gcol = A("gcol", [128, NT])
    rank = A("rank", [128, NT])
    sel = A("sel", [128, NT])
    slot = A("slot", [128, NT])
    s2 = A("s2", [128, NT])
    tmp1 = A("tmp1", [128, Tl])
    tmp2 = A("tmp2", [128, Tl])
    tmp3 = A("tmp3", [128, Tl])
    psb = A("psb", [NCORES, 2])
    mrow = A("mrow", [1, 2])
    u32c = A("u32c", [128, NT], U32)
    idxu = A("idxu", [128, KC], U32)
    combo = A("combo", [128, NT, 8])
    iscsb = A("iscsb", [128, KC, 8])
    stile = A("stile", [128, C])
    sfmt = A("sfmt", [128, NCH, 128])
    cstile = A("cstile", [128, 64])

    dma_in(p, ident, idc)
    dma_in(p, sb_cst, cst)
    dma_in(p, grow, dst_d)
    dma_in(p, drow, dch_d)
    if p.fused:
        dma_in(p, mrow, parts[0:1, 0:2])
    else:
        dma_in(p, psb, parts)
    dma_in(p, iobc, iosb_d)
    dma_in(p, ivals, ivals_d)
    dma_in(p, prem, prem_d)
    p.emit("vector", lambda e: e.memset(ones_col, 1.0), inc=True)
    p.emit("vector", lambda e: e.memset(ones_row, 1.0), inc=True)
    p.barrier()
    cK = sb_cst[:, 6:7]
    cBIG = sb_cst[:, 7:8]

    if p.fused:
        t = p.emit("vector", lambda e: e.tensor_scalar(
            mrow, mrow, 1.0 / (NCORES * Tl), None, ALU.mult), inc=True)
        p.w("vector", t)
    else:
        b = p.bank()
        t = p.emit("tensor", lambda e, b=b: e.matmul(p.banks[b][0:1, 0:2],
                                                     ones_col[0:8, :], psb,
                                                     start=True, stop=True), inc=True)
        p.w("vector", t)
        t = p.emit("vector", lambda e, b=b: e.tensor_scalar(
            mrow, p.banks[b][0:1, 0:2], 1.0 / (NCORES * Tl), None, ALU.mult),
            inc=True)
        p.bank_done(b, t)
        p.w("vector", t)
    t = p.emit("vector", lambda e: e.tensor_tensor(
        mrow[0:1, 0:1], mrow[0:1, 0:1], mrow[0:1, 1:2], ALU.add), inc=True)
    p.w("vector", t)
    t = p.emit("vector", lambda e: e.tensor_tensor(grow, grow, drow, ALU.add),
               inc=True)
    p.w("vector", t)
    t = p.emit("vector", lambda e: e.tensor_scalar(
        grow, grow, mrow[0:1, 0:1], None, ALU.subtract), inc=True)
    p.w("scalar", t)
    t = p.emit("scalar", lambda e: e.activation(grow, grow, AF.Sigmoid), inc=True)
    p.w("sync", t)
    tg = dma_out(p, g2_d.rearrange("p t -> (p t)").rearrange("(o f) -> o f", o=1), grow)
    p.w("sync", tg)
    tgl = dma_in(p, gcol, g2_d)
    p.w("tensor", t)
    for s0 in range(0, Tl, 512):
        b = p.bank()
        tpe = p.emit("tensor", lambda e, b=b, s0=s0: e.matmul(
            p.banks[b][:, 0:512], ones_row, grow[0:1, s0:s0 + 512],
            start=True, stop=True), inc=True)
        p.w("vector", tpe)
        tev = p.emit("vector", lambda e, b=b, s0=s0: e.tensor_copy(
            gbc[:, s0:s0 + 512], p.banks[b][:, 0:512]), inc=True)
        p.bank_done(b, tev)
    p.w("vector", tgl)
    p.barrier(("vector",))
    for tc in range(NT):
        t1 = p.emit("vector", lambda e, tc=tc: e.tensor_scalar(
            tmp1, gbc, gcol[:, tc:tc + 1], None, ALU.is_gt), inc=True)
        t2 = p.emit("vector", lambda e, tc=tc: e.tensor_scalar(
            tmp2, gbc, gcol[:, tc:tc + 1], None, ALU.is_equal), inc=True)
        t3 = p.emit("vector", lambda e, tc=tc: e.tensor_scalar(
            tmp3, iobc, ivals[:, tc:tc + 1], None, ALU.is_lt), inc=True)
        p.w("vector", t3)
        t4 = p.emit("vector", lambda e: e.tensor_tensor(tmp2, tmp2, tmp3, ALU.mult),
                    inc=True)
        p.w("vector", t4)
        t5 = p.emit("vector", lambda e: e.tensor_tensor(tmp1, tmp1, tmp2, ALU.add),
                    inc=True)
        p.w("vector", t5)
        t6 = p.emit("vector", lambda e, tc=tc: e.reduce_sum(
            rank[:, tc:tc + 1], tmp1, axis=AX.X), inc=True)
        p.w("vector", t6)
    t = p.emit("vector", lambda e: e.tensor_scalar(sel, rank, cK, None, ALU.is_lt),
               inc=True)
    p.w("vector", t)
    t = p.emit("vector", lambda e: e.tensor_tensor_scan(
        slot, ones_col.to_broadcast([128, NT]), sel, 0.0, ALU.mult, ALU.add), inc=True)
    p.w("vector", t)
    p.w("tensor", t)
    b = p.bank()
    tpe = p.emit("tensor", lambda e, b=b: e.matmul(
        p.banks[b][:, 0:1], prem, slot[:, NT - 1:NT], start=True, stop=True), inc=True)
    p.w("vector", tpe)
    toffs = p.emit("vector", lambda e, b=b: e.tensor_copy(
        s2[:, 0:1], p.banks[b][:, 0:1]), inc=True)
    p.bank_done(b, toffs)
    p.w("vector", toffs)
    t = p.emit("vector", lambda e: e.tensor_tensor(slot, slot, sel, ALU.subtract),
               inc=True)
    p.w("vector", t)
    t = p.emit("vector", lambda e: e.tensor_scalar(
        slot, slot, s2[:, 0:1], None, ALU.add), inc=True)
    p.w("vector", t)
    t = p.emit("vector", lambda e: e.tensor_scalar(
        tmp1[:, 0:NT], slot, 128.0, None, ALU.is_ge), inc=True)
    for j in range(2, KC):
        p.w("vector", t)
        t = p.emit("vector", lambda e, j=j: e.tensor_scalar(
            tmp2[:, 0:NT], slot, float(j) * 128.0, None, ALU.is_ge), inc=True)
        p.w("vector", t)
        t = p.emit("vector", lambda e: e.tensor_tensor(
            tmp1[:, 0:NT], tmp1[:, 0:NT], tmp2[:, 0:NT], ALU.add), inc=True)
    p.w("vector", t)
    t = p.emit("vector", lambda e: e.scalar_tensor_tensor(
        s2, tmp1[:, 0:NT], -128.0, slot, ALU.mult, ALU.add), inc=True)
    p.w("vector", t)
    t = p.emit("vector", lambda e: e.scalar_tensor_tensor(
        s2, s2, float(KC), tmp1[:, 0:NT], ALU.mult, ALU.add), inc=True)
    p.w("vector", t)
    t = p.emit("vector", lambda e: e.tensor_scalar(
        tmp1[:, 0:NT], sel, -1.0, 1.0, ALU.mult, ALU.add), inc=True)
    p.w("vector", t)
    t = p.emit("vector", lambda e: e.tensor_scalar(
        tmp1[:, 0:NT], tmp1[:, 0:NT], cBIG, None, ALU.mult), inc=True)
    p.w("vector", t)
    t = p.emit("vector", lambda e: e.tensor_tensor(s2, s2, sel, ALU.mult), inc=True)
    p.w("vector", t)
    t = p.emit("vector", lambda e: e.tensor_tensor(s2, s2, tmp1[:, 0:NT], ALU.add),
               inc=True)
    p.w("vector", t)
    tu = p.emit("vector", lambda e: e.tensor_copy(u32c, s2), inc=True)
    p.w("gpsimd", tu)
    t = p.emit("vector", lambda e: e.memset(combo, 0.0), inc=True)
    p.w("vector", t)
    t = p.emit("vector", lambda e: e.tensor_copy(combo[:, :, 0:1],
                                                 ivals.rearrange(
                                                     "p (t o) -> p t o", o=1)),
               inc=True)
    p.w("vector", t)
    t = p.emit("vector", lambda e: e.tensor_copy(combo[:, :, 1:2],
                                                 gcol.rearrange(
                                                     "p (t o) -> p t o", o=1)),
               inc=True)
    p.w("gpsimd", t)
    tprev = None
    for tc in range(NT):
        p.w("gpsimd", tprev)
        tprev = p.dma("gpsimd", lambda e, tc=tc: e.indirect_dma_start(
            out=isc_d, out_offset=bass.IndirectOffsetOnAxis(u32c[:, tc:tc + 1], 0),
            in_=combo[:, tc, :], in_offset=None,
            bounds_check=K - 1, oob_is_err=False))
    p.barrier()
    dma_in(p, iscsb, isc_d.rearrange("(p c) e -> p c e", p=128))
    p.barrier(("vector",))
    tu = p.emit("vector", lambda e: e.tensor_copy(
        idxu, iscsb[:, :, 0:1].rearrange("p c o -> p (c o)")), inc=True)
    p.w("gpsimd", tu)
    for c2 in range(KC):
        tg = p.dma("gpsimd", lambda e, c2=c2: e.indirect_dma_start(
            out=stile, out_offset=None, in_=ptok,
            in_offset=bass.IndirectOffsetOnAxis(idxu[:, c2:c2 + 1], 0)))
        p.w("sync", tg)
        dma_out(p, seltok[c2 * 128:(c2 + 1) * 128, :], stile)
        p.w("tensor", tg)
        for j in range(NCH):
            b = p.bank()
            tpe = p.emit("tensor", lambda e, b=b, j=j: e.transpose(
                p.banks[b][:, 0:128], stile[:, j * 128:(j + 1) * 128], ident),
                inc=True)
            p.w("vector", tpe)
            tev = p.emit("vector", lambda e, b=b, j=j: e.tensor_copy(
                sfmt[:, j, :], p.banks[b][:, 0:128]), inc=True)
            p.bank_done(b, tev)
        p.barrier(("sync",))
        dma_out(p, selfm[:, :, c2 * 128:(c2 + 1) * 128].rearrange("c p t -> p c t"),
                sfmt)
        tg = p.dma("gpsimd", lambda e, c2=c2: e.indirect_dma_start(
            out=cstile[:, 0:32], out_offset=None, in_=cos_d,
            in_offset=bass.IndirectOffsetOnAxis(idxu[:, c2:c2 + 1], 0)))
        tg2 = p.dma("gpsimd", lambda e, c2=c2: e.indirect_dma_start(
            out=cstile[:, 32:64], out_offset=None, in_=sin_d,
            in_offset=bass.IndirectOffsetOnAxis(idxu[:, c2:c2 + 1], 0)))
        p.w("sync", tg)
        p.w("sync", tg2)
        dma_out(p, cossel[c2 * 128:(c2 + 1) * 128, :], cstile[:, 0:32])
        tls = dma_out(p, sinsel[c2 * 128:(c2 + 1) * 128, :], cstile[:, 32:64])
        p.w("gpsimd", tls)
        p.barrier(("gpsimd",))
    p.barrier()
    p.pop_scope()


def emit_stage5(p, Tl=T):
    nc = p.nc
    p.push_scope()
    K = Tl // 4
    KC = K // 128

    D = lambda n, s, k, dt=F32: p.dram("s5_" + n, s, k, dt)
    proc2 = D("proc2", [NCH, 128, K], "ExternalInput")
    seltok = D("seltok", [K, C], "ExternalInput")
    isc_d = D("isc", [K, 8], "ExternalInput")
    f0tok = D("f0tok", [Tl, C], "ExternalInput")
    x0 = D("x0", [NCH, 128, Tl], "ExternalInput")
    a0 = D("a0", [NCH, 128, Tl], "ExternalInput")
    idc = D("identity", [128, 128], "ExternalInput")
    x1_d = D("x1", [NCH, 128, Tl], "ExternalOutput")
    f0w = D("f0w", [Tl, C], "Internal")

    A = lambda n, s, dt=F32: p.sbuf("s5_" + n, s, dt)
    ident = A("ident", [128, 128])
    ytile = A("ytile", [128, C])
    stile = A("stile", [128, C])
    ptile = A("ptile", [128, NCH, 128])
    iscsb = A("iscsb", [128, KC, 8])
    idxu = A("idxu", [128, KC], U32)
    xa = A("xa", [128, NCH, TT])
    xb = A("xb", [128, NCH, TT])
    xc = A("xc", [128, NCH, TT])

    dma_in(p, ident, idc)
    dma_in(p, iscsb, isc_d.rearrange("(p c) e -> p c e", p=128))
    p.barrier()
    tu = p.emit("vector", lambda e: e.tensor_copy(
        idxu, iscsb[:, :, 0:1].rearrange("p c o -> p (c o)")), inc=True)
    Q4 = Tl // 4
    for c2 in range(4):
        p.dma("sync", lambda e, c2=c2: e.dma_start(
            out=f0w[c2 * Q4:(c2 + 1) * Q4, :], in_=f0tok[c2 * Q4:(c2 + 1) * Q4, :]))
    p.barrier()
    for c2 in range(KC):
        dma_in(p, ptile, proc2[:, :, c2 * 128:(c2 + 1) * 128].rearrange(
            "c p t -> p c t"))
        tld = dma_in(p, stile, seltok[c2 * 128:(c2 + 1) * 128, :])
        p.barrier(("tensor", "vector"))
        for j in range(NCH):
            b = p.bank()
            tpe = p.emit("tensor", lambda e, b=b, j=j: e.transpose(
                p.banks[b][:, 0:128], ptile[:, j, :], ident), inc=True)
            p.w("vector", tpe)
            tev = p.emit("vector", lambda e, b=b, j=j: e.tensor_copy(
                ytile[:, j * 128:(j + 1) * 128], p.banks[b][:, 0:128]), inc=True)
            p.bank_done(b, tev)
        tev = p.t("qv")
        p.w("vector", tev)
        t = p.emit("vector", lambda e: e.tensor_tensor(ytile, ytile, stile,
                                                       ALU.subtract), inc=True)
        p.w("vector", t)
        t = p.emit("vector", lambda e, c2=c2: e.tensor_scalar(
            ytile, ytile, iscsb[:, c2, 1:2], None, ALU.mult), inc=True)
        p.w("vector", t)
        t = p.emit("vector", lambda e: e.tensor_tensor(ytile, ytile, stile, ALU.add),
                   inc=True)
        p.w("gpsimd", t)
        tsc = p.dma("gpsimd", lambda e, c2=c2: e.indirect_dma_start(
            out=f0w, out_offset=bass.IndirectOffsetOnAxis(idxu[:, c2:c2 + 1], 0),
            in_=ytile, in_offset=None))
        p.w("sync", tsc)
        p.w("vector", tsc)
        p.barrier(("gpsimd",))
    p.barrier()
    for it in range(Tl // TT):
        t0 = it * TT
        dma_in(p, xa, x0[:, :, t0:t0 + TT].rearrange("c p t -> p c t"))
        dma_in(p, xb, a0[:, :, t0:t0 + TT].rearrange("c p t -> p c t"))
        p.barrier()
        for st in range(TT // 128):
            tld = dma_in(p, ytile, f0w[t0 + st * 128:t0 + st * 128 + 128, :])
            p.w("tensor", tld)
            for j in range(NCH):
                b = p.bank()
                tpe = p.emit("tensor", lambda e, b=b, j=j: e.transpose(
                    p.banks[b][:, 0:128], ytile[:, j * 128:(j + 1) * 128], ident),
                    inc=True)
                p.w("vector", tpe)
                tev = p.emit("vector", lambda e, b=b, j=j, st=st: e.tensor_copy(
                    xc[:, j, st * 128:(st + 1) * 128], p.banks[b][:, 0:128]),
                    inc=True)
                p.bank_done(b, tev)
            p.w("sync", p.t("qt"))
        tev = p.t("qv")
        p.w("vector", tev)
        for kc in range(NCH):
            t = p.emit("vector", lambda e, kc=kc: e.tensor_tensor(
                xa[:, kc, :], xa[:, kc, :], xb[:, kc, :], ALU.add), inc=True)
            p.w("vector", t)
            t = p.emit("vector", lambda e, kc=kc: e.tensor_tensor(
                xa[:, kc, :], xa[:, kc, :], xc[:, kc, :], ALU.add), inc=True)
        p.w("sync", p.t("qv"))
        dma_out(p, x1_d[:, :, t0:t0 + TT].rearrange("c p t -> p c t"), xa)
        p.barrier()
    p.barrier()
    p.pop_scope()


SC = 16


def emit_stage6(p, Tl=T):
    nc = p.nc
    p.push_scope()
    ntt = Tl // TT
    ntt128 = Tl // 128

    D = lambda n, s, k: p.dram("s6_" + n, s, k)
    x1 = D("x1", [NCH, 128, Tl], "ExternalInput")
    Wr = D("Wr", [C, C], "ExternalInput")
    Wk = D("Wk", [C, C], "ExternalInput")
    Wv = D("Wv", [C, C], "ExternalInput")
    Ww = D("Ww", [C, C], "ExternalInput")
    Wg = D("Wg", [C, C], "ExternalInput")
    Wo = D("Wo", [C, C], "ExternalInput")
    vecs = D("vecs", [128, 8 * NCH], "ExternalInput")
    cst = D("cst", [128, 8], "ExternalInput")
    idc = D("identity", [128, 128], "ExternalInput")
    rmask_d = D("rmask", [128, Tl], "ExternalInput")
    bar_d = D("bar", [128, Tl], "ExternalInput")
    bmask_d = D("bmask", [128, 128], "ExternalInput")
    xn1 = D("xn1", [NCH, 128, Tl], "ExternalOutput")
    a1 = D("a1", [NCH, 128, Tl], "ExternalOutput")
    xl_d = D("xl_d", [NCH, 128, Tl], "Internal")
    r_d = D("r_d", [NCH, 128, Tl], "Internal")
    k_d = D("k_d", [NCH, 128, Tl], "Internal")
    v_d = D("v_d", [NCH, 128, Tl], "Internal")
    w_d = D("w_d", [NCH, 128, Tl], "Internal")
    g_d = D("g_d", [NCH, 128, Tl], "Internal")
    e_d = D("e_d", [NCH, 128, Tl], "Internal")
    C_d = D("C_d", [NCH, 128, Tl], "Internal")
    rt_d = D("rt_d", [NCH, 128, Tl], "Internal")
    kh_d = D("kh_d", [NCH, 128, Tl], "Internal")
    dE_d = D("dE_d", [NCH, 128, Tl // SC], "Internal")
    khtok = D("khtok", [Tl, C], "Internal")
    vtok = D("vtok", [Tl, C], "Internal")
    gy_d = D("gy_d", [NCH, 128, Tl], "Internal")

    A = lambda n, s, dt=F32: p.sbuf("s6_" + n, s, dt)
    ident = A("ident", [128, 128])
    ones_col = A("ones_col", [128, 1])
    ones_row = A("ones_row", [1, 128])
    sb_vec = A("sb_vec", [128, 8 * NCH])
    sb_cst = A("sb_cst", [128, 8])
    sigm = A("sigm", [128, 4 * NCH])
    wT = A("wT", [128, NCH, C])
    wnat = A("wnat", [128, NCH, C])
    xa = A("xa", [128, NCH, TT])
    xb = A("xb", [128, NCH, TT])
    xc = A("xc", [128, NCH, TT])
    xm = A("xm", [128, NCH, TT])
    xsq = A("xsq", [128, NCH, TT])
    osb = A("osb", [128, NCH, TT])
    a_bc = A("a_bc", [128, TT])
    b_bc = A("b_bc", [128, TT])
    rows = A("rows", [1, 2048])

    dma_in(p, ident, idc)
    tv = dma_in(p, sb_vec, vecs)
    dma_in(p, sb_cst, cst)
    mixs = [sb_vec[:, i * NCH:(i + 1) * NCH] for i in range(4)]
    tdec = sb_vec[:, 4 * NCH:5 * NCH]
    lnw = sb_vec[:, 5 * NCH:6 * NCH]
    lnb = sb_vec[:, 6 * NCH:7 * NCH]
    ucol = sb_vec[:, 7 * NCH:8 * NCH]
    eps_ln = sb_cst[:, 0:1]
    p.w("scalar", tv)
    p.emit("scalar", lambda e: e.activation(sigm, sb_vec[:, 0:4 * NCH], AF.Sigmoid),
           inc=True)
    p.emit("vector", lambda e: e.memset(ones_col, 1.0), inc=True)
    p.emit("vector", lambda e: e.memset(ones_row, 1.0), inc=True)
    p.barrier()

    for it in range(ntt):
        t0 = it * TT
        dma_in(p, xa, x1[:, :, t0:t0 + TT].rearrange("c p t -> p c t"))
        p.barrier()
        tvv = ln_tile(p, xa, TT, xsq, a_bc, b_bc, rows, ones_col, ones_row, eps_ln,
                      "ln", xb)
        p.w("sync", tvv)
        dma_out(p, xn1[:, :, t0:t0 + TT].rearrange("c p t -> p c t"), xb)
        p.barrier()
        tvv = ln_tile(p, xb, TT, xsq, a_bc, b_bc, rows, ones_col, ones_row, eps_ln,
                      "ln", xc, w_col=lnw, b_col=lnb)
        p.w("sync", tvv)
        dma_out(p, xl_d[:, :, t0:t0 + TT].rearrange("c p t -> p c t"), xc)
        p.barrier()

    inner6 = ExitStack()
    Ai6 = lambda n, sh: inner6.enter_context(nc.sbuf_tensor("s6_" + n, sh, F32)).ap()
    xcB = Ai6("xcB", [128, NCH, TT])
    xmB = Ai6("xmB", [128, NCH, TT])
    osbB = wnat[:, :, 0:TT]
    for (W_dram, mi, dst, sig) in ((Wr, 2, r_d, True), (Wg, 2, g_d, True),
                                   (Wk, 0, k_d, False), (Wv, 1, v_d, False),
                                   (Ww, 3, w_d, False), (Wo, None, None, False)):
        if dst is None:
            break
        mix = sigm[:, mi * NCH:(mi + 1) * NCH]
        load_weight_T(p, W_dram, wT, wnat, ident)
        p.barrier()
        mix_proj_pass(p, xl_d, mix, wT, dst, sig, [xc, xcB], [xm, xmB], xsq,
                      [osb, osbB], Tl)

    inner6.close()
    wc_t = wT.rearrange("p a b -> p (a b)")[:, 0:Tl]
    cc_t = wT.rearrange("p a b -> p (a b)")[:, Tl:2 * Tl] if Tl <= 4096 else None
    rm_t = wnat.rearrange("p a b -> p (a b)")[:, 0:Tl]
    sc_t = wnat.rearrange("p a b -> p (a b)")[:, Tl:2 * Tl]
    for kc in range(NCH):
        dma_in(p, wc_t, w_d[kc])
        dma_in(p, rm_t, rmask_d)
        p.barrier()
        t = p.emit("vector", lambda e, kc=kc: e.tensor_scalar(
            wc_t, wc_t, tdec[:, kc:kc + 1], None, ALU.add), inc=True)
        p.w("vector", t)
        t = p.emit("vector", lambda e: e.tensor_scalar(
            wc_t, wc_t, -10.0, None, ALU.max), inc=True)
        p.w("vector", t)
        t = p.emit("vector", lambda e: e.tensor_scalar(
            wc_t, wc_t, 10.0, None, ALU.min), inc=True)
        p.w("scalar", t)
        t = p.emit("scalar", lambda e: e.activation(sc_t, wc_t, AF.Exp), inc=True)
        p.w("sync", t)
        dma_out(p, e_d[kc], sc_t)
        p.w("vector", t)
        t = p.emit("vector", lambda e: e.tensor_tensor_scan(
            cc_t, rm_t, sc_t, 0.0, ALU.mult, ALU.add), inc=True)
        p.w("sync", t)
        dma_out(p, C_d[kc], cc_t)
        p.barrier()
    for kc in range(NCH):
        dma_in(p, wc_t, C_d[kc])
        dma_in(p, rm_t, e_d[kc])
        dma_in(p, sc_t, r_d[kc])
        p.barrier()
        t = p.emit("vector", lambda e: e.tensor_tensor(rm_t, wc_t, rm_t,
                                                       ALU.subtract), inc=True)
        p.w("scalar", t)
        t = p.emit("scalar", lambda e: e.activation(rm_t, rm_t, AF.Exp, scale=-1.0),
                   inc=True)
        p.w("vector", t)
        tmul = p.emit("vector", lambda e: e.tensor_tensor(sc_t, sc_t, rm_t,
                                                          ALU.mult), inc=True)
        p.w("sync", tmul)
        tso = dma_out(p, rt_d[kc], sc_t)
        cv = wc_t.rearrange("p (s j) -> p s j", j=SC)
        ends = cv[:, :, SC - 1:SC]
        p.w("scalar", tmul)
        t = p.emit("scalar", lambda e: e.activation(
            rm_t[:, 0:Tl // SC], ends.rearrange("p s o -> p (s o)"), AF.Exp,
            scale=-1.0), inc=True)
        p.w("sync", t)
        dma_out(p, dE_d[kc], rm_t[:, 0:Tl // SC])
        p.w("vector", t)
        t = p.emit("vector", lambda e: e.tensor_tensor(
            cv, cv, ends.to_broadcast([128, Tl // SC, SC]), ALU.subtract), inc=True)
        p.w("scalar", t)
        t = p.emit("scalar", lambda e: e.activation(wc_t, wc_t, AF.Exp), inc=True)
        p.w("sync", tso)
        tld = dma_in(p, sc_t, k_d[kc])
        p.w("vector", t)
        p.w("vector", tld)
        t = p.emit("vector", lambda e: e.tensor_tensor(sc_t, sc_t, wc_t, ALU.mult),
                   inc=True)
        p.w("sync", t)
        dma_out(p, kh_d[kc], sc_t)
        p.barrier()
    xm_flat = xm.rearrange("p a b -> p (a b)")
    otile = xm_flat[:, 0:C]
    o2 = xm_flat[:, C:2 * C]
    for st in range(ntt128):
        dma_in(p, xa[:, :, 0:128], kh_d[:, :, st * 128:(st + 1) * 128].rearrange(
            "c p t -> p c t"))
        dma_in(p, xb[:, :, 0:128], v_d[:, :, st * 128:(st + 1) * 128].rearrange(
            "c p t -> p c t"))
        p.barrier()
        for (src, dst_t, ot) in ((xa, khtok, otile), (xb, vtok, o2)):
            for j in range(NCH):
                b = p.bank()
                tpe = p.emit("tensor", lambda e, b=b, j=j, src=src: e.transpose(
                    p.banks[b][:, 0:128], src[:, j, 0:128], ident), inc=True)
                p.w("vector", tpe)
                tev = p.emit("vector", lambda e, b=b, j=j, ot=ot: e.tensor_copy(
                    ot[:, j * 128:(j + 1) * 128], p.banks[b][:, 0:128]), inc=True)
                p.bank_done(b, tev)
            p.barrier(("sync",))
            dma_out(p, dst_t[st * 128:(st + 1) * 128, :], ot)
        p.barrier()

    S_t = [A(f"S{c}", [128, 128]) for c in range(NCH)]
    khsg = wT[0:16, :, :]
    vsg = wnat[0:16, :, :]
    rtt = xa[:, :, 0:128]
    rft = xa[:, :, 128:256]
    kft = xa[:, :, 256:384]
    eft = xa[:, :, 384:512]
    vft = xb[:, :, 0:128]
    yt = xb[:, :, 128:256]
    Dt = xb[:, :, 256:384]
    Et = xb[:, :, 384:512]
    Pt = xc[:, :, 0:128]
    gt_ = xc[:, :, 128:256]
    bart = A("bart", [128, 128])
    rmt4 = A("rmt4", [128, 128])
    bmask = A("bmask_sb", [128, 128])
    dEt = A("dEt", [128, NCH, 8])

    dma_in(p, bmask, bmask_d)
    for c in range(NCH):
        p.emit("vector", lambda e, c=c: e.memset(S_t[c], 0.0), inc=True)
    p.barrier()
    for it in range(ntt128):
        t0 = it * 128
        dma_in(p, rtt, rt_d[:, :, t0:t0 + 128].rearrange("c p t -> p c t"))
        dma_in(p, rft, r_d[:, :, t0:t0 + 128].rearrange("c p t -> p c t"))
        dma_in(p, kft, k_d[:, :, t0:t0 + 128].rearrange("c p t -> p c t"))
        dma_in(p, eft, e_d[:, :, t0:t0 + 128].rearrange("c p t -> p c t"))
        dma_in(p, vft, v_d[:, :, t0:t0 + 128].rearrange("c p t -> p c t"))
        dma_in(p, bart, bar_d[:, t0:t0 + 128])
        dma_in(p, rmt4, rmask_d[:, t0:t0 + 128])
        dma_in(p, khsg, khtok[t0:t0 + 128, :].rearrange("(s o) f -> o s f", o=SC))
        dma_in(p, vsg, vtok[t0:t0 + 128, :].rearrange("(s o) f -> o s f", o=SC))
        dma_in(p, dEt, dE_d[:, :, it * 8:(it + 1) * 8].rearrange("c p s -> p c s"))
        p.barrier()
        t = p.emit("vector", lambda e: e.tensor_tensor(
            eft, eft, bart.rearrange("p (o t) -> p o t", o=1).to_broadcast(
                [128, NCH, 128]), ALU.add), inc=True)
        tz = p.emit("vector", lambda e: e.memset(yt, 0.0), inc=True)
        tz2 = p.emit("vector", lambda e: e.memset(Dt, 0.0), inc=True)
        p.barrier(("vector", "scalar", "tensor"))
        for dl in range(SC):
            p.w("vector", p.t("qt"))
            p.w("scalar", p.t("qv"))
            if dl == 0:
                t = p.emit("vector", lambda e: e.tensor_tensor(Pt, rft, kft,
                                                               ALU.mult), inc=True)
                p.w("vector", t)
                t = p.emit("vector", lambda e: e.tensor_tensor(
                    Pt, Pt, ucol.rearrange("p (c o) -> p c o", o=1).to_broadcast(
                        [128, NCH, 128]), ALU.mult), inc=True)
                nn = 128
            else:
                if dl > 1:
                    t = p.emit("vector", lambda e, dl=dl: e.tensor_tensor(
                        Dt[:, :, dl:], Dt[:, :, dl:], eft[:, :, 1:128 - dl + 2 - 1],
                        ALU.add), inc=True)
                    p.w("vector", t)
                    p.w("scalar", t)
                t = p.emit("scalar", lambda e, dl=dl: e.activation(
                    Et[:, :, dl:], Dt[:, :, dl:], AF.Exp, scale=-1.0), inc=True)
                p.w("vector", t)
                t = p.emit("vector", lambda e, dl=dl: e.tensor_tensor(
                    Et[:, :, dl:], Et[:, :, dl:], rft[:, :, dl:], ALU.mult), inc=True)
                p.w("vector", t)
                t = p.emit("vector", lambda e, dl=dl: e.tensor_tensor(
                    Pt[:, :, dl:], Et[:, :, dl:], kft[:, :, 0:128 - dl], ALU.mult),
                    inc=True)
                if dl == 1:
                    p.w("vector", t)
                    t = p.emit("vector", lambda e: e.tensor_tensor(
                        Pt[:, :, 1:], Pt[:, :, 1:],
                        rmt4.rearrange("p (o t) -> p o t", o=1).to_broadcast(
                            [128, NCH, 128])[:, :, 1:], ALU.mult), inc=True)
                nn = 128 - dl
            p.w("vector", t)
            p.w("tensor", t)
            for hh in range(2):
                b = p.bank()
                tpe = p.emit("tensor", lambda e, b=b, hh=hh, dl=dl, nn=nn: e.matmul(
                    p.banks[b][:, 0:4 * nn],
                    bmask, Pt[:, 4 * hh:4 * (hh + 1), 128 - nn:],
                    start=True, stop=True), inc=True)
                p.w("vector", tpe)
                t = p.emit("vector", lambda e, b=b, hh=hh, dl=dl, nn=nn:
                           e.tensor_tensor(
                               Et[:, 4 * hh:4 * (hh + 1), 128 - nn:],
                               p.banks[b][:, 0:4 * nn].rearrange(
                                   "p (c t) -> p c t", c=4),
                               vft[:, 4 * hh:4 * (hh + 1), 0:nn], ALU.mult),
                           inc=True)
                p.bank_done(b, t)
                p.w("vector", t)
                t = p.emit("vector", lambda e, hh=hh, dl=dl, nn=nn: e.tensor_tensor(
                    yt[:, 4 * hh:4 * (hh + 1), 128 - nn:],
                    yt[:, 4 * hh:4 * (hh + 1), 128 - nn:],
                    Et[:, 4 * hh:4 * (hh + 1), 128 - nn:], ALU.add), inc=True)
                p.w("vector", t)
        for sc in range(8):
            c0 = sc * SC
            p.w("tensor", p.t("qv"))
            b = p.bank()
            for c in range(NCH):
                tpe = p.emit("tensor", lambda e, b=b, c=c, c0=c0: e.matmul(
                    p.banks[b][:, c * SC:(c + 1) * SC], S_t[c],
                    rtt[:, c, c0:c0 + SC], start=True, stop=True),
                    inc=(c == NCH - 1))
            p.w("vector", tpe)
            t = p.emit("vector", lambda e, b=b, c0=c0: e.tensor_tensor(
                yt[:, :, c0:c0 + SC],
                yt[:, :, c0:c0 + SC],
                p.banks[b][:, 0:128].rearrange("p (c t) -> p c t", c=NCH), ALU.add),
                inc=True)
            p.bank_done(b, t)
            for c in range(NCH):
                b = p.bank()
                tpe = p.emit("tensor", lambda e, b=b, c=c, sc=sc: e.matmul(
                    p.banks[b][:, 0:128], khsg[:, sc, c * 128:(c + 1) * 128],
                    vsg[:, sc, c * 128:(c + 1) * 128], start=True, stop=True),
                    inc=True)
                p.w("vector", tpe)
                t = p.emit("vector", lambda e, b=b: e.tensor_tensor(
                    Pt[:, 0, :], p.banks[b][:, 0:128], bmask, ALU.mult), inc=True)
                p.bank_done(b, t)
                p.w("vector", t)
                t = p.emit("vector", lambda e, c=c, sc=sc: e.scalar_tensor_tensor(
                    S_t[c], S_t[c], dEt[:, c, sc:sc + 1], Pt[:, 0, :],
                    ALU.mult, ALU.add), inc=True)
                p.w("vector", t)
        tld = dma_in(p, gt_, g_d[:, :, t0:t0 + 128].rearrange("c p t -> p c t"))
        p.w("vector", tld)
        t = p.emit("vector", lambda e: e.tensor_tensor(yt, yt, gt_, ALU.mult),
                   inc=True)
        p.w("sync", t)
        dma_out(p, gy_d[:, :, t0:t0 + 128].rearrange("c p t -> p c t"), yt)
        p.barrier()

    load_weight_T(p, Wo, wT, wnat, ident)
    p.barrier()
    for it in range(ntt):
        t0 = it * TT
        dma_in(p, xa, gy_d[:, :, t0:t0 + TT].rearrange("c p t -> p c t"))
        p.barrier()

        def evac(m, b, pet):
            p.w("vector", pet)
            return p.emit("vector", lambda e, m=m, b=b: e.tensor_copy(
                osb[:, m, :], p.banks[b][:, 0:TT]), inc=True)
        proj_fm(p, evac, xa, wT, TT)
        p.barrier(("sync",))
        dma_out(p, a1[:, :, t0:t0 + TT].rearrange("c p t -> p c t"), osb)
        p.barrier()
    p.barrier()
    p.pop_scope()


def emit_stage7(p, Tl=T, e_lo=0, e_hi=8, first=True):
    nc = p.nc
    p.push_scope()
    pfx = f"s7{e_lo}_"
    ntt = Tl // TT
    FB = 512
    NFB = 4096 // FB

    D = lambda n, s, k: p.dram(pfx + n, s, k)
    xn1 = D("xn1", [NCH, 128, Tl], "ExternalInput")
    if first:
        x1_ = D("x1", [NCH, 128, Tl], "ExternalInput")
        a1_ = D("a1", [NCH, 128, Tl], "ExternalInput")
    else:
        base = D("base", [NCH, 128, Tl], "ExternalInput")
    Wg = D("Wg", [C, 8], "ExternalInput")
    bg = D("bg", [1, 8], "ExternalInput")
    W1 = D("W1", [8, C, 4 * C], "ExternalInput")
    b1 = D("b1", [8, 128, 32], "ExternalInput")
    W2 = D("W2", [8, 4 * C, C], "ExternalInput")
    b2 = D("b2", [128, 64], "ExternalInput")
    emask_d = D("emask", [128, 64], "ExternalInput")
    idc = D("identity", [128, 128], "ExternalInput")
    out_d = D("out", [NCH, 128, Tl], "ExternalOutput")
    ct_d = D("ct", [8, Tl], "Internal")

    A = lambda n, s, dt=F32: p.sbuf(pfx + n, s, dt)
    ident = A("ident", [128, 128])
    ones_row = A("ones_row", [1, 128])
    wgsb = A("wgsb", [128, NCH, 8])
    bgsb = A("bgsb", [1, 8])
    emask = A("emask_sb", [128, 64])
    b2sb = A("b2sb", [128, 64])
    xa2 = [A(f"xa{i}", [128, NCH, TT]) for i in range(2)]
    ob2 = [A(f"osb{i}", [128, NCH, TT]) for i in range(2)]
    xc2 = [None, None]
    gsb = A("gsb", [128, 4, TT])
    csb = [A(f"csb{i}", [128, TT]) for i in range(4)]
    ptile = A("ptile", [128, 8])
    prank = A("prank", [128, 8])
    pw1f = A("pw1f", [128, NCH, FB])
    pw2f = A("pw2f", [128, FB // 128, C])
    pw1b = [A(f"pw1b_{i}", [128, NCH, FB], BF16) for i in range(2)]
    pw2b = [A(f"pw2b_{i}", [128, FB // 128, C], BF16) for i in range(2)]
    xab2 = [A(f"xab{i}", [128, NCH, TT], BF16) for i in range(2)]
    gsbb = A("gsbb", [128, 4, TT], BF16)
    b1sb = [A(f"b1sb_{i}", [128, 32]) for i in range(2)]
    lsum = A("lsum", [128, 4])
    ct_sb = A("ct_sb", [8, 128])
    if first:
        xc2 = [A(f"xc7{i}", [128, NCH, TT]) for i in range(2)]
    crow = [A(f"crow{i}", [1, TT]) for i in range(4)]

    dma_in(p, ident, idc)
    dma_in(p, wgsb, Wg.rearrange("(a p) b -> p a b", p=128))
    dma_in(p, bgsb, bg)
    dma_in(p, emask, emask_d)
    dma_in(p, b2sb, b2)
    p.emit("vector", lambda e: e.memset(ones_row, 1.0), inc=True)
    p.barrier()

    if first:
        for st in range(Tl // 128):
            t0 = st * 128
            dma_in(p, xa2[0][:, :, 0:128], xn1[:, :, t0:t0 + 128].rearrange(
                "c p t -> p c t"))
            p.barrier(("tensor", "vector", "scalar"))
            b = p.bank()
            for kc in range(NCH):
                p.emit("tensor", lambda e, b=b, kc=kc: e.matmul(
                    p.banks[b][:, 0:8], xa2[0][:, kc, 0:128], wgsb[:, kc, :],
                    start=(kc == 0), stop=False), inc=False)
            tpe = p.emit("tensor", lambda e, b=b: e.matmul(
                p.banks[b][:, 0:8], ones_row, bgsb, start=False, stop=True),
                inc=True)
            p.w("vector", tpe)
            t = p.emit("vector", lambda e, b=b: e.tensor_copy(ptile,
                                                              p.banks[b][:, 0:8]),
                       inc=True)
            p.bank_done(b, t)
            p.w("vector", t)
            t = p.emit("vector", lambda e: e.reduce_max(lsum[:, 0:1], ptile,
                                                        axis=AX.X, negate=True),
                       inc=True)
            p.w("scalar", t)
            t = p.emit("scalar", lambda e: e.activation(
                ptile, ptile, AF.Exp, bias=lsum[:, 0:1], accum_out=lsum[:, 1:2]),
                inc=True)
            p.w("vector", t)
            t = p.emit("vector", lambda e: e.reciprocal(lsum[:, 2:3], lsum[:, 1:2]),
                       inc=True)
            p.w("vector", t)
            t = p.emit("vector", lambda e: e.tensor_scalar(
                ptile, ptile, lsum[:, 2:3], None, ALU.mult), inc=True)
            p.w("vector", t)
            for ee in range(8):
                t1 = p.emit("vector", lambda e, ee=ee: e.tensor_scalar(
                    prank, ptile, ptile[:, ee:ee + 1], None, ALU.is_gt), inc=True)
                t2 = p.emit("scalar", lambda e, ee=ee: e.activation(
                    lsum[:, 3:4], ptile[:, ee:ee + 1], AF.Copy), inc=True) \
                    if False else None
                t2 = p.emit("vector", lambda e, ee=ee: e.tensor_scalar(
                    gsb[:, 0, 0:8], ptile, ptile[:, ee:ee + 1], None, ALU.is_equal),
                    inc=True)
                p.w("vector", t2)
                t3 = p.emit("vector", lambda e, ee=ee: e.tensor_tensor(
                    gsb[:, 0, 0:8], gsb[:, 0, 0:8], emask[:, ee * 8:(ee + 1) * 8],
                    ALU.mult), inc=True)
                p.w("vector", t3)
                t4 = p.emit("vector", lambda e: e.tensor_tensor(
                    prank, prank, gsb[:, 0, 0:8], ALU.add), inc=True)
                p.w("vector", t4)
                t5 = p.emit("vector", lambda e, ee=ee: e.reduce_sum(
                    gsb[:, 1, ee:ee + 1], prank, axis=AX.X), inc=True)
                p.w("vector", t5)
            t = p.emit("vector", lambda e: e.tensor_scalar(
                gsb[:, 1, 0:8], gsb[:, 1, 0:8], 2.0, None, ALU.is_lt), inc=True)
            p.w("vector", t)
            t = p.emit("vector", lambda e: e.tensor_tensor(
                ptile, ptile, gsb[:, 1, 0:8], ALU.mult), inc=True)
            p.w("vector", t)
            t = p.emit("vector", lambda e: e.reduce_sum(lsum[:, 0:1], ptile,
                                                        axis=AX.X), inc=True)
            p.w("vector", t)
            t = p.emit("vector", lambda e: e.tensor_scalar(
                lsum[:, 0:1], lsum[:, 0:1], 1e-8, None, ALU.add), inc=True)
            p.w("vector", t)
            t = p.emit("vector", lambda e: e.reciprocal(lsum[:, 1:2], lsum[:, 0:1]),
                       inc=True)
            p.w("vector", t)
            t = p.emit("vector", lambda e: e.tensor_scalar(
                ptile, ptile, lsum[:, 1:2], None, ALU.mult), inc=True)
            p.w("vector", t)
            p.w("tensor", t)
            b = p.bank()
            tpe = p.emit("tensor", lambda e, b=b: e.transpose(
                p.banks[b][0:8, 0:128], ptile, ident), inc=True)
            p.w("vector", tpe)
            t = p.emit("vector", lambda e, b=b: e.tensor_copy(
                ct_sb, p.banks[b][0:8, 0:128]), inc=True)
            p.bank_done(b, t)
            p.w("sync", t)
            tst = dma_out(p, ct_d[:, t0:t0 + 128], ct_sb)
            p.w("vector", tst)
            p.w("tensor", tst)
    else:
        pass

    p.barrier()
    ct_in = D("ct_in", [8, Tl], "ExternalInput") if not first else ct_d
    ct_out = D("ct_out", [8, Tl], "ExternalOutput")
    tile_use = {}

    def ld_tile(it, ti):
        t0 = it * TT
        p.w("sync", tile_use.get(ti))
        t1 = dma_in(p, xa2[ti], xn1[:, :, t0:t0 + TT].rearrange("c p t -> p c t"))
        if first:
            t2 = dma_in(p, ob2[ti], x1_[:, :, t0:t0 + TT].rearrange("c p t -> p c t"))
            t3 = dma_in(p, xc2[ti], a1_[:, :, t0:t0 + TT].rearrange("c p t -> p c t"))
            return (t1, t2, t3)
        t2 = dma_in(p, ob2[ti], base[:, :, t0:t0 + TT].rearrange("c p t -> p c t"))
        return (t1, t2)

    grps = [list(range(g, min(g + 2, ntt))) for g in range(0, ntt, 2)]
    tile_ld = {}
    for grp in grps:
        nu = len(grp)
        for u, it in enumerate(grp):
            tile_ld[u] = ld_tile(it, u)
        for u in range(nu):
            for tt in tile_ld[u]:
                p.w("tensor", tt)
                p.w("vector", tt)
            p.w("scalar", tile_ld[u][0])
            p.w("scalar", getattr(p, "xab_use", None))
            txab = p.emit("scalar", lambda e, u=u: e.activation(
                xab2[u][:, :, :], xa2[u][:, :, :], AF.Copy), inc=True)
            p.w("tensor", txab)
            if first:
                for kc in range(NCH):
                    p.w("vector", p.t("qv"))
                    p.emit("vector", lambda e, kc=kc, u=u: e.tensor_tensor(
                        ob2[u][:, kc, :], ob2[u][:, kc, :], xc2[u][:, kc, :],
                        ALU.add), inc=True)
        if not hasattr(p, "wld"):
            p.wld = {}
            p.wuse = {}
            p.wcast = {}
            p.stg_use = None
            p.crow_use = {}

        def ld_fb(job, wi):
            ee2, fb = job
            f0 = fb * FB
            p.w("sync", p.stg_use)
            t1 = dma_in(p, pw1f, W1[ee2][:, f0:f0 + FB].rearrange(
                "(a p) b -> p a b", p=128))
            p.w("gpsimd", p.stg_use)
            t2 = dma_in(p, pw2f, W2[ee2][f0:f0 + FB, :].rearrange(
                "(a p) b -> p a b", p=128), eng="gpsimd")
            p.w("sync", p.wuse.get(wi))
            t3 = dma_in(p, b1sb[wi][:, 0:FB // 128],
                        b1[ee2][:, fb * (FB // 128):(fb + 1) * (FB // 128)])
            p.wld[wi] = (t1, t2, t3)

        def cast_fb(wi):
            t1, t2, t3 = p.wld[wi]
            p.w("scalar", t1)
            p.w("scalar", p.wuse.get(wi))
            c1 = p.emit("scalar", lambda e, wi=wi: e.activation(
                pw1b[wi][:, :, :], pw1f[:, :, :], AF.Copy), inc=True)
            p.w("scalar", t2)
            c2 = p.emit("scalar", lambda e, wi=wi: e.activation(
                pw2b[wi][:, :, :], pw2f[:, :, :], AF.Copy), inc=True)
            p.wcast[wi] = (c1, c2, t3)
            p.stg_use = c2

        def ld_crow(ee2, u, ci):
            t0u = grp[u] * TT
            p.w("sync", p.crow_use.get(ci))
            return dma_in(p, crow[ci], ct_in[ee2:ee2 + 1, t0u:t0u + TT])

        def ld_crows(ee2):
            return [ld_crow(ee2, u, (ee2 % 2) * 2 + u) for u in range(nu)]

        jobs = [(ee2, fb) for ee2 in range(e_lo, e_hi) for fb in range(NFB)]
        ld_fb(jobs[0], 0)
        cast_fb(0)
        tcr = {jobs[0][0]: ld_crows(jobs[0][0])}
        tcsb = {}
        for ji, (ee, fb) in enumerate(jobs):
            wi = ji % 2
            if ji + 1 < len(jobs):
                ld_fb(jobs[ji + 1], 1 - wi)
                ne = jobs[ji + 1][0]
                if ne != ee and ne not in tcr:
                    tcr[ne] = ld_crows(ne)
            if fb == 0:
                tl = []
                for u in range(nu):
                    ci = (ee % 2) * 2 + u
                    b = p.bank()
                    p.w("tensor", tcr[ee][u])
                    tpe = p.emit("tensor", lambda e, b=b, ci=ci: e.matmul(
                        p.banks[b][:, 0:TT], ones_row, crow[ci],
                        start=True, stop=True), inc=True)
                    p.w("vector", tpe)
                    p.w("vector", p.t("qv"))
                    tc = p.emit("vector", lambda e, b=b, ci=ci: e.tensor_copy(
                        csb[ci], p.banks[b][:, 0:TT]), inc=True)
                    p.bank_done(b, tc)
                    tl.append(tc)
                tcsb[ee] = tl
                for u in range(nu):
                    p.crow_use[(ee % 2) * 2 + u] = p.t("qt")
            c1, c2, t3 = p.wcast[wi]
            p.w("tensor", c1)
            p.w("tensor", c2)
            p.w("scalar", t3)
            for u in range(nu):
                ci = (ee % 2) * 2 + u
                for m in range(FB // 128):
                    b = p.bank()
                    for kc in range(NCH):
                        tpe = p.emit("tensor", lambda e, b=b, m=m, kc=kc, wi=wi,
                                     u=u: e.matmul(
                            p.banks[b][:, 0:TT],
                            pw1b[wi][:, kc, m * 128:(m + 1) * 128],
                            xab2[u][:, kc, 0:TT], start=(kc == 0),
                            stop=(kc == NCH - 1)), inc=(kc == NCH - 1))
                    p.w("scalar", tpe)
                    tev = p.emit("scalar", lambda e, b=b, m=m, wi=wi: e.activation(
                        gsb[:, m, :], p.banks[b][:, 0:TT], AF.Gelu_apprx_tanh,
                        bias=b1sb[wi][:, m:m + 1]), inc=True)
                    p.bank_done(b, tev)
                    p.w("vector", tev)
                    p.w("vector", tcsb[ee][u])
                    t = p.emit("vector", lambda e, m=m, ci=ci: e.tensor_tensor(
                        gsbb[:, m, :], gsb[:, m, :], csb[ci], ALU.mult), inc=True)
                p.w("tensor", p.t("qv"))
                for m in range(NCH):
                    b = p.bank()
                    for kf in range(FB // 128):
                        tpe = p.emit("tensor", lambda e, b=b, m=m, kf=kf, wi=wi:
                                     e.matmul(
                            p.banks[b][:, 0:TT],
                            pw2b[wi][:, kf, m * 128:(m + 1) * 128],
                            gsbb[:, kf, 0:TT], start=(kf == 0),
                            stop=(kf == FB // 128 - 1)), inc=(kf == FB // 128 - 1))
                    p.w("vector", tpe)
                    p.w("vector", p.t("qv"))
                    tev = p.emit("vector", lambda e, b=b, m=m, u=u: e.tensor_tensor(
                        ob2[u][:, m, :], p.banks[b][:, 0:TT], ob2[u][:, m, :],
                        ALU.add), inc=True)
                    p.bank_done(b, tev)
            p.wuse[wi] = p.t("qt")
            if ji + 1 < len(jobs):
                cast_fb(1 - wi)
            if fb == NFB - 1:
                for u in range(nu):
                    ci = (ee % 2) * 2 + u
                    for m in range(NCH):
                        p.w("vector", p.t("qv"))
                        p.emit("vector", lambda e, m=m, ee=ee, ci=ci, u=u:
                               e.scalar_tensor_tensor(
                            ob2[u][:, m, :], csb[ci],
                            b2sb[:, ee * 8 + m:ee * 8 + m + 1],
                            ob2[u][:, m, :], ALU.mult, ALU.add), inc=True)
                    p.crow_use[ci] = p.t("qv")
        del tcr
        p.xab_use = p.t("qt")
        p.w("sync", p.t("qv"))
        for u, it in enumerate(grp):
            tile_use[u] = dma_out(
                p, out_d[:, :, it * TT:it * TT + TT].rearrange("c p t -> p c t"),
                ob2[u])
    p.barrier()
    if first:
        for st in range(Tl // 512):
            tld = dma_in(p, csb[0][0:8, :], ct_d[:, st * 512:(st + 1) * 512])
            p.w("sync", tld)
            tst = dma_out(p, ct_out[:, st * 512:(st + 1) * 512], csb[0][0:8, :])
            p.w("sync", tst)
    else:
        for st in range(Tl // 512):
            tld = dma_in(p, csb[0][0:8, :], ct_in[:, st * 512:(st + 1) * 512])
            p.w("sync", tld)
            tst = dma_out(p, ct_out[:, st * 512:(st + 1) * 512], csb[0][0:8, :])
            p.w("sync", tst)
    p.barrier()
    p.pop_scope()


def fm(x):
    return np.ascontiguousarray(np.asarray(x).T.reshape(NCH, 128, -1))


def fm_inv(x):
    return np.ascontiguousarray(np.asarray(x).reshape(C, -1).T)


def vec_col(v):
    return np.ascontiguousarray(np.asarray(v).reshape(NCH, 128).T)


def stage1_inputs(x_row_fm, w):
    vecs = np.concatenate([
        vec_col(w["L0_attn_time_mix_k"]), vec_col(w["L0_attn_time_mix_v"]),
        vec_col(w["L0_attn_time_mix_r"]), vec_col(w["L0_attn_time_decay"]),
        vec_col(w["L0_attn_time_first"]), vec_col(w["L0_attn_ln_w"]),
        vec_col(w["L0_attn_ln_b"]), np.zeros((128, NCH), np.float32)], axis=1)
    consts = np.zeros((128, 8), np.float32)
    consts[:, 0] = LN_EPS
    consts[:, 1] = RMS_EPS
    return {
        "x0": x_row_fm,
        "Wk": np.ascontiguousarray(w["L0_attn_Wk"]),
        "Wv": np.ascontiguousarray(w["L0_attn_Wv"]),
        "Wr": np.ascontiguousarray(w["L0_attn_Wr"]),
        "Wo": np.ascontiguousarray(w["L0_attn_Wo"]),
        "vecs": np.ascontiguousarray(vecs.astype(np.float32)), "consts": consts,
        "identity": np.eye(128, dtype=np.float32),
    }


def _mk(pfx, emit, *a, **kw):
    es = ExitStack()
    nc = bass.Bass("TRN2", target_bir_lowering=False)
    p = Prog(nc, es, pfx=pfx)
    emit(p, *a, **kw)
    p.finalize()
    es.close()
    return nc


def build_stage1(Tl=T):
    return _mk("s1_", emit_stage1, Tl)


def build_dec_qkv(pfx, S):
    return _mk(pfx, emit_dec_qkv, pfx, S)


def build_dec_attn(pfx, S):
    return _mk(pfx, emit_dec_attn, pfx, S)


def build_dec_tail(pfx, S):
    return _mk(pfx, emit_dec_tail, pfx, S)


def build_stage3(Tl=T):
    return _mk("s3_", emit_stage3, Tl)


def build_stage4(Tl=T):
    return _mk("s4_", emit_stage4, Tl)


def build_stage5(Tl=T):
    return _mk("s5_", emit_stage5, Tl)


def build_stage6(Tl=T):
    return _mk("s6_", emit_stage6, Tl)


def build_stage7(Tl=T, e_lo=0, e_hi=8, first=True):
    return _mk(f"s7{e_lo}_", emit_stage7, Tl, e_lo, e_hi, first)


def fused_aliases(Tl):
    K = Tl // 4
    al = {}
    for st in ("dA_", "dB_", "dC_", "dD_", "dE_", "dF_", "s3_", "s4_", "s5_",
               "s6_", "s70_", "s74_"):
        al[st + "identity"] = "identity"
        al[st + "cst"] = "G_cst"
    al["consts"] = "G_cst8" if False else "consts"
    al.update({
        "s1_identity": "identity", "s1_consts": "G_cst",
        "dA_hs": "s1_xn0", "dA_cos": "G_cos", "dA_sin": "G_sin",
        "dB_QT": "dA_QT", "dB_KT": "dA_KT", "dB_Vtok": "dA_Vtok",
        "dB_maskn": "G_maskn",
        "dC_hs": "s1_xn0", "dC_Otok": "dB_Otok",
        "s3_xn0": "s1_xn0", "s3_proc": "dC_proc",
        "s4_dst": "s3_dst", "s4_dch": "s3_dch", "s4_ptok": "s3_ptok",
        "s4_cos": "G_cos", "s4_sin": "G_sin",
        "dD_hs": "s4_selfm", "dD_cos": "s4_cossel", "dD_sin": "s4_sinsel",
        "dE_QT": "dD_QT", "dE_KT": "dD_KT", "dE_Vtok": "dD_Vtok",
        "dE_maskn": "G_maskn",
        "dF_hs": "s4_selfm", "dF_Otok": "dE_Otok",
        "s5_proc2": "dF_proc", "s5_seltok": "s4_seltok", "s5_isc": "s4_isc",
        "s5_f0tok": "s3_f0tok", "s5_x0": "s1_x0", "s5_a0": "s1_a0",
        "s6_x1": "s5_x1",
        "s70_xn1": "s6_xn1", "s70_x1": "s5_x1", "s70_a1": "s6_a1",
        "s74_xn1": "s6_xn1", "s74_base": "s70_out", "s74_ct_in": "s70_ct_out",
    })
    for nm in ("Wg", "bg", "W1", "b1", "W2", "b2", "emask"):
        al["s74_" + nm] = "s70_" + nm
    return al


def build_fused(Tl=T):
    es = ExitStack()
    nc = bass.Bass("TRN2", target_bir_lowering=False)
    p = Prog(nc, es, pfx="f_", fused=True, final_outputs={"s74_out"},
             alias=fused_aliases(Tl))
    p.dram("identity", [128, 128], "ExternalInput")
    p.dram("G_cst", [128, 8], "ExternalInput")
    p.dram("G_cos", [Tl, 32], "ExternalInput")
    p.dram("G_sin", [Tl, 32], "ExternalInput")
    p.dram("G_maskn", [128, 128], "ExternalInput")
    ccsrc = p.dram("G_ccsrc", [128, 8], "Internal")
    ccdst = p.dram("G_ccdst", [128, 8], "Internal")

    K = Tl // 4
    emit_stage1(p, Tl)
    emit_dec_qkv(p, "dA_", Tl)
    emit_dec_attn(p, "dB_", Tl)
    emit_dec_tail(p, "dC_", Tl)
    emit_stage3(p, Tl)
    p.push_scope()
    padt = p.sbuf("f_pad", [128, 8])
    t = p.emit("vector", lambda e: e.memset(padt, 0.0), inc=True)
    parts_ap = p.dram_map["s3_parts"]
    p.w("sync", t)
    tld = dma_in(p, padt[0:1, 0:2], parts_ap)
    p.w("sync", tld)
    if COLLECTIVE:
        tst = dma_out(p, ccsrc, padt)
        p.w("gpsimd", tst)
        p.emit("gpsimd", lambda e: e.collective_compute(
            "AllReduce", ALU.add, replica_groups=[list(range(NCORES))],
            ins=[ccsrc.opt()], outs=[ccdst.opt()]), inc=True)
    else:
        p.w("vector", t)
        p.w("vector", tld)
        t2 = p.emit("vector", lambda e: e.tensor_scalar(
            padt, padt, 8.0, None, ALU.mult), inc=True)
        p.w("sync", t2)
        dma_out(p, ccdst, padt)
    p.barrier()
    p.pop_scope()
    emit_stage4(p, Tl)
    emit_dec_qkv(p, "dD_", K)
    emit_dec_attn(p, "dE_", K)
    emit_dec_tail(p, "dF_", K)
    emit_stage5(p, Tl)
    emit_stage6(p, Tl)
    emit_stage7(p, Tl, 0, 4, True)
    emit_stage7(p, Tl, 4, 8, False)
    p.finalize()
    es.close()
    return nc


_NC_CACHE = {}
COLLECTIVE = True
SIM = False
FUSED = True
RUN_CORES = NCORES


def get_nc(name, builder):
    if name not in _NC_CACHE:
        _NC_CACHE[name] = builder()
    return _NC_CACHE[name]


def run_stage(nc, in_maps, out_names):
    import time as _time
    _t0 = _time.time()
    ncore = len(in_maps)
    if SIM:
        from concourse import bass_interp
        res = []
        for m in in_maps:
            sim = bass_interp.MultiCoreSim(nc, 1)
            for k, v in m.items():
                sim.cores[0].tensor(k)[:] = np.asarray(v).reshape(
                    sim.cores[0].tensor(k).shape)
            sim.simulate()
            res.append({o: np.array(sim.cores[0].mem_tensor(o)) for o in out_names})
        return res
    r = run_bass_kernel_spmd(nc, in_maps, core_ids=list(range(ncore)))
    out = [{o: np.asarray(r.results[b][o]) for o in out_names}
           for b in range(ncore)]
    print(f"[stage {out_names[0]}] {_time.time() - _t0:.0f}s", flush=True)
    return out


def cos_sin_tables(S):
    half = 32
    inv = (1.0 / (np.float32(10000.0) ** (np.arange(half, dtype=np.float32)
                                          / np.float32(half)))).astype(np.float32)
    pos = np.arange(S, dtype=np.float32)
    ang = pos[:, None] * inv
    return np.cos(ang).astype(np.float32), np.sin(ang).astype(np.float32)


def module_consts(Tl):
    K = Tl // 4
    cst = np.zeros((128, 8), np.float32)
    cst[:, 0] = LN_EPS
    cst[:, 1] = RMS_EPS
    cst[:, 2] = 0.125
    cst[:, 3] = 1.0 / 1024
    cst[:, 4] = 0.5 / 1024
    cst[:, 5] = -0.5
    cst[:, 6] = float(K)
    cst[:, 7] = 1e9
    ident = np.eye(128, dtype=np.float32)
    maskn = np.where(np.tril(np.ones((128, 128), bool)), 0.0,
                     -3.0e38).astype(np.float32)
    cos, sin = cos_sin_tables(Tl)
    iota = np.broadcast_to(np.arange(Tl, dtype=np.float32), (128, Tl)).copy()
    ivals = (np.arange(128)[:, None] * (Tl // 128)
             + np.arange(Tl // 128)[None, :]).astype(np.float32)
    prem = np.triu(np.ones((128, 128), np.float32), 1)
    tau = np.arange(Tl)
    rmask = np.broadcast_to((tau % SC != 0).astype(np.float32), (128, Tl)).copy()
    bar = np.broadcast_to((1e6 * ((tau % SC == 0) | (tau % SC == SC - 1))
                           ).astype(np.float32), (128, Tl)).copy()
    bmask = np.zeros((128, 128), np.float32)
    bmask[:64, :64] = 1
    bmask[64:, 64:] = 1
    emask = np.zeros((128, 64), np.float32)
    for e in range(8):
        emask[:, e * 8:e * 8 + e] = 1.0
    return dict(cst=cst, ident=ident, maskn=maskn, cos=cos, sin=sin, iota=iota,
                ivals=ivals, prem=prem, rmask=rmask, bar=bar, bmask=bmask,
                emask=emask)


def kernel(**inputs):
    Tl = T
    w = {k: np.asarray(v) for k, v in inputs.items()}
    x = w["L0_attn_x"]
    nb = x.shape[0]
    CC = module_consts(Tl)
    ccos2, csin2 = cos_sin_tables(Tl)
    ar = lambda v: np.ascontiguousarray(v.astype(np.float32))

    if FUSED:
        nc = get_nc("fused", lambda: build_fused(Tl))
        b1cols = np.stack([np.ascontiguousarray(
            np.asarray(w["L1_ffn_b1"][e]).reshape(32, 128).T) for e in range(8)]
            ).astype(np.float32)
        b2cols = np.concatenate([vec_col(w["L1_ffn_b2"][e]) for e in range(8)],
                                axis=1).astype(np.float32)
        vecs6 = np.concatenate([
            vec_col(w["L1_attn_mix_k"].reshape(-1)),
            vec_col(w["L1_attn_mix_v"].reshape(-1)),
            vec_col(w["L1_attn_mix_r"].reshape(-1)),
            vec_col(w["L1_attn_mix_w"].reshape(-1)),
            vec_col(w["L1_attn_time_decay"].reshape(-1)),
            vec_col(w["L1_attn_ln_w"]), vec_col(w["L1_attn_ln_b"]),
            vec_col(w["L1_attn_time_first"].reshape(-1))], axis=1
            ).astype(np.float32)
        maps = []
        for b in range(nb):
            s1m = stage1_inputs(fm(np.ascontiguousarray(x[b])), w)
            m = {
                "identity": CC["ident"], "G_cst": CC["cst"], "G_cos": CC["cos"],
                "G_sin": CC["sin"], "G_maskn": CC["maskn"],
                "s1_x0": s1m["x0"], "s1_Wk": s1m["Wk"], "s1_Wv": s1m["Wv"],
                "s1_Wr": s1m["Wr"], "s1_Wo": s1m["Wo"], "s1_vecs": s1m["vecs"],
                "dA_wq": ar(w["L0_ffn_wq1"]), "dA_wk": ar(w["L0_ffn_wk1"]),
                "dA_wv": ar(w["L0_ffn_wv1"]), "dA_vec": vec_col(w["L0_ffn_ln1_1"]),
                "dC_wo": ar(w["L0_ffn_wo1"]), "dC_wg": ar(w["L0_ffn_wg1"]),
                "dC_wu": ar(w["L0_ffn_wu1"]), "dC_wd": ar(w["L0_ffn_wd1"]),
                "dC_vec": vec_col(w["L0_ffn_ln2_1"]),
                "s3_pw": ar(w["L0_ffn_prior_w"]),
                "s3_vec": vec_col(w["L0_ffn_prior_norm_w"]),
                "s4_iota": CC["iota"], "s4_ivals": CC["ivals"],
                "s4_prem": CC["prem"],
                "dD_wq": ar(w["L0_ffn_wq2"]), "dD_wk": ar(w["L0_ffn_wk2"]),
                "dD_wv": ar(w["L0_ffn_wv2"]), "dD_vec": vec_col(w["L0_ffn_ln1_2"]),
                "dF_wo": ar(w["L0_ffn_wo2"]), "dF_wg": ar(w["L0_ffn_wg2"]),
                "dF_wu": ar(w["L0_ffn_wu2"]), "dF_wd": ar(w["L0_ffn_wd2"]),
                "dF_vec": vec_col(w["L0_ffn_ln2_2"]),
                "s6_Wr": ar(w["L1_attn_Wr"]), "s6_Wk": ar(w["L1_attn_Wk"]),
                "s6_Wv": ar(w["L1_attn_Wv"]), "s6_Ww": ar(w["L1_attn_Ww"]),
                "s6_Wg": ar(w["L1_attn_Wg"]), "s6_Wo": ar(w["L1_attn_Wo"]),
                "s6_vecs": vecs6, "s6_rmask": CC["rmask"], "s6_bar": CC["bar"],
                "s6_bmask": CC["bmask"],
                "s70_Wg": ar(w["L1_ffn_Wg"]),
                "s70_bg": ar(w["L1_ffn_bg"]).reshape(1, 8),
                "s70_W1": ar(w["L1_ffn_W1"]), "s70_b1": b1cols,
                "s70_W2": ar(w["L1_ffn_W2"]), "s70_b2": b2cols,
                "s70_emask": CC["emask"],
            }
            maps.append(m)
        r = run_stage(nc, maps, ["s74_out"])
        out = np.stack([fm_inv(np.asarray(r[b]["s74_out"]).reshape(NCH, 128, Tl))
                        for b in range(nb)])
        return out.astype(np.float32)

    nc = get_nc("s1", lambda: build_stage1(Tl))
    maps = [stage1_inputs(fm(np.ascontiguousarray(x[b])), w) for b in range(nb)]
    r1 = run_stage(nc, maps, ["xn0", "a0"])

    nc = get_nc("dA", lambda: build_dec_qkv("dA_", Tl))
    maps = [{"dA_hs": r1[b]["xn0"], "dA_wq": ar(w["L0_ffn_wq1"]),
             "dA_wk": ar(w["L0_ffn_wk1"]), "dA_wv": ar(w["L0_ffn_wv1"]),
             "dA_vec": vec_col(w["L0_ffn_ln1_1"]), "dA_cst": CC["cst"],
             "dA_cos": CC["cos"], "dA_sin": CC["sin"],
             "dA_identity": CC["ident"]} for b in range(nb)]
    r2 = run_stage(nc, maps, ["dA_QT", "dA_KT", "dA_Vtok"])

    nc = get_nc("dB", lambda: build_dec_attn("dB_", Tl))
    maps = [{"dB_QT": r2[b]["dA_QT"], "dB_KT": r2[b]["dA_KT"],
             "dB_Vtok": r2[b]["dA_Vtok"], "dB_maskn": CC["maskn"],
             "dB_cst": CC["cst"], "dB_identity": CC["ident"]} for b in range(nb)]
    r3 = run_stage(nc, maps, ["dB_Otok"])

    nc = get_nc("dC", lambda: build_dec_tail("dC_", Tl))
    maps = [{"dC_hs": r1[b]["xn0"], "dC_Otok": r3[b]["dB_Otok"],
             "dC_wo": ar(w["L0_ffn_wo1"]), "dC_wg": ar(w["L0_ffn_wg1"]),
             "dC_wu": ar(w["L0_ffn_wu1"]), "dC_wd": ar(w["L0_ffn_wd1"]),
             "dC_vec": vec_col(w["L0_ffn_ln2_1"]), "dC_cst": CC["cst"],
             "dC_identity": CC["ident"]} for b in range(nb)]
    r4 = run_stage(nc, maps, ["dC_proc"])

    nc = get_nc("s3", lambda: build_stage3(Tl))
    maps = [{"s3_xn0": r1[b]["xn0"], "s3_proc": r4[b]["dC_proc"],
             "s3_pw": ar(w["L0_ffn_prior_w"]),
             "s3_vec": vec_col(w["L0_ffn_prior_norm_w"]), "s3_cst": CC["cst"],
             "s3_identity": CC["ident"]} for b in range(nb)]
    r5 = run_stage(nc, maps, ["s3_dst", "s3_dch", "s3_parts", "s3_ptok",
                              "s3_f0tok"])

    parts_all = np.concatenate([np.asarray(r5[b]["s3_parts"]).reshape(1, 2)
                                for b in range(nb)], axis=0).astype(np.float32)
    if nb < NCORES:
        parts_all = np.tile(parts_all[0:1], (NCORES, 1)) if nb == 1 else             np.concatenate([parts_all] * (NCORES // nb), axis=0)

    nc = get_nc("s4", lambda: build_stage4(Tl))
    maps = [{"s4_dst": r5[b]["s3_dst"], "s4_dch": r5[b]["s3_dch"],
             "s4_parts": parts_all, "s4_ptok": r5[b]["s3_ptok"],
             "s4_cos": CC["cos"], "s4_sin": CC["sin"], "s4_iota": CC["iota"],
             "s4_ivals": CC["ivals"], "s4_prem": CC["prem"], "s4_cst": CC["cst"],
             "s4_identity": CC["ident"]} for b in range(nb)]
    r6 = run_stage(nc, maps, ["s4_isc", "s4_seltok", "s4_selfm", "s4_cossel",
                              "s4_sinsel"])

    K = Tl // 4
    nc = get_nc("dD", lambda: build_dec_qkv("dD_", K))
    maps = [{"dD_hs": r6[b]["s4_selfm"], "dD_wq": ar(w["L0_ffn_wq2"]),
             "dD_wk": ar(w["L0_ffn_wk2"]), "dD_wv": ar(w["L0_ffn_wv2"]),
             "dD_vec": vec_col(w["L0_ffn_ln1_2"]), "dD_cst": CC["cst"],
             "dD_cos": r6[b]["s4_cossel"], "dD_sin": r6[b]["s4_sinsel"],
             "dD_identity": CC["ident"]} for b in range(nb)]
    r7 = run_stage(nc, maps, ["dD_QT", "dD_KT", "dD_Vtok"])

    nc = get_nc("dE", lambda: build_dec_attn("dE_", K))
    maps = [{"dE_QT": r7[b]["dD_QT"], "dE_KT": r7[b]["dD_KT"],
             "dE_Vtok": r7[b]["dD_Vtok"], "dE_maskn": CC["maskn"],
             "dE_cst": CC["cst"], "dE_identity": CC["ident"]} for b in range(nb)]
    r8 = run_stage(nc, maps, ["dE_Otok"])

    nc = get_nc("dF", lambda: build_dec_tail("dF_", K))
    maps = [{"dF_hs": r6[b]["s4_selfm"], "dF_Otok": r8[b]["dE_Otok"],
             "dF_wo": ar(w["L0_ffn_wo2"]), "dF_wg": ar(w["L0_ffn_wg2"]),
             "dF_wu": ar(w["L0_ffn_wu2"]), "dF_wd": ar(w["L0_ffn_wd2"]),
             "dF_vec": vec_col(w["L0_ffn_ln2_2"]), "dF_cst": CC["cst"],
             "dF_identity": CC["ident"]} for b in range(nb)]
    r9 = run_stage(nc, maps, ["dF_proc"])

    nc = get_nc("s5", lambda: build_stage5(Tl))
    maps = [{"s5_proc2": r9[b]["dF_proc"], "s5_seltok": r6[b]["s4_seltok"],
             "s5_isc": r6[b]["s4_isc"], "s5_f0tok": r5[b]["s3_f0tok"],
             "s5_x0": fm(np.ascontiguousarray(x[b])), "s5_a0": r1[b]["a0"],
             "s5_identity": CC["ident"]} for b in range(nb)]
    r10 = run_stage(nc, maps, ["s5_x1"])

    vecs6 = np.concatenate([
        vec_col(w["L1_attn_mix_k"].reshape(-1)),
        vec_col(w["L1_attn_mix_v"].reshape(-1)),
        vec_col(w["L1_attn_mix_r"].reshape(-1)),
        vec_col(w["L1_attn_mix_w"].reshape(-1)),
        vec_col(w["L1_attn_time_decay"].reshape(-1)),
        vec_col(w["L1_attn_ln_w"]), vec_col(w["L1_attn_ln_b"]),
        vec_col(w["L1_attn_time_first"].reshape(-1))], axis=1).astype(np.float32)
    nc = get_nc("s6", lambda: build_stage6(Tl))
    maps = [{"s6_x1": r10[b]["s5_x1"], "s6_Wr": ar(w["L1_attn_Wr"]),
             "s6_Wk": ar(w["L1_attn_Wk"]), "s6_Wv": ar(w["L1_attn_Wv"]),
             "s6_Ww": ar(w["L1_attn_Ww"]), "s6_Wg": ar(w["L1_attn_Wg"]),
             "s6_Wo": ar(w["L1_attn_Wo"]), "s6_vecs": vecs6, "s6_cst": CC["cst"],
             "s6_identity": CC["ident"], "s6_rmask": CC["rmask"],
             "s6_bar": CC["bar"], "s6_bmask": CC["bmask"]} for b in range(nb)]
    r11 = run_stage(nc, maps, ["s6_xn1", "s6_a1"])

    b1cols = np.stack([np.ascontiguousarray(
        np.asarray(w["L1_ffn_b1"][e]).reshape(32, 128).T) for e in range(8)]
        ).astype(np.float32)
    b2cols = np.concatenate([vec_col(w["L1_ffn_b2"][e]) for e in range(8)],
                            axis=1).astype(np.float32)
    com = {"Wg": ar(w["L1_ffn_Wg"]), "bg": ar(w["L1_ffn_bg"]).reshape(1, 8),
           "W1": ar(w["L1_ffn_W1"]), "b1": b1cols, "W2": ar(w["L1_ffn_W2"]),
           "b2": b2cols, "emask": CC["emask"], "identity": CC["ident"]}
    nc = get_nc("s70", lambda: build_stage7(Tl, 0, 4, True))
    maps = [{**{f"s70_{k}": v for k, v in com.items()},
             "s70_xn1": r11[b]["s6_xn1"], "s70_x1": r10[b]["s5_x1"],
             "s70_a1": r11[b]["s6_a1"]} for b in range(nb)]
    r12 = run_stage(nc, maps, ["s70_out", "s70_ct_out"])

    nc = get_nc("s74", lambda: build_stage7(Tl, 4, 8, False))
    maps = [{**{f"s74_{k}": v for k, v in com.items()},
             "s74_xn1": r11[b]["s6_xn1"], "s74_base": r12[b]["s70_out"],
             "s74_ct_in": r12[b]["s70_ct_out"]} for b in range(nb)]
    r13 = run_stage(nc, maps, ["s74_out"])

    out = np.stack([fm_inv(np.asarray(r13[b]["s74_out"]).reshape(NCH, 128, Tl))
                    for b in range(nb)])
    return out.astype(np.float32)
